# version 07 (final)
# speedup vs baseline: 1.1915x; 1.0141x over previous
import numpy as np
import ml_dtypes
from contextlib import ExitStack

import concourse.bass as bass
import concourse.mybir as mybir
from concourse.bass_utils import run_bass_kernel_spmd

F32 = mybir.dt.float32
BF16 = mybir.dt.bfloat16
AF = mybir.ActivationFunctionType
ALU = mybir.AluOpType
AX = mybir.AxisListType
NBF = ml_dtypes.bfloat16

NCORE = 8
B = 4
S = 4096
D = 1024
TPC = S // NCORE
T = B * TPC
NKT = D // 128
NTT = T // 128
NCH = T // 512
LN_EPS = 1e-5

RG = [list(range(NCORE))]



class Prog:
    ENGS = ("sync", "gpsimd", "tensor", "vector", "scalar")

    def __init__(self, nc):
        self.nc = nc
        self.seq = {e: [] for e in self.ENGS}
        self.tok = 0
        self.es = ExitStack()
        self.outer_es = self.es
        self.prefix = ""
        self.tok_h = self.es.enter_context(nc.semaphore("tok"))
        self.mmsem = self.es.enter_context(nc.semaphore("mmsem"))
        self.mmcnt = 0
        self.wsem = self.es.enter_context(nc.semaphore("wsem"))
        self.wcnt = 0
        self.ssem = self.es.enter_context(nc.semaphore("ssem"))
        self.scnt = 0
        self.vsem = self.es.enter_context(nc.semaphore("vsem"))
        self.vcnt = 0

    def scope(self, prefix):
        from contextlib import contextmanager

        @contextmanager
        def _cm():
            old_es, old_pfx = self.es, self.prefix
            self.es = ExitStack()
            self.prefix = prefix
            try:
                yield
            finally:
                self.es.close()
                self.es, self.prefix = old_es, old_pfx
        return _cm()

    def sb(self, name, shape, dtype):
        return self.es.enter_context(self.nc.sbuf_tensor(self.prefix + name, shape, dtype))

    def ps(self, name, shape, dtype=F32):
        return self.es.enter_context(self.nc.psum_tensor(self.prefix + name, shape, dtype))

    def dram(self, name, shape, dtype):
        return self.nc.dram_tensor(self.prefix + name, shape, dtype)

    def inline(self, arr, name):
        return self.nc.inline_tensor(arr, self.prefix + name)

    def reg(self, eng_type):
        if not hasattr(self, "_regs"):
            self._regs = {}
        if eng_type not in self._regs:
            self._regs[eng_type] = self.nc.alloc_register(eng_type, "dynreg1")
        return self._regs[eng_type]

    def reg2(self, eng_type):
        if not hasattr(self, "_regs2"):
            self._regs2 = {}
        if eng_type not in self._regs2:
            self._regs2[eng_type] = self.nc.alloc_register(eng_type, "dynreg2")
        return self._regs2[eng_type]

    def dyn(self, E, key):
        if not hasattr(self, "_dynvals"):
            self._dynvals = {}
        if key not in self._dynvals:
            handle, r, cidx = self.dyn_handles[key]
            gr = self.nc.alloc_register(mybir.EngineType.Pool, "dynr_" + key)
            E.reg_load(gr, handle[r:r + 1, cidx:cidx + 1])
            self._dynvals[key] = E.compute_val(gr)
        return self._dynvals[key]

    def op(self, eng, build, inc=1, wait=None, extra=()):
        tv = self.tok if wait is None else wait
        th = self.tok_h

        def f(E, tv=tv, inc=inc, build=build, extra=tuple(extra)):
            E.wait_ge(th, tv)
            for (sh, sv) in extra:
                E.wait_ge(sh, sv)
            ins = build(E)
            assert ins is not None, "build must return its last instruction"
            ins.then_inc(th, inc)

        self.seq[eng].append(f)
        self.tok += inc

    def rop(self, eng, build, wait_tok, sem, inc=1, extra=()):
        th = self.tok_h

        def f(E, build=build, wait_tok=wait_tok, extra=tuple(extra)):
            E.wait_ge(th, wait_tok)
            for (sh, sv) in extra:
                E.wait_ge(sh, sv)
            ins = build(E)
            assert ins is not None
            ins.then_inc(sem, inc)

        self.seq[eng].append(f)

    def xop(self, eng, build, sem_out, inc=1, wait_tok=None, extra=()):
        th = self.tok_h

        def f(E, build=build, wait_tok=wait_tok, extra=tuple(extra)):
            if wait_tok is not None:
                E.wait_ge(th, wait_tok)
            for (sh, sv) in extra:
                E.wait_ge(sh, sv)
            ins = build(E)
            assert ins is not None
            ins.then_inc(sem_out, inc)

        self.seq[eng].append(f)

    def mm_rop(self, build, wait_tok, extra=()):
        self.rop("tensor", build, wait_tok, self.mmsem, inc=1, extra=extra)
        self.mmcnt += 1
        return self.mmcnt

    def bridge(self, eng, sem, val):
        self.op(eng, lambda E: E.nop(), extra=((sem, val),))

    def dma(self, out, in_, eng="sync", wait=None, extra=(), **kw):
        self.op(eng, lambda E: E.dma_start(out=out, in_=in_, **kw), inc=16, wait=wait,
                extra=extra)

    def cast_dma(self, out, in_, wait=None, **kw):
        self.op("gpsimd", lambda E: E.dma_start(out=out, in_=in_, **kw), inc=16,
                wait=wait)

    def cc(self, kind, alu, ins_, outs_):
        def b(E):
            if kind == "AllToAll":
                from concourse.aot_env import allow_ici_all_to_all
                with allow_ici_all_to_all():
                    return E.collective_compute(kind, alu, replica_groups=RG,
                                                ins=ins_, outs=outs_)
            return E.collective_compute(kind, alu, replica_groups=RG,
                                        ins=ins_, outs=outs_)
        self.op("gpsimd", b, inc=1)

    def finalize(self):
        total = self.tok
        th = self.tok_h
        self.seq["sync"].append(lambda E: E.wait_ge(th, total))
        with self.nc.Block() as block:
            for eng in self.ENGS:
                fns = self.seq[eng]
                if not fns:
                    continue
                getattr(block, eng)(lambda E, fns=fns: [fn(E) for fn in fns])
        self.es.close()


class PP:

    def __init__(self, P, pvs, beng="vector"):
        self.P = P
        self.pv = list(pvs)
        self.evac_tok = [0] * len(self.pv)
        self.base = P.tok
        self.beng = beng
        self.i = 0

    def step(self, mmfn, evacfn):
        P = self.P
        j = self.i % len(self.pv)
        pv = self.pv[j]
        wait = max(self.base, self.evac_tok[j])
        idx = P.mm_rop(lambda E, pv=pv: mmfn(E, pv), wait)
        P.bridge(self.beng, P.mmsem, idx)
        evacfn(pv)
        self.evac_tok[j] = P.tok
        self.i += 1


class Consts:
    def __init__(self, P):
        nc = P.nc
        self.ones_col = P.sb("c_ones_col", [128, 1], BF16)
        self.ones_row = P.sb("c_ones_row", [1, 512], BF16)
        self.ident = P.sb("c_ident", [128, 128], BF16)
        self.ident32 = P.sb("c_ident32", [128, 128], F32)
        h_oc = nc.inline_tensor(np.ones((128, 1), dtype=NBF), "h_ones_col")
        h_or = nc.inline_tensor(np.ones((1, 512), dtype=NBF), "h_ones_row")
        h_id = nc.inline_tensor(np.eye(128, dtype=NBF), "h_ident")
        h_id32 = nc.inline_tensor(np.eye(128, dtype=np.float32), "h_ident32")
        P.dma(self.ones_col[:, :], h_oc[:, :])
        P.dma(self.ones_row[:, :], h_or[:, :])
        P.dma(self.ident[:, :], h_id[:, :])
        P.dma(self.ident32[:, :], h_id32[:, :])



def load_x_fm(P, xd, xres, nkt=NKT, t=T):
    for kf in range(nkt):
        src = xd[:, kf * 128:(kf + 1) * 128, :].rearrange("b p t -> p b t")
        dst = xres[:, kf * t:(kf + 1) * t].rearrange("p (b t) -> p b t", b=B)
        P.dma(dst, src)


def store_x_fm(P, xres, xd, nkt=NKT, t=T):
    for kf in range(nkt):
        dst = xd[:, kf * 128:(kf + 1) * 128, :].rearrange("b p t -> p b t")
        src = xres[:, kf * t:(kf + 1) * t].rearrange("p (b t) -> p b t", b=B)
        P.dma(dst, src)



class LnScratch:

    def __init__(self, P, pst, pbc):
        self.sq = P.sb("ln_sq", [128, 512], BF16)
        self.rowm = P.sb("ln_rowm", [1, 512], F32)
        self.rowe = P.sb("ln_rowe", [1, 512], F32)
        self.rowa = P.sb("ln_rowa", [1, 512], F32)
        self.rowbf = P.sb("ln_rowbf", [1, 1024], BF16)
        self.u32 = P.sb("ln_u32", [128, 512], F32)
        self.pst = pst
        self.pbc = pbc


def emit_ln(P, C, L, xres, xhat, gamma_col=None, beta_col=None, t=T, nkt=NKT, out32=None):
    nch = t // 512
    for kf in range(nkt):
        P.op("vector", lambda E, kf=kf: E.tensor_copy(
            xhat[:, kf * t:(kf + 1) * t], xres[:, kf * t:(kf + 1) * t]))
    for c in range(nch):
        c0 = c * 512
        def mm_sum(E, c0=c0):
            last = None
            for kf in range(nkt):
                last = E.matmul(L.pst[0:1, 0:512], C.ones_col[:, :],
                                xhat[:, kf * t + c0:kf * t + c0 + 512],
                                start=(kf == 0), stop=(kf == nkt - 1))
            return last
        P.op("tensor", mm_sum)
        for kf in range(nkt):
            P.op("scalar", lambda E, kf=kf, c0=c0: E.activation(
                L.sq[:, :], xres[:, kf * t + c0:kf * t + c0 + 512], AF.Square))
            P.op("tensor", lambda E, kf=kf: E.matmul(
                L.pst[0:1, 512:1024], C.ones_col[:, :], L.sq[:, :],
                start=(kf == 0), stop=(kf == nkt - 1)))
        P.op("vector", lambda E: E.tensor_scalar_mul(L.rowm[:, :], L.pst[0:1, 0:512], 1.0 / D))
        P.op("vector", lambda E: E.tensor_scalar_mul(L.rowe[:, :], L.pst[0:1, 512:1024], 1.0 / D))
        P.op("vector", lambda E: E.tensor_mul(L.rowa[:, :], L.rowm[:, :], L.rowm[:, :]))
        P.op("vector", lambda E: E.tensor_sub(L.rowe[:, :], L.rowe[:, :], L.rowa[:, :]))
        P.op("vector", lambda E: E.tensor_scalar_add(L.rowe[:, :], L.rowe[:, :], LN_EPS))
        P.op("scalar", lambda E: E.activation(L.rowe[:, :], L.rowe[:, :], AF.Sqrt))
        P.op("vector", lambda E: E.reciprocal(L.rowa[:, :], L.rowe[:, :]))
        P.op("vector", lambda E: E.tensor_mul(L.rowm[:, :], L.rowm[:, :], L.rowa[:, :]))
        P.op("vector", lambda E: E.tensor_scalar_mul(L.rowm[:, :], L.rowm[:, :], -1.0))
        P.op("vector", lambda E: E.tensor_copy(L.rowbf[0:1, 0:512], L.rowa[:, :]))
        P.op("vector", lambda E: E.tensor_copy(L.rowbf[0:1, 512:1024], L.rowm[:, :]))
        def mm_bc(E):
            E.matmul(L.pbc[:, 0:512], C.ones_row[0:1, 0:128], L.rowbf[0:1, 0:512],
                     start=True, stop=True)
            return E.matmul(L.pbc[:, 512:1024], C.ones_row[0:1, 0:128], L.rowbf[0:1, 512:1024],
                            start=True, stop=True)
        P.op("tensor", mm_bc)
        for kf in range(nkt):
            xv = xres[:, kf * t + c0:kf * t + c0 + 512]
            hv = xhat[:, kf * t + c0:kf * t + c0 + 512]
            P.op("vector", lambda E, xv=xv: E.tensor_mul(L.u32[:, :], xv, L.pbc[:, 0:512]))
            if out32 is not None:
                ov = out32[:, kf * t + c0:kf * t + c0 + 512]
                P.op("vector", lambda E, ov=ov: E.tensor_add(ov, L.u32[:, :], L.pbc[:, 512:1024]))
            elif gamma_col is None:
                P.op("vector", lambda E, hv=hv: E.tensor_add(hv, L.u32[:, :], L.pbc[:, 512:1024]))
            else:
                P.op("vector", lambda E: E.tensor_add(L.u32[:, :], L.u32[:, :], L.pbc[:, 512:1024]))
                g = gamma_col[kf]
                bcol = beta_col[kf]
                P.op("vector", lambda E, hv=hv, g=g, bcol=bcol: E.tensor_scalar(
                    hv, L.u32[:, :], g, bcol, ALU.mult, ALU.add))



def load_w_nat(P, w_ap, In, Out, arena, aoff, col0=0, kgran=128, wait=None):
    views = []
    nk = (In + kgran - 1) // kgran
    for k in range(nk):
        r = min(kgran, In - k * kgran)
        dst = arena[0:r, aoff + k * Out: aoff + (k + 1) * Out]
        srcv = w_ap[k * kgran:k * kgran + r, col0:col0 + Out]
        if wait is None:
            P.cast_dma(dst, srcv)
        else:
            P.rop("gpsimd", lambda E, dst=dst, srcv=srcv: E.dma_start(
                out=dst, in_=srcv), wait, P.wsem, inc=16)
            P.wcnt += 16
        views.append(arena[0:r, aoff + k * Out: aoff + (k + 1) * Out])
    return views


def load_w_tr(P, C, w_ap, Out, In, arena, aoff, nat, psT, kgran=128):
    nmo = (Out + 127) // 128
    nki = (In + kgran - 1) // kgran
    for mo in range(nmo):
        rm = min(128, Out - mo * 128)
        P.cast_dma(nat[0:rm, 0:In], w_ap[mo * 128:mo * 128 + rm, :])
        for ki in range(nki):
            rk = min(kgran, In - ki * kgran)
            P.op("tensor", lambda E, ki=ki, rk=rk, rm=rm: E.transpose(
                psT[0:rk, 0:rm], nat[0:rm, ki * kgran:ki * kgran + rk], C.ident[0:rm, 0:rm]))
            P.op("vector", lambda E, ki=ki, mo=mo, rk=rk, rm=rm: E.tensor_copy(
                arena[0:rk, aoff + ki * Out + mo * 128: aoff + ki * Out + mo * 128 + rm],
                psT[0:rk, 0:rm]))
    return [arena[0:min(kgran, In - ki * kgran), aoff + ki * Out: aoff + (ki + 1) * Out]
            for ki in range(nki)]


def load_row(P, v_ap, n, arena, aoff, cast=True):
    dst = arena[0:1, aoff:aoff + n]
    P.cast_dma(dst, v_ap[:].rearrange("n -> 1 n") if len(v_ap.shape) == 1 else v_ap)
    return dst


def rows_to_cols(P, C, row_bf, n, outsb, pscol):
    nt = n // 128
    def mm(E):
        last = None
        for i in range(nt):
            last = E.matmul(pscol[:, i:i + 1], row_bf[0:1, i * 128:(i + 1) * 128],
                            C.ones_row[0:1, 0:1], start=True, stop=True)
        return last
    P.op("tensor", mm)
    P.op("vector", lambda E: E.tensor_copy(outsb[:, 0:nt], pscol[:, 0:nt]))
    return outsb



def emit_proj_fm(P, C, wk_views, rhs_fn, out_fn, mtiles, nch=NCH, psum=None,
                 bias_row=None, extra_fn=None, mstep=128):
    pp = PP(P, [psum[:, 0:512], psum[:, 512:1024]])
    for m in range(mtiles):
        for c in range(nch):
            def mm(E, pv, m=m, c=c):
                last = None
                terms = []
                for k, wv in enumerate(wk_views):
                    terms.append((wv[:, m * 128:(m + 1) * 128], rhs_fn(k, c)))
                if bias_row is not None:
                    terms.append((bias_row[0:1, m * 128:(m + 1) * 128], C.ones_row[0:1, 0:512]))
                if extra_fn is not None:
                    terms.extend(extra_fn(m, c))
                for i, (lv, rv) in enumerate(terms):
                    last = E.matmul(pv, lv, rv, start=(i == 0), stop=(i == len(terms) - 1))
                return last
            pp.step(mm, lambda pv, m=m, c=c: out_fn(m, c, pv))



def emit_l0_attn(P, C, G, xd, xo):
    nc = P.nc
    wq_d = G["L0_attn_Wq"]
    wk_d = G["L0_attn_Wk"]
    wv_d = G["L0_attn_Wv"]
    wr_d = G["L0_attn_Wr"]

    cc_in = P.dram("cc_in", [1, B * D], F32)
    cc_out = P.dram("cc_out", [1, B * D], F32)

    xres = P.sb("xres", [128, NKT * T], F32)
    xhat = P.sb("xhat", [128, NKT * T], BF16)
    WQ = P.sb("WQ", [128, NKT * D], BF16)
    WS = P.sb("WS", [128, NKT * D], BF16)
    U = P.sb("U", [128, NKT * T], BF16)
    SC1 = P.sb("SC1", [128, 1024], F32)
    SCe = P.sb("SCe", [128, 1024], BF16)
    colm = P.sb("colm", [128, 1], F32)
    cols = P.sb("cols", [128, 1], F32)
    colr = P.sb("colr", [128, 1], F32)
    gq_acc = P.sb("gq_acc", [1, B * D], F32)
    gq_bf = P.sb("gq_bf", [1, B * D], BF16)
    gkcol = P.sb("gkcol", [128, 32], F32)

    pq = P.ps("pq", [128, 1024], F32)
    pbc = P.ps("pbc2", [128, 1024], F32)
    pr = P.ps("pr", [1, 1024], F32)
    pcol = P.ps("pcol", [128, 32], F32)
    L = LnScratch(P, pr, pbc)

    load_x_fm(P, xd, xres)
    emit_ln(P, C, L, xres, xhat)

    wq = load_w_nat(P, wq_d, D, D, WQ, 0)
    wk = load_w_nat(P, wk_d, D, D, WS, 0)

    SCALE = float(D) ** -0.5

    def softmax_weighted_partial(tt, src_sbuf):
        b = tt // (TPC // 128)
        P.op("vector", lambda E: E.reduce_max(colm[:, :], src_sbuf, axis=AX.X, negate=True))
        P.op("vector", lambda E: E.tensor_scalar_mul(colm[:, :], colm[:, :], SCALE))
        P.op("scalar", lambda E: E.activation(SCe[:, :], src_sbuf, AF.Exp,
                                              bias=colm[:, :], scale=SCALE,
                                              accum_out=cols[:, :]))
        P.op("vector", lambda E: E.reciprocal(colr[:, :], cols[:, :]))
        P.op("vector", lambda E: E.tensor_scalar_mul(SCe[:, :], SCe[:, :], colr[:, :]))
        P.op("vector", lambda E: E.tensor_mul(SCe[:, :], src_sbuf, SCe[:, :]))
        def mm(E, b=b):
            E.matmul(pr[0:1, 0:512], C.ones_col[:, :], SCe[:, 0:512], start=True, stop=True)
            return E.matmul(pr[0:1, 512:1024], C.ones_col[:, :], SCe[:, 512:1024], start=True, stop=True)
        P.op("tensor", mm)
        P.op("vector", lambda E, b=b: E.tensor_add(
            gq_acc[0:1, b * D:(b + 1) * D], gq_acc[0:1, b * D:(b + 1) * D], pr[0:1, 0:1024]))

    P.op("vector", lambda E: E.memset(gq_acc[:, :], 0.0))
    for tt in range(NTT):
        t0 = tt * 128

        def mmq(E, t0=t0):
            last = None
            for n in range(2):
                for k in range(NKT):
                    last = E.matmul(pq[:, n * 512:(n + 1) * 512],
                                    xhat[:, k * T + t0:k * T + t0 + 128],
                                    wq[k][:, n * 512:(n + 1) * 512],
                                    start=(k == 0), stop=(k == NKT - 1))
            return last
        P.op("tensor", mmq)
        P.op("vector", lambda E: E.tensor_copy(SC1[:, :], pq[:, :]))
        softmax_weighted_partial(tt, SC1[:, :])

    P.dma(cc_in[:, :], gq_acc[:, :])
    P.cc("AllReduce", ALU.add, [cc_in.ap().opt()], [cc_out.ap().opt()])
    P.dma(gq_acc[:, :], cc_out[:, :])
    P.op("vector", lambda E: E.tensor_copy(gq_bf[:, :], gq_acc[:, :]))

    gk_acc = gq_acc
    P.op("vector", lambda E: E.memset(gk_acc[:, :], 0.0))
    for tt in range(NTT):
        t0 = tt * 128
        b = tt // (TPC // 128)

        def mmk(E, t0=t0):
            last = None
            for n in range(2):
                for k in range(NKT):
                    last = E.matmul(pq[:, n * 512:(n + 1) * 512],
                                    xhat[:, k * T + t0:k * T + t0 + 128],
                                    wk[k][:, n * 512:(n + 1) * 512],
                                    start=(k == 0), stop=(k == NKT - 1))
            return last
        P.op("tensor", mmk)

        def mmbc(E, b=b):
            E.matmul(pbc[:, 0:512], C.ones_row[0:1, 0:128], gq_bf[0:1, b * D:b * D + 512],
                     start=True, stop=True)
            return E.matmul(pbc[:, 512:1024], C.ones_row[0:1, 0:128],
                            gq_bf[0:1, b * D + 512:(b + 1) * D], start=True, stop=True)
        P.op("tensor", mmbc)
        P.op("vector", lambda E: E.tensor_copy(SC1[:, :], pq[:, :]))
        P.op("vector", lambda E: E.tensor_mul(SC1[:, :], SC1[:, :], pbc[:, :]))
        softmax_weighted_partial(tt, SC1[:, :])

    P.dma(cc_in[:, :], gk_acc[:, :])
    P.cc("AllReduce", ALU.add, [cc_in.ap().opt()], [cc_out.ap().opt()])
    P.dma(gk_acc[:, :], cc_out[:, :])
    P.op("vector", lambda E: E.tensor_copy(gq_bf[:, :], gk_acc[:, :]))

    def mmcols(E):
        last = None
        for b in range(B):
            for kf in range(NKT):
                last = E.matmul(pcol[:, b * 8 + kf:b * 8 + kf + 1],
                                gq_bf[0:1, b * D + kf * 128:b * D + (kf + 1) * 128],
                                C.ones_row[0:1, 0:1], start=True, stop=True)
        return last
    P.op("tensor", mmcols)
    P.op("vector", lambda E: E.tensor_copy(gkcol[:, 0:32], pcol[:, 0:32]))

    wv = load_w_nat(P, wv_d, D, D, WS, 0)

    def rhs_x(k, c):
        return xhat[:, k * T + c * 512:k * T + (c + 1) * 512]

    def out_u(m, c, pv):
        P.op("vector", lambda E, m=m, c=c, pv=pv: E.tensor_scalar_mul(
            U[:, m * T + c * 512:m * T + (c + 1) * 512], pv, gkcol[:, c * 8 + m:c * 8 + m + 1]))

    emit_proj_fm(P, C, wv, rhs_x, out_u, NKT, psum=pq)

    wr = load_w_nat(P, wr_d, D, D, WS, 0)

    def out_res(m, c, pv):
        xv = xres[:, m * T + c * 512:m * T + (c + 1) * 512]
        P.op("vector", lambda E, xv=xv, pv=pv: E.tensor_add(xv, xv, pv))

    def extra_q(m, c):
        return [(wq[k][:, m * 128:(m + 1) * 128], rhs_x(k, c)) for k in range(NKT)]

    def rhs_u(k, c):
        return U[:, k * T + c * 512:k * T + (c + 1) * 512]

    emit_proj_fm(P, C, wr, rhs_u, out_res, NKT, psum=pq, extra_fn=extra_q)

    store_x_fm(P, xres, xo)


def _g_weights(nc, names, w_shapes):
    return {n: nc.dram_tensor(n, list(w_shapes[n]), F32, kind="ExternalInput")
            for n in names}


W_SHAPES = {}


def build_l0_attn(nc):
    P = Prog(nc)
    xd = nc.dram_tensor("xin", [B, D, TPC], F32, kind="ExternalInput")
    xo = nc.dram_tensor("xout", [B, D, TPC], F32, kind="ExternalOutput")
    G = _g_weights(nc, ["L0_attn_Wq", "L0_attn_Wk", "L0_attn_Wv", "L0_attn_Wr"], W_SHAPES)
    C = Consts(P)
    with P.scope("A0_"):
        emit_l0_attn(P, C, G, xd, xo)
    P.finalize()
    return P


def emit_l0_ffn(P, C, G, xd, xo):
    nc = P.nc
    W_d = G["L0_ffn_W"]
    Wp_d = G["L0_ffn_Wp"]
    bp_d = G["L0_ffn_bp"]
    sim_d = G["L0_ffn_sim"]
    temp_d = G["L0_ffn_temp"]
    RANKS = [1, 2, 4, 8]
    wd_d = [G[f"L0_ffn_Wd{i}"] for i in range(4)]
    wu_d = [G[f"L0_ffn_Wu{i}"] for i in range(4)]

    xres = P.sb("xres", [128, NKT * T], F32)
    xhat = P.sb("xhat", [128, NKT * T], BF16)
    WT = P.sb("WT", [128, NKT * D], BF16)
    WpT = P.sb("WpT", [128, NKT * 256], BF16)
    WdT = P.sb("WdT", [128, NKT * 15], BF16)
    WuT = P.sb("WuT", [8, 4 * D], BF16)
    NAT = P.sb("NAT", [128, D], BF16)
    PJ = P.sb("PJ", [128, 2 * T], BF16)
    SQ = P.sb("SQ", [128, 512], BF16)
    bpr = P.sb("bpr", [1, 256], BF16)
    simS = P.sb("simS", [128, 8], F32)
    sim4 = P.sb("sim4", [128, 8], BF16)
    rsim = P.sb("rsim", [1, 8], F32)
    tmp1 = P.sb("tmp1", [1, 512], F32)
    rns = P.sb("rns", [1, T], BF16)
    rnf = P.sb("rnf", [1, 512], F32)
    scal = P.sb("scal", [1, 1], F32)
    LG = P.sb("LG", [4, T], F32)
    lgf = P.sb("lgf", [4, 512], F32)
    LGT = P.sb("LGT", [128, 4], F32)
    ecol = P.sb("ecol", [128, 4], F32)
    colm = P.sb("colm", [128, 1], F32)
    cols = P.sb("cols", [128, 1], F32)
    colr = P.sb("colr", [128, 1], F32)
    MSK = P.sb("MSK", [128, 4], F32)
    RT = P.sb("RT", [128, 4], F32)
    RTF = P.sb("RTF", [1, 4 * T], BF16)
    DSA = P.sb("DSA", [8, 4 * 512], BF16)
    dff = P.sb("dff", [8, 512], F32)

    pmain = P.ps("pmain", [128, 1024], F32)
    pbc = P.ps("pbcf", [128, 1024], F32)
    pst = P.ps("pstf", [1, 1024], F32)
    psT = P.ps("psTf", [128, 128], BF16)

    L = LnScratch(P, pst, pbc)

    load_x_fm(P, xd, xres)
    emit_ln(P, C, L, xres, xhat)

    wt = load_w_tr(P, C, W_d.ap(), D, D, WT, 0, NAT, psT)
    wpt = load_w_tr(P, C, Wp_d.ap(), 256, D, WpT, 0, NAT, psT)
    wdt = []
    off = 0
    for i, r in enumerate(RANKS):
        wdt.append(load_w_tr(P, C, wd_d[i].ap(), r, D, WdT, off, NAT, psT))
        off += NKT * r
    wut = []
    for i, r in enumerate(RANKS):
        wut.append(load_w_tr(P, C, wu_d[i].ap(), D, r, WuT, i * D, NAT, psT)[0])

    P.cast_dma(bpr[:, :], bp_d[:].rearrange("(a n) -> a n", a=1))

    P.dma(simS[:, 0:4], sim_d[0:128, :])
    P.dma(simS[:, 4:8], sim_d[128:256, :])
    P.op("scalar", lambda E: E.activation(SQ[:, 0:8], simS[:, 0:8], AF.Square))
    def mm_sim(E):
        E.matmul(pst[0:1, 0:4], C.ones_col[:, :], SQ[:, 0:4], start=True, stop=False)
        return E.matmul(pst[0:1, 0:4], C.ones_col[:, :], SQ[:, 4:8], start=False, stop=True)
    P.op("tensor", mm_sim)
    P.op("scalar", lambda E: E.activation(rsim[0:1, 0:4], pst[0:1, 0:4], AF.Sqrt))
    P.op("vector", lambda E: E.tensor_scalar_max(rsim[0:1, 0:4], rsim[0:1, 0:4], 1e-12))
    P.op("vector", lambda E: E.reciprocal(rsim[0:1, 0:4], rsim[0:1, 0:4]))
    rsimb = P.sb("rsimb", [1, 4], BF16)
    P.op("vector", lambda E: E.tensor_copy(rsimb[0:1, 0:4], rsim[0:1, 0:4]))
    P.op("tensor", lambda E: E.matmul(pbc[:, 0:4], C.ones_row[0:1, 0:128], rsimb[0:1, 0:4],
                                      start=True, stop=True))
    P.op("vector", lambda E: E.tensor_mul(sim4[:, 0:4], simS[:, 0:4], pbc[:, 0:4]))
    P.op("vector", lambda E: E.tensor_mul(sim4[:, 4:8], simS[:, 4:8], pbc[:, 0:4]))

    P.dma(scal[:, :], temp_d[:].rearrange("(a n) -> a n", a=1))
    P.op("vector", lambda E: E.tensor_scalar_min(scal[:, :], scal[:, :], 4.605170185988092))
    P.op("scalar", lambda E: E.activation(scal[:, :], scal[:, :], AF.Exp))

    for c in range(NCH):
        c0 = c * 512
        for m in range(2):
            def mm_pj(E, m=m, c0=c0):
                last = None
                for k in range(NKT):
                    last = E.matmul(pmain[:, 0:512], wpt[k][:, m * 128:(m + 1) * 128],
                                    xhat[:, k * T + c0:k * T + c0 + 512],
                                    start=(k == 0), stop=False)
                return E.matmul(pmain[:, 0:512], bpr[0:1, m * 128:(m + 1) * 128],
                                C.ones_row[0:1, 0:512], start=False, stop=True)
            P.op("tensor", mm_pj)
            P.op("vector", lambda E, m=m, c0=c0: E.tensor_copy(
                PJ[:, m * T + c0:m * T + c0 + 512], pmain[:, 0:512]))
        for m in range(2):
            P.op("scalar", lambda E, m=m, c0=c0: E.activation(
                SQ[:, 0:512], PJ[:, m * T + c0:m * T + c0 + 512], AF.Square))
            P.op("tensor", lambda E, m=m: E.matmul(
                pst[0:1, 0:512], C.ones_col[:, :], SQ[:, 0:512],
                start=(m == 0), stop=(m == 1)))
        P.op("scalar", lambda E: E.activation(rnf[0:1, :], pst[0:1, 0:512], AF.Sqrt))
        P.op("vector", lambda E: E.tensor_scalar_max(rnf[0:1, :], rnf[0:1, :], 1e-12))
        P.op("vector", lambda E: E.reciprocal(rnf[0:1, :], rnf[0:1, :]))
        P.op("vector", lambda E: E.tensor_scalar_mul(rnf[0:1, :], rnf[0:1, :], scal[0:1, 0:1]))
        P.op("vector", lambda E, c0=c0: E.tensor_copy(rns[0:1, c0:c0 + 512], rnf[0:1, :]))
        def mm_lg(E, c0=c0):
            E.matmul(pmain[0:4, 0:512], sim4[:, 0:4], PJ[:, 0 * T + c0:0 * T + c0 + 512],
                     start=True, stop=False)
            return E.matmul(pmain[0:4, 0:512], sim4[:, 4:8], PJ[:, 1 * T + c0:1 * T + c0 + 512],
                            start=False, stop=True)
        P.op("tensor", mm_lg)
        P.op("tensor", lambda E, c0=c0: E.matmul(
            pbc[0:4, 0:512], C.ones_row[0:1, 0:4], rns[0:1, c0:c0 + 512],
            start=True, stop=True))
        P.op("vector", lambda E: E.tensor_copy(lgf[:, :], pmain[0:4, 0:512]))
        P.op("vector", lambda E, c0=c0: E.tensor_mul(LG[0:4, c0:c0 + 512], lgf[:, :], pbc[0:4, 0:512]))

    for tt in range(NTT):
        t0 = tt * 128
        P.op("tensor", lambda E, t0=t0: E.transpose(pbc[:, 0:4], LG[0:4, t0:t0 + 128], C.ident32[0:4, 0:4]))
        P.op("vector", lambda E: E.tensor_copy(LGT[:, :], pbc[:, 0:4]))
        P.op("vector", lambda E: E.reduce_max(colm[:, :], LGT[:, :], axis=AX.X, negate=True))
        P.op("scalar", lambda E: E.activation(ecol[:, :], LGT[:, :], AF.Exp,
                                              bias=colm[:, :], accum_out=cols[:, :]))
        P.op("vector", lambda E: E.reciprocal(colr[:, :], cols[:, :]))
        P.op("vector", lambda E: E.tensor_scalar_mul(ecol[:, :], ecol[:, :], colr[:, :]))
        P.op("vector", lambda E: E.reduce_max(colm[:, :], ecol[:, :], axis=AX.X))
        P.op("vector", lambda E: E.tensor_scalar(MSK[:, :], ecol[:, :], colm[:, :], None, ALU.is_ge))
        P.op("vector", lambda E: E.tensor_mul(RT[:, :], ecol[:, :], MSK[:, :]))
        for e in range(4):
            P.op("tensor", lambda E, e=e: E.transpose(pst[0:1, 0:128], RT[:, e:e + 1], C.ident32[:, :]))
            P.op("vector", lambda E, e=e, t0=t0: E.tensor_copy(
                RTF[0:1, e * T + t0:e * T + t0 + 128], pst[0:1, 0:128]))

    for c in range(NCH):
        c0 = c * 512
        for e, r in enumerate(RANKS):
            def mm_d(E, e=e, r=r, c0=c0):
                last = None
                for k in range(NKT):
                    last = E.matmul(pmain[0:r, 512:1024], wdt[e][k],
                                    xhat[:, k * T + c0:k * T + c0 + 512],
                                    start=(k == 0), stop=(k == NKT - 1))
                return last
            P.op("tensor", mm_d)
            P.op("vector", lambda E, r=r: E.tensor_copy(dff[0:r, :], pmain[0:r, 512:1024]))
            P.op("tensor", lambda E, e=e, r=r, c0=c0: E.matmul(
                pbc[0:r, 0:512], C.ones_row[0:1, 0:r], RTF[0:1, e * T + c0:e * T + c0 + 512],
                start=True, stop=True))
            P.op("vector", lambda E, e=e, r=r: E.tensor_mul(
                DSA[0:r, e * 512:(e + 1) * 512], dff[0:r, :], pbc[0:r, 0:512]))
        for m in range(NKT):
            def mm_base(E, m=m, c0=c0, c=c):
                last = None
                for k in range(NKT):
                    last = E.matmul(pmain[:, 0:512], wt[k][:, m * 128:(m + 1) * 128],
                                    xhat[:, k * T + c0:k * T + c0 + 512],
                                    start=(k == 0), stop=False)
                for e, r in enumerate(RANKS):
                    last = E.matmul(pmain[:, 0:512], wut[e][0:r, m * 128:(m + 1) * 128],
                                    DSA[0:r, e * 512:(e + 1) * 512],
                                    start=False, stop=(e == len(RANKS) - 1))
                return last
            P.op("tensor", mm_base)
            xv = xres[:, m * T + c0:m * T + c0 + 512]
            P.op("vector", lambda E, xv=xv: E.tensor_add(xv, xv, pmain[:, 0:512]))

    store_x_fm(P, xres, xo)


L0FFN_NAMES = ["L0_ffn_W", "L0_ffn_Wp", "L0_ffn_bp", "L0_ffn_sim", "L0_ffn_temp",
               "L0_ffn_Wd0", "L0_ffn_Wu0", "L0_ffn_Wd1", "L0_ffn_Wu1",
               "L0_ffn_Wd2", "L0_ffn_Wu2", "L0_ffn_Wd3", "L0_ffn_Wu3"]


def build_l0_ffn(nc):
    P = Prog(nc)
    xd = nc.dram_tensor("xin", [B, D, TPC], F32, kind="ExternalInput")
    xo = nc.dram_tensor("xout", [B, D, TPC], F32, kind="ExternalOutput")
    G = _g_weights(nc, L0FFN_NAMES, W_SHAPES)
    C = Consts(P)
    with P.scope("F0_"):
        emit_l0_ffn(P, C, G, xd, xo)
    P.finalize()
    return P





def _emit_l1attn_wo(P, C, wo, OHB, brow, pmain, xres, b, half):
    pp = PP(P, [pmain[:, 0:512], pmain[:, 512:1024]])
    for m in range(NKT):
        def mmo(E, pv2, m=m, half=half):
            last = None
            for k in range(8):
                last = E.matmul(pv2, wo[half * 8 + k][:, m * 128:(m + 1) * 128],
                                OHB[:, k * 512:(k + 1) * 512],
                                start=(k == 0), stop=(half == 1 and k == 7))
            if half == 0:
                return E.matmul(pv2, brow[0:1, D + m * 128:D + (m + 1) * 128],
                                C.ones_row[0:1, 0:512], start=False, stop=True)
            return last
        def evo(pv2, m=m, b=b):
            xv = xres[:, m * T + b * 512:m * T + (b + 1) * 512]
            P.op("vector", lambda E, xv=xv: E.tensor_add(xv, xv, pv2))
        pp.step(mmo, evo)

DBG_L1A = False


def emit_l1_attn(P, C, G, xd, xo, dbg=None, dbg2=None):
    nc = P.nc
    wq_d = G["L1_attn_Wq"]
    wk_d = G["L1_attn_Wk"]
    wv_d = G["L1_attn_Wv"]
    wo_d = G["L1_attn_Wo"]
    bq_d = G["L1_attn_bq"]
    bk_d = G["L1_attn_bk"]
    bv_d = G["L1_attn_bv"]
    bo_d = G["L1_attn_bo"]

    kin = P.dram("kin", [16 * B * 64, TPC], BF16)
    kout = P.dram("kout", [NCORE * 16 * B * 64, TPC], BF16)
    vin = P.dram("vin", [T, D], BF16)
    vout = P.dram("vout", [NCORE * T, D], BF16)
    mk_in = P.dram("mk_in", [1, 64], F32)
    mk_out = P.dram("mk_out", [1, 64], F32)

    xres = P.sb("xres", [128, NKT * T], F32)
    xhat = P.sb("xhat", [128, NKT * T], BF16)
    WA = P.sb("WA", [128, NKT * D], BF16)
    WO = P.sb("WO", [64, 16 * D], BF16)
    NAT = P.sb("NAT", [128, D], BF16)
    brow = P.sb("brow", [1, 2 * D], BF16)
    KT = P.sb("KT", [65, 2 * S], BF16)
    VT = P.sb("VT", [128, 2 * 32 * 65], BF16)
    QH = P.sb("QH", [65, 512], BF16)
    EXPT = P.sb("EXPT", [128, 1024], BF16)
    KOW = QH
    VTM = EXPT
    OHB = P.sb("OHB", [64, 8 * 512], BF16)
    SQH = P.sb("SQH", [64, 512], BF16)
    rowq = P.sb("rowq", [1, 512], F32)
    rowd = rowq
    mkrow = P.sb("mkrow", [1, 64], F32)
    mkpart = P.sb("mkpart", [1, 64], F32)
    RB = SQH

    pmain = P.ps("pmain", [128, 1024], F32)
    pst = P.ps("pst", [1, 1024], F32)
    pbc = P.ps("pbc", [128, 1024], F32)
    pv = P.ps("pv", [65, 512], F32)
    psT = P.ps("psT", [128, 128], BF16)

    L = LnScratch(P, pst, pbc)
    rdbf = L.rowbf[0:1, 0:512]

    load_x_fm(P, xd, xres)
    emit_ln(P, C, L, xres, xhat)

    for i, bd in ((0, bq_d), (1, bk_d)):
        P.cast_dma(brow[0:1, i * D:(i + 1) * D], bd[:].rearrange("(a n) -> a n", a=1))

    wk = load_w_tr(P, C, wk_d.ap(), D, D, WA, 0, NAT, psT)
    for h in range(16):
        for c in range(NCH):
            def mmk(E, h=h, c=c):
                last = None
                for k in range(NKT):
                    last = E.matmul(pmain[0:64, 0:512], wk[k][:, h * 64:(h + 1) * 64],
                                    xhat[:, k * T + c * 512:k * T + (c + 1) * 512],
                                    start=(k == 0), stop=False)
                return E.matmul(pmain[0:64, 0:512], brow[0:1, D + h * 64:D + (h + 1) * 64],
                                C.ones_row[0:1, 0:512], start=False, stop=True)
            P.op("tensor", mmk)
            P.op("vector", lambda E: E.tensor_copy(KOW[0:64, :], pmain[0:64, 0:512]))
            P.dma(kin[(h * B + c) * 64:(h * B + c + 1) * 64, :], KOW[0:64, :])
            P.op("scalar", lambda E: E.activation(SQH[:, :], KOW[0:64, :], AF.Square))
            P.op("tensor", lambda E: E.matmul(pst[0:1, 0:512], C.ones_col[0:64, :],
                                              SQH[:, :], start=True, stop=True))
            P.op("vector", lambda E, h=h, c=c: E.reduce_max(
                mkpart[0:1, h * B + c:h * B + c + 1], pst[0:1, 0:512], axis=AX.X))
    P.dma(mk_in[:, :], mkpart[:, :])
    P.cc("AllReduce", ALU.max, [mk_in.ap().opt()], [mk_out.ap().opt()])
    P.dma(mkrow[:, :], mk_out[:, :])
    P.op("scalar", lambda E: E.activation(mkrow[:, :], mkrow[:, :], AF.Sqrt))

    wv = load_w_tr(P, C, wv_d.ap(), D, D, WA, 0, NAT, psT)
    P.cast_dma(brow[0:1, D:2 * D], bv_d[:].rearrange("(a n) -> a n", a=1))
    for tt in range(NTT):
        t0 = tt * 128
        def mmv(E, t0=t0):
            last = None
            for n in range(2):
                for k in range(NKT):
                    last = E.matmul(pmain[:, n * 512:(n + 1) * 512],
                                    xhat[:, k * T + t0:k * T + t0 + 128],
                                    wv[k][:, n * 512:(n + 1) * 512],
                                    start=(k == 0), stop=False)
                last = E.matmul(pmain[:, n * 512:(n + 1) * 512], C.ones_row[0:1, 0:128],
                                brow[0:1, D + n * 512:D + (n + 1) * 512],
                                start=False, stop=True)
            return last
        P.op("tensor", mmv)
        P.op("vector", lambda E: E.tensor_copy(VTM[:, :], pmain[:, :]))
        P.dma(vin[t0:t0 + 128, :], VTM[:, :])

    P.cc("AllGather", ALU.bypass, [kin.ap().opt()], [kout.ap().opt()])
    P.cc("AllGather", ALU.bypass, [vin.ap().opt()], [vout.ap().opt()])

    wq = load_w_tr(P, C, wq_d.ap(), D, D, WA, 0, NAT, psT)
    wo = load_w_tr(P, C, wo_d.ap(), D, D, WO, 0, NAT, psT, kgran=64)

    P.cast_dma(brow[0:1, D:2 * D], bo_d[:].rearrange("(a n) -> a n", a=1))
    P.op("vector", lambda E: E.memset(VT[:, :], 1.0))
    P.op("vector", lambda E: E.memset(KT[64:65, :], 1.0))

    kout3 = kout.ap().rearrange("(r x) t -> r x t", r=NCORE)
    vout3 = vout.ap().rearrange("(r x) d -> r x d", r=NCORE)

    bh_list = [(b, h) for b in range(B) for h in range(16)]
    freed_tok = [0, 0]

    def emit_loads(i, relaxed):
        b, h = bh_list[i]
        buf = i % 2
        off = (h * B + b) * 64
        ktv = KT[0:64, buf * S:buf * S + S].rearrange("p (r t) -> p r t", r=NCORE)
        srck = kout3[:, off:off + 64, :].rearrange("r p t -> p r t")
        if relaxed:
            P.rop("gpsimd", lambda E: E.dma_start(out=ktv, in_=srck),
                  freed_tok[buf], P.wsem, inc=16)
            P.wcnt += 16
        else:
            P.dma(ktv, srck, eng="gpsimd")
        for r in range(NCORE):
            vsrc = vout3[r, b * 512:(b + 1) * 512, h * 64:(h + 1) * 64]
            vsrc = vsrc.rearrange("(n p) f -> p n f", p=128)
            vdst = VT[:, buf * 2080 + r * 260:buf * 2080 + (r + 1) * 260].rearrange(
                "p (n g) -> p n g", g=65)[:, :, 0:64]
            if relaxed:
                P.rop("gpsimd", lambda E, vdst=vdst, vsrc=vsrc: E.dma_start(
                    out=vdst, in_=vsrc), freed_tok[buf], P.wsem, inc=16)
                P.wcnt += 16
            else:
                P.dma(vdst, vsrc, eng="gpsimd")

    emit_loads(0, relaxed=False)
    for i, (b, h) in enumerate(bh_list):
        buf = i % 2
        ktv = KT[:, buf * S:(buf + 1) * S]
        vtb = buf * 2080
        P.bridge("vector", P.wsem, P.wcnt)
        if i + 1 < len(bh_list):
            emit_loads(i + 1, relaxed=True)
        def mmq(E, h=h, b=b):
            last = None
            for k in range(NKT):
                last = E.matmul(pmain[0:64, 0:512], wq[k][:, h * 64:(h + 1) * 64],
                                xhat[:, k * T + b * 512:k * T + (b + 1) * 512],
                                start=(k == 0), stop=False)
            return E.matmul(pmain[0:64, 0:512], brow[0:1, h * 64:(h + 1) * 64],
                            C.ones_row[0:1, 0:512], start=False, stop=True)
        P.op("tensor", mmq)
        P.op("scalar", lambda E: E.activation(QH[0:64, :], pmain[0:64, 0:512],
                                              AF.Copy, scale=0.125))
        P.op("scalar", lambda E: E.activation(SQH[:, :], QH[0:64, :], AF.Square))
        P.op("tensor", lambda E: E.matmul(pst[0:1, 512:1024], C.ones_col[0:64, :],
                                          SQH[:, :], start=True, stop=True))
        P.op("scalar", lambda E: E.activation(rowq[:, :], pst[0:1, 512:1024], AF.Sqrt))
        P.op("vector", lambda E, h=h, b=b: E.tensor_scalar_mul(
            rowq[:, :], rowq[:, :], mkrow[0:1, h * B + b:h * B + b + 1]))
        P.op("vector", lambda E: E.tensor_scalar_mul(rowq[:, :], rowq[:, :], -1.0))
        P.op("vector", lambda E: E.tensor_copy(QH[64:65, :], rowq[:, :]))
        base_tok = P.tok

        def mmst(E, it, ktv=ktv):
            j = it % 2
            return E.matmul(pmain[:, j * 512:(j + 1) * 512],
                            ktv[:, it * 128:(it + 1) * 128], QH[:, :],
                            start=True, stop=True)

        def mmpv(E, it, vtb=vtb):
            j = it % 2
            return E.matmul(pv[:, :], VT[:, vtb + it * 65:vtb + (it + 1) * 65],
                            EXPT[:, j * 512:(j + 1) * 512],
                            start=(it == 0), stop=(it == 31))

        s_base = P.scnt
        P.xop("tensor", lambda E, f=mmst: f(E, 0), P.mmsem, wait_tok=base_tok)
        P.mmcnt += 1
        for it in range(32):
            j = it % 2
            st_pos = P.mmcnt
            P.xop("scalar", lambda E, j=j: E.activation(
                EXPT[:, j * 512:(j + 1) * 512], pmain[:, j * 512:(j + 1) * 512],
                AF.Exp), P.ssem, extra=((P.mmsem, st_pos),))
            P.scnt += 1
            if it + 1 < 32:
                P.xop("tensor", lambda E, it=it, f=mmst: f(E, it + 1), P.mmsem,
                      extra=((P.ssem, s_base + it),))
                P.mmcnt += 1
            P.xop("tensor", lambda E, it=it, f=mmpv: f(E, it), P.mmsem,
                  extra=((P.ssem, s_base + it + 1),))
            P.mmcnt += 1
        P.bridge("vector", P.mmsem, P.mmcnt)
        P.op("vector", lambda E: E.tensor_copy(rowd[:, :], pv[64:65, :]))
        P.op("vector", lambda E: E.reciprocal(rowd[:, :], rowd[:, :]))
        P.op("vector", lambda E: E.tensor_copy(rdbf[:, :], rowd[:, :]))
        P.op("tensor", lambda E: E.matmul(pbc[0:64, 0:512], C.ones_row[0:1, 0:64],
                                          rdbf[0:1, :], start=True, stop=True))
        P.op("vector", lambda E: E.tensor_copy(RB[0:64, :], pbc[0:64, 0:512]))
        P.op("vector", lambda E, h=h: E.tensor_mul(OHB[:, (h % 8) * 512:(h % 8 + 1) * 512],
                                                   pv[0:64, :], RB[0:64, :]))
        if DBG_L1A and i == 0:
            P.dma(dbg[0:65, 0:4096], KT[:, 0:S])
            P.dma(dbg[65:130, 0:512], QH[:, :])
            P.dma(dbg[0:128, 4096:6176], VT[:, 0:2080])
            P.dma(dbg[0:128, 6688:7712], EXPT[:, :])
            P.op("vector", lambda E: E.tensor_copy(EXPT[0:65, 0:512], pv[:, :]))
            P.dma(dbg[0:65, 6176:6688], EXPT[0:65, 0:512])
            P.dma(dbg2[0:1, 512:576], mkrow[:, :])
            P.dma(dbg2[0:1, 1024:1536], rowq[:, :])
        freed_tok[buf] = P.tok
        if h % 8 == 7:
            _emit_l1attn_wo(P, C, wo, OHB, brow, pmain, xres, b, h // 8)
            freed_tok = [P.tok, P.tok]
    store_x_fm(P, xres, xo)


L1ATTN_NAMES = ["L1_attn_Wq", "L1_attn_bq", "L1_attn_Wk", "L1_attn_bk",
                "L1_attn_Wv", "L1_attn_bv", "L1_attn_Wo", "L1_attn_bo"]


def build_l1_attn(nc):
    P = Prog(nc)
    xd = nc.dram_tensor("xin", [B, D, TPC], F32, kind="ExternalInput")
    xo = nc.dram_tensor("xout", [B, D, TPC], F32, kind="ExternalOutput")
    G = _g_weights(nc, L1ATTN_NAMES, W_SHAPES)
    C = Consts(P)
    dbg = dbg2 = None
    if DBG_L1A:
        dbg = nc.dram_tensor("dbg", [130, 8192], BF16, kind="ExternalOutput")
        dbg2 = nc.dram_tensor("dbg2", [65, 2048], F32, kind="ExternalOutput")
    with P.scope("A1_"):
        emit_l1_attn(P, C, G, xd, xo, dbg, dbg2)
    P.finalize()
    return P



def layer2d(h, l):
    return h[l:l + 1, :, :].rearrange("a i j -> (a i) j")


def emit_l1f_prep(P, C, G, xd, yo):
    xres = P.sb("xres", [128, NKT * T], F32)
    xhat = P.sb("xhat", [128, NKT * T], BF16)
    yres = P.sb("yres", [128, NKT * T], F32)
    pst = P.ps("pst", [1, 1024], F32)
    pbc = P.ps("pbc", [128, 1024], F32)
    L = LnScratch(P, pst, pbc)
    load_x_fm(P, xd, xres)
    emit_ln(P, C, L, xres, xhat, out32=yres)
    store_x_fm(P, yres, yo)


def build_l1f_prep(nc):
    P = Prog(nc)
    xd = nc.dram_tensor("xin", [B, D, TPC], F32, kind="ExternalInput")
    yo = nc.dram_tensor("yout", [B, D, TPC], F32, kind="ExternalOutput")
    C = Consts(P)
    with P.scope("P1_"):
        emit_l1f_prep(P, C, {}, xd, yo)
    P.finalize()
    return P


ROPE_NAMES = ["rope"]


def emit_head_process(P, C, A, psrc, dst_bf, cs_col, ss_col, pst, pbc):
    SQH, QF, QN, rown, RQ = A["SQH"], A["QF"], A["QN"], A["rown"], A["RQ"]
    P.op("scalar", lambda E: E.activation(SQH[:, :], psrc, AF.Square))
    P.op("vector", lambda E: E.tensor_copy(QF[:, :], psrc))
    P.op("tensor", lambda E: E.matmul(pst[0:1, 0:512], C.ones_col[0:64, :],
                                      SQH[:, :], start=True, stop=True))
    P.op("scalar", lambda E: E.activation(rown[:, :], pst[0:1, 0:512], AF.Sqrt))
    P.op("vector", lambda E: E.tensor_scalar_max(rown[:, :], rown[:, :], 1e-12))
    P.op("vector", lambda E: E.reciprocal(rown[:, :], rown[:, :]))
    P.op("vector", lambda E: E.tensor_copy(A["rnbf"][:, :], rown[:, :]))
    P.op("tensor", lambda E: E.matmul(pbc[0:64, 0:512], C.ones_row[0:1, 0:64],
                                      A["rnbf"][0:1, :], start=True, stop=True))
    P.op("vector", lambda E: E.tensor_mul(QN[:, :], QF[:, :], pbc[0:64, 0:512]))
    P.op("tensor", lambda E: E.matmul(A["prq"][0:64, 0:512], A["Rt64"][:, :], QN[:, :],
                                      start=True, stop=True))
    P.op("vector", lambda E: E.tensor_mul(RQ[:, :], QN[:, :], cs_col))
    P.op("vector", lambda E: E.tensor_mul(QN[:, :], A["prq"][0:64, 0:512], ss_col))
    P.op("vector", lambda E: E.tensor_add(dst_bf, RQ[:, :], QN[:, :]))


def emit_l1f_qkv(P, C, G, l, yd, qo, ko, vo):
    if True:
        nc = P.nc
        rope = G["rope"]
        wqkv_d = G["L1_ffn_wqkv"]
        qs_d = G["L1_ffn_q_scale"]
        ks_d = G["L1_ffn_k_scale"]

        yres = P.sb("yres", [128, NKT * T], F32)
        yhat = P.sb("yhat", [128, NKT * T], BF16)
        SHY = P.sb("SHY", [128, 4 * T], BF16)
        YB = P.sb("YB", [128, 16], F32)
        WQKV = P.sb("WQKV", [128, NKT * 1536], BF16)
        ROPE = P.sb("ROPE", [64, 4 * TPC], F32)
        CS = P.sb("CS", [64, 4 * TPC], BF16)
        QP = P.sb("QP", [64, 512], BF16)
        KP = P.sb("KP", [64, 512], BF16)
        VTM = P.sb("VTM", [128, 512], BF16)
        srow = P.sb("srow", [1, 128], BF16)
        scol = P.sb("scol", [64, 4], F32)
        scbf = P.sb("scbf", [64, 2], BF16)
        Rt64 = P.sb("Rt64", [64, 64], BF16)
        PT64 = P.sb("PT64", [64, 64], BF16)
        A = {
            "SQH": P.sb("SQH", [64, 512], BF16),
            "QF": P.sb("QF", [64, 512], F32),
            "QN": P.sb("QN", [64, 512], BF16),
            "RQ": P.sb("RQ", [64, 512], F32),
            "rown": P.sb("rown", [1, 512], F32),
            "rnbf": P.sb("rnbf", [1, 512], BF16),
            "Rt64": Rt64,
        }

        pmain = P.ps("pmain", [128, 1024], F32)
        pst = P.ps("pst", [1, 1024], F32)
        pbc = P.ps("pbc", [128, 1024], F32)
        prq = P.ps("prq", [64, 512], F32)
        A["prq"] = prq

        rt_np = np.zeros((64, 64), dtype=NBF)
        for dd in range(32):
            rt_np[dd, dd + 32] = 1
        for dd in range(32, 64):
            rt_np[dd, dd - 32] = -1
        pt_np = np.zeros((64, 64), dtype=NBF)
        for dd in range(64):
            pt_np[(dd + 32) % 64, dd] = 1
        h_rt = P.inline(np.asarray(rt_np), "h_rt64")
        h_pt = P.inline(np.asarray(pt_np), "h_pt64")
        P.dma(Rt64[:, :], h_rt[:, :])
        P.dma(PT64[:, :], h_pt[:, :])

        load_x_fm(P, yd, yres)
        for kf in range(NKT):
            P.op("vector", lambda E, kf=kf: E.tensor_copy(
                yhat[:, kf * T:(kf + 1) * T], yres[:, kf * T:(kf + 1) * T]))
        emit_boundary_cols(P, G, yres, YB, 4, 8)
        for kf in range(4):
            for c in range(NCH):
                P.op("vector", lambda E, kf=kf, c=c: E.tensor_copy(
                    SHY[:, kf * T + c * 512 + 1:kf * T + (c + 1) * 512],
                    yhat[:, (4 + kf) * T + c * 512:(4 + kf) * T + (c + 1) * 512 - 1]))
                P.op("vector", lambda E, kf=kf, c=c: E.tensor_copy(
                    SHY[:, kf * T + c * 512:kf * T + c * 512 + 1],
                    YB[:, kf * B + c:kf * B + c + 1]))

        def hview(k, c0, w):
            if k < 4:
                return yhat[:, k * T + c0:k * T + c0 + w]
            return SHY[:, (k - 4) * T + c0:(k - 4) * T + c0 + w]

        wq = load_w_nat(P, layer2d(wqkv_d, l), D, 1536, WQKV, 0)

        for i in range(4):
            P.dma(ROPE[:, i * TPC:(i + 1) * TPC], rope[i, :, :])
        P.cast_dma(srow[0:1, 0:64], qs_d[l:l + 1, :])
        P.cast_dma(srow[0:1, 64:128], ks_d[l:l + 1, :])
        def mm_cols(E):
            E.matmul(pbc[0:64, 0:1], srow[0:1, 0:64], C.ones_row[0:1, 0:1], start=True, stop=True)
            return E.matmul(pbc[0:64, 1:2], srow[0:1, 64:128], C.ones_row[0:1, 0:1], start=True, stop=True)
        P.op("tensor", mm_cols)
        P.op("vector", lambda E: E.tensor_copy(scol[:, 0:1], pbc[0:64, 0:1]))
        P.op("vector", lambda E: E.tensor_copy(scol[:, 2:3], pbc[0:64, 1:2]))
        P.op("vector", lambda E: E.tensor_copy(scbf[:, 0:1], pbc[0:64, 0:1]))
        P.op("vector", lambda E: E.tensor_copy(scbf[:, 1:2], pbc[0:64, 1:2]))
        def mm_rot(E):
            E.matmul(prq[0:64, 0:1], PT64[:, :], scbf[:, 0:1], start=True, stop=True)
            return E.matmul(prq[0:64, 1:2], PT64[:, :], scbf[:, 1:2], start=True, stop=True)
        P.op("tensor", mm_rot)
        P.op("vector", lambda E: E.tensor_copy(scol[:, 1:2], prq[0:64, 0:1]))
        P.op("vector", lambda E: E.tensor_copy(scol[:, 3:4], prq[0:64, 1:2]))
        for i, sc in [(0, 0), (1, 1), (2, 2), (3, 3)]:
            P.op("vector", lambda E, i=i, sc=sc: E.tensor_scalar_mul(
                CS[:, i * TPC:(i + 1) * TPC], ROPE[:, i * TPC:(i + 1) * TPC],
                scol[:, sc:sc + 1]))

        for h in range(8):
            for b in range(B):
                c0 = b * 512
                for which, cofs, out_d, dst in (("q", 0, qo, QP), ("k", 512, ko, KP)):
                    def mmp(E, h=h, c0=c0, cofs=cofs):
                        last = None
                        for k in range(NKT):
                            last = E.matmul(pmain[0:64, 0:512],
                                            wq[k][:, cofs + h * 64:cofs + (h + 1) * 64],
                                            hview(k, c0, 512),
                                            start=(k == 0), stop=(k == NKT - 1))
                        return last
                    P.op("tensor", mmp)
                    ti = 0 if which == "q" else 2
                    emit_head_process(P, C, A, pmain[0:64, 0:512], dst[:, :],
                                      CS[:, ti * TPC:(ti + 1) * TPC],
                                      CS[:, (ti + 1) * TPC:(ti + 2) * TPC], pst, pbc)
                    P.dma(out_d[(h * B + b) * 64:(h * B + b + 1) * 64, :], dst[:, :])

        for tt in range(NTT):
            t0 = tt * 128
            def mmv(E, t0=t0):
                last = None
                for k in range(NKT):
                    last = E.matmul(pmain[:, 0:512], hview(k, t0, 128),
                                    wq[k][:, 1024:1536],
                                    start=(k == 0), stop=(k == NKT - 1))
                return last
            P.op("tensor", mmv)
            P.op("vector", lambda E: E.tensor_copy(VTM[:, :], pmain[:, 0:512]))
            P.dma(vo[t0:t0 + 128, :], VTM[:, :])


def emit_boundary_cols(P, G, yres, YB, kf0, kf1):
    nc = P.nc
    nk = kf1 - kf0
    if not G.get("fused"):
        ybnd = G["ybnd"]
        for bb in range(B):
            for kf in range(kf0, kf1):
                P.dma(YB[:, (kf - kf0) * B + bb:(kf - kf0) * B + bb + 1],
                      ybnd[bb:bb + 1, kf * 128:(kf + 1) * 128, :]
                      .rearrange("b p a -> p (b a)"))
        return
    bin_d = P.dram("bnd_in", [D, B], F32)
    bout_d = P.dram("bnd_out", [NCORE * D, B], F32)
    BROW = P.sb("bnd_row", [128, NKT * B], F32)
    for kf in range(NKT):
        for bb in range(B):
            P.op("vector", lambda E, kf=kf, bb=bb: E.tensor_copy(
                BROW[:, kf * B + bb:kf * B + bb + 1],
                yres[:, kf * T + bb * 512 + 511:kf * T + bb * 512 + 512]))
    P.dma(bin_d[:, :].rearrange("(kf p) b -> p kf b", p=128),
          BROW[:, :].rearrange("p (kf b) -> p kf b", b=B))
    P.cc("AllGather", ALU.bypass, [bin_d.ap().opt()], [bout_d.ap().opt()])
    prv = G["prv"]

    def rd(E):
        off = P.dyn(E, "prv") * (D * B) + kf0 * 128 * B
        return E.dma_start(
            out=YB[:, 0:nk * B].rearrange("p (kf b) -> p kf b", b=B),
            in_=bass.AP(bout_d, off, [[B, 128], [128 * B, nk], [1, B]]))
    P.op("gpsimd", rd, inc=16)
    P.op("vector", lambda E: E.tensor_scalar_mul(
        YB[:, 0:nk * B], YB[:, 0:nk * B], G["bmcol"]))


def build_l1f_qkv(l):
    def b(nc):
        P = Prog(nc)
        yd = nc.dram_tensor("yin", [B, D, TPC], F32, kind="ExternalInput")
        G = _g_weights(nc, ["L1_ffn_wqkv", "L1_ffn_q_scale", "L1_ffn_k_scale"], W_SHAPES)
        G["ybnd"] = nc.dram_tensor("ybnd", [B, D, 1], F32, kind="ExternalInput")
        G["rope"] = nc.dram_tensor("rope", [4, 64, TPC], F32, kind="ExternalInput")
        qo = nc.dram_tensor("qout", [8 * B * 64, TPC], BF16, kind="ExternalOutput")
        ko = nc.dram_tensor("kout", [8 * B * 64, TPC], BF16, kind="ExternalOutput")
        vo = nc.dram_tensor("vout", [T, 512], BF16, kind="ExternalOutput")
        C = Consts(P)
        with P.scope(f"Q1{l}_"):
            emit_l1f_qkv(P, C, G, l, yd, qo, ko, vo)
        P.finalize()
        return P
    return b


def emit_l1f_attn(P, C, G, l, yd, yo, q_d, kw_d, vw_d, kp_d=None, vp_d=None,
                  kall_d=None, vall_d=None):
    if True:
        nc = P.nc
        pm_d = G["pmcol"]
        wo_d = G["L1_ffn_wo"]
        fused = G.get("fused")
        prv = G.get("prv")

        yres = P.sb("yres", [128, NKT * T], F32)
        WOA = P.sb("WOA", [64, 8 * D], BF16)
        QP = P.sb("QP", [64, 512], BF16)
        KW = P.sb("KW", [64, 1024], BF16)
        VW = P.sb("VW", [128, 512], BF16)
        EX = P.sb("EX", [128, 640], BF16)
        PTS = P.sb("PTS", [128, 640], BF16)
        OHB = P.sb("OHB", [64, 8 * 512], BF16)
        MSKA = P.sb("MSKA", [128, 128], F32)
        MSKB = P.sb("MSKB", [128, 128], F32)
        pmc = P.sb("pmc", [128, 1], F32)
        colm = P.sb("colm", [128, 1], F32)
        cols = P.sb("cols", [128, 1], F32)
        colr = P.sb("colr", [128, 1], F32)

        pmain = P.ps("pmain", [128, 1024], F32)
        po = P.ps("po", [64, 512], F32)
        psT = P.ps("psT", [128, 128], BF16)

        ma = np.zeros((128, 128), dtype=np.float32)
        mb = np.zeros((128, 128), dtype=np.float32)
        for rr in range(128):
            for cc in range(128):
                if cc < rr:
                    ma[rr, cc] = -1e30
                if cc > rr:
                    mb[rr, cc] = -1e30
        h_ma = P.inline(ma, "h_mska")
        h_mb = P.inline(mb, "h_mskb")
        P.dma(MSKA[:, :], h_ma[:, :])
        P.dma(MSKB[:, :], h_mb[:, :])
        P.dma(pmc[:, :], pm_d[:, :])

        load_x_fm(P, yd, yres)
        wo = load_w_nat(P, layer2d(wo_d, l), 512, D, WOA, 0, kgran=64)

        kstage = P.dram("kstage", [8 * B * 64, TPC], BF16)

        def dyn_kstage(E):
            sv = P.dyn(E, "prv") * (8 * B * 64 * TPC)
            return E.dma_start(out=kstage[:, :],
                               in_=bass.AP(kall_d, sv, [[TPC, 8 * B * 64], [1, TPC]]))

        vstage = P.dram("vstage", [T, 512], BF16)

        def dyn_vstage(E):
            sv = P.dyn(E, "prv") * (T * 512)
            return E.dma_start(out=vstage[:, :],
                               in_=bass.AP(vall_d, sv, [[512, T], [1, 512]]))

        if fused:
            P.op("gpsimd", dyn_kstage, inc=16)
            P.op("gpsimd", dyn_vstage, inc=16)
        for b in range(B):
            for h in range(8):
                off = (h * B + b) * 64
                P.dma(QP[:, :], q_d[off:off + 64, :])
                if fused:
                    P.dma(KW[:, 0:512], kstage[off:off + 64, :])
                    P.dma(VW[:, 0:256].rearrange("p (n f) -> p n f", f=64),
                          vstage[b * 512:(b + 1) * 512, h * 64:(h + 1) * 64]
                          .rearrange("(n p) f -> p n f", p=128))
                else:
                    P.dma(KW[:, 0:512], kp_d[off:off + 64, :])
                    P.dma(VW[:, 0:256].rearrange("p (n f) -> p n f", f=64),
                          vp_d[b * 512:(b + 1) * 512, h * 64:(h + 1) * 64]
                          .rearrange("(n p) f -> p n f", p=128))
                P.dma(KW[:, 512:1024], kw_d[off:off + 64, :])
                P.dma(VW[:, 256:512].rearrange("p (n f) -> p n f", f=64),
                      vw_d[b * 512:(b + 1) * 512, h * 64:(h + 1) * 64]
                      .rearrange("(n p) f -> p n f", p=128))
                for qt in range(4):
                    i0 = qt * 128
                    def mms(E, qt=qt, i0=i0):
                        E.matmul(pmain[:, 0:512], QP[:, i0:i0 + 128], KW[:, i0:i0 + 512],
                                 start=True, stop=True)
                        return E.matmul(pmain[:, 512:640], QP[:, i0:i0 + 128],
                                        KW[:, i0 + 512:i0 + 640], start=True, stop=True)
                    P.op("tensor", mms)
                    P.op("vector", lambda E: E.tensor_add(pmain[:, 0:128], pmain[:, 0:128], MSKA[:, :]))
                    P.op("vector", lambda E: E.tensor_add(pmain[:, 512:640], pmain[:, 512:640], MSKB[:, :]))
                    wprev = 512 - i0
                    if wprev > 0:
                        P.op("vector", lambda E, wprev=wprev: E.tensor_scalar_add(
                            pmain[:, 0:wprev], pmain[:, 0:wprev], pmc[:, :]))
                    P.op("vector", lambda E: E.reduce_max(colm[:, :], pmain[:, 0:640], axis=AX.X, negate=True))
                    P.op("scalar", lambda E: E.activation(EX[:, :], pmain[:, 0:640], AF.Exp,
                                                          bias=colm[:, :], accum_out=cols[:, :]))
                    P.op("vector", lambda E: E.reciprocal(colr[:, :], cols[:, :]))
                    P.op("vector", lambda E: E.tensor_scalar_mul(EX[:, :], EX[:, :], colr[:, :]))
                    for j in range(5):
                        P.op("tensor", lambda E, j=j: E.transpose(
                            psT[:, :], EX[:, j * 128:(j + 1) * 128], C.ident[:, :]))
                        P.op("vector", lambda E, j=j: E.tensor_copy(
                            PTS[:, j * 128:(j + 1) * 128], psT[:, :]))
                    def mmpv(E, qt=qt, i0=i0):
                        last = None
                        for j in range(5):
                            kt = qt + j
                            last = E.matmul(po[:, i0:i0 + 128],
                                            VW[:, kt * 64:(kt + 1) * 64],
                                            PTS[:, j * 128:(j + 1) * 128],
                                            start=(j == 0), stop=(j == 4))
                        return last
                    P.op("tensor", mmpv)
                P.op("vector", lambda E, h=h: E.tensor_copy(OHB[:, h * 512:(h + 1) * 512], po[:, :]))
            for m in range(NKT):
                def mmo(E, m=m):
                    last = None
                    for k in range(8):
                        last = E.matmul(pmain[:, 0:512], wo[k][:, m * 128:(m + 1) * 128],
                                        OHB[:, k * 512:(k + 1) * 512],
                                        start=(k == 0), stop=(k == 7))
                    return last
                P.op("tensor", mmo)
                xv = yres[:, m * T + b * 512:m * T + (b + 1) * 512]
                P.op("vector", lambda E, xv=xv: E.tensor_add(xv, xv, pmain[:, 0:512]))

        store_x_fm(P, yres, yo)


def build_l1f_attn(l):
    def b(nc):
        P = Prog(nc)
        yd = nc.dram_tensor("yin", [B, D, TPC], F32, kind="ExternalInput")
        yo = nc.dram_tensor("yout", [B, D, TPC], F32, kind="ExternalOutput")
        q_d = nc.dram_tensor("qin", [8 * B * 64, TPC], BF16, kind="ExternalInput")
        kp_d = nc.dram_tensor("kprev", [8 * B * 64, TPC], BF16, kind="ExternalInput")
        kw_d = nc.dram_tensor("kown", [8 * B * 64, TPC], BF16, kind="ExternalInput")
        vp_d = nc.dram_tensor("vprev", [T, 512], BF16, kind="ExternalInput")
        vw_d = nc.dram_tensor("vown", [T, 512], BF16, kind="ExternalInput")
        G = _g_weights(nc, ["L1_ffn_wo"], W_SHAPES)
        G["pmcol"] = nc.dram_tensor("pmcol", [128, 1], F32, kind="ExternalInput")
        C = Consts(P)
        with P.scope(f"B1{l}_"):
            emit_l1f_attn(P, C, G, l, yd, yo, q_d, kw_d, vw_d, kp_d=kp_d, vp_d=vp_d)
        P.finalize()
        return P
    return b


L1F_HID = 2730


def emit_l1f_ffn(P, C, G, l, yd, yo, final=False, x3d=None):
    HID = L1F_HID
    if True:
        nc = P.nc
        lng_d = G["L1_ffn_ln_g"]
        lnb_d = G["L1_ffn_ln_b"]
        w1_d = G["L1_ffn_w1"]
        b1_d = G["L1_ffn_b1"]
        w2_d = G["L1_ffn_w2"]
        b2_d = G["L1_ffn_b2"]

        NH = (HID + 127) // 128
        W1A = P.sb("W1A", [128, NKT * 2 * HID], BF16)
        W2A = P.sb("W2A", [128, NH * D], BF16)
        YC = P.sb("YC", [128, NKT * 512], F32)
        HC = P.sb("HC", [128, NKT * 512], BF16)
        H4 = P.sb("H4", [128, NH * 512], BF16)
        GT = P.sb("GT", [128, 512], BF16)
        YB = P.sb("YB", [128, 32], F32)
        b1r = P.sb("b1r", [1, 2 * HID], BF16)
        b2r = P.sb("b2r", [1, D], BF16)
        grow = P.sb("grow", [1, 2 * D], BF16)
        gcol = P.sb("gcol", [128, 16], F32)

        pmain = P.ps("pmain", [128, 1024], F32)
        pm2 = P.ps("pm2", [128, 1024], F32)
        pst = P.ps("pst", [1, 1024], F32)
        pbc = P.ps("pbc", [128, 1024], F32)
        L = LnScratch(P, pst, pbc)

        w1 = load_w_nat(P, layer2d(w1_d, l), D, 2 * HID, W1A, 0)
        w2 = load_w_nat(P, layer2d(w2_d, l), HID, D, W2A, 0)
        P.cast_dma(b1r[:, :], b1_d[l:l + 1, :])
        P.cast_dma(b2r[:, :], b2_d[l:l + 1, :])
        P.cast_dma(grow[0:1, 0:D], lng_d[l:l + 1, :])
        P.cast_dma(grow[0:1, D:2 * D], lnb_d[l:l + 1, :])
        def mm_gc(E):
            last = None
            for kf in range(NKT):
                last = E.matmul(pbc[:, kf:kf + 1], grow[0:1, kf * 128:(kf + 1) * 128],
                                C.ones_row[0:1, 0:1], start=True, stop=True)
            for kf in range(NKT):
                last = E.matmul(pbc[:, 8 + kf:9 + kf], grow[0:1, D + kf * 128:D + (kf + 1) * 128],
                                C.ones_row[0:1, 0:1], start=True, stop=True)
            return last
        P.op("tensor", mm_gc)
        P.op("vector", lambda E: E.tensor_copy(gcol[:, 0:16], pbc[:, 0:16]))
        gamma_col = [gcol[:, kf:kf + 1] for kf in range(NKT)]
        beta_col = [gcol[:, 8 + kf:9 + kf] for kf in range(NKT)]

        if G.get("fused"):
            emit_boundary_cols_hbm(P, G, yd, YB, 0, NKT)
        else:
            ybnd = G["ybnd"]
            for bb in range(B):
                for kf in range(NKT):
                    P.dma(YB[:, kf * B + bb:kf * B + bb + 1],
                          ybnd[bb:bb + 1, kf * 128:(kf + 1) * 128, :]
                          .rearrange("b p a -> p (b a)"))

        for c in range(NCH):
            c0 = c * 512
            for kf in range(NKT):
                src = yd[c:c + 1, kf * 128:(kf + 1) * 128, :].rearrange("b p t -> p (b t)")
                P.dma(YC[:, kf * 512:(kf + 1) * 512], src)
            for kf in range(NKT):
                hv = HC[:, kf * 512:(kf + 1) * 512]
                if kf < 4:
                    P.op("vector", lambda E, kf=kf, hv=hv: E.tensor_copy(
                        hv, YC[:, kf * 512:(kf + 1) * 512]))
                else:
                    P.op("vector", lambda E, kf=kf, hv=hv: E.tensor_copy(
                        hv[:, 1:512], YC[:, kf * 512:(kf + 1) * 512 - 1]))
                    P.op("vector", lambda E, kf=kf, hv=hv, c=c: E.tensor_copy(
                        hv[:, 0:1], YB[:, kf * B + c:kf * B + c + 1]))
            emit_ln_chunk(P, C, L, HC, gamma_col, beta_col)
            ppa = PP(P, [(pmain[:, 0:512], pmain[:, 512:1024]),
                         (pm2[:, 0:512], pm2[:, 512:1024])])
            for m in range(NH):
                rm = min(128, HID - m * 128)
                def mm_ag(E, pv, m=m, rm=rm):
                    last = None
                    for k in range(NKT):
                        last = E.matmul(pv[0][0:rm, :], w1[k][:, m * 128:m * 128 + rm],
                                        HC[:, k * 512:(k + 1) * 512],
                                        start=(k == 0), stop=False)
                    E.matmul(pv[0][0:rm, :], b1r[0:1, m * 128:m * 128 + rm],
                             C.ones_row[0:1, 0:512], start=False, stop=True)
                    for k in range(NKT):
                        last = E.matmul(pv[1][0:rm, :],
                                        w1[k][:, HID + m * 128:HID + m * 128 + rm],
                                        HC[:, k * 512:(k + 1) * 512],
                                        start=(k == 0), stop=False)
                    return E.matmul(pv[1][0:rm, :], b1r[0:1, HID + m * 128:HID + m * 128 + rm],
                                    C.ones_row[0:1, 0:512], start=False, stop=True)
                def ev_ag(pv, m=m, rm=rm):
                    P.op("scalar", lambda E, pv=pv, rm=rm: E.activation(
                        GT[0:rm, :], pv[1][0:rm, :], AF.Gelu))
                    P.op("vector", lambda E, pv=pv, m=m, rm=rm: E.tensor_mul(
                        H4[0:rm, m * 512:m * 512 + 512], pv[0][0:rm, :], GT[0:rm, :]))
                ppa.step(mm_ag, ev_ag)
            ppo = PP(P, [pmain[:, 0:512], pmain[:, 512:1024]])
            for m in range(NKT):
                def mm_o(E, pv, m=m):
                    last = None
                    for k in range(NH):
                        rk = min(128, HID - k * 128)
                        last = E.matmul(pv, w2[k][0:rk, m * 128:(m + 1) * 128],
                                        H4[0:rk, k * 512:(k + 1) * 512],
                                        start=(k == 0), stop=False)
                    return E.matmul(pv, b2r[0:1, m * 128:(m + 1) * 128],
                                    C.ones_row[0:1, 0:512], start=False, stop=True)
                def ev_o(pv, m=m):
                    P.op("vector", lambda E, pv=pv, m=m: E.tensor_add(
                        YC[:, m * 512:(m + 1) * 512], YC[:, m * 512:(m + 1) * 512], pv))
                ppo.step(mm_o, ev_o)
            if final:
                for kf in range(NKT):
                    s3 = x3d[c:c + 1, kf * 128:(kf + 1) * 128, :].rearrange("b p t -> p (b t)")
                    P.dma(L.u32[:, :], s3)
                    P.op("vector", lambda E, kf=kf: E.tensor_add(
                        YC[:, kf * 512:(kf + 1) * 512], YC[:, kf * 512:(kf + 1) * 512],
                        L.u32[:, :]))
            for kf in range(NKT):
                dst = yo[c:c + 1, kf * 128:(kf + 1) * 128, :].rearrange("b p t -> p (b t)")
                P.dma(dst, YC[:, kf * 512:(kf + 1) * 512])


def emit_boundary_cols_hbm(P, G, ybuf, YB, kf0, kf1):
    nc = P.nc
    nk = kf1 - kf0
    bin_d = P.dram("bndh_in", [D, B], F32)
    bout_d = P.dram("bndh_out", [NCORE * D, B], F32)
    BR2 = P.sb("bndh_row", [128, NKT * B], F32)
    for bb in range(B):
        for kf in range(NKT):
            P.dma(BR2[:, kf * B + bb:kf * B + bb + 1],
                  ybuf[bb:bb + 1, kf * 128:(kf + 1) * 128, TPC - 1:TPC]
                  .rearrange("b p a -> p (b a)"), allow_slow_non_contiguous=True)
    P.dma(bin_d[:, :].rearrange("(kf p) b -> p kf b", p=128),
          BR2[:, :].rearrange("p (kf b) -> p kf b", b=B))
    P.cc("AllGather", ALU.bypass, [bin_d.ap().opt()], [bout_d.ap().opt()])
    prv = G["prv"]

    def rd(E):
        off = P.dyn(E, "prv") * (D * B) + kf0 * 128 * B
        return E.dma_start(
            out=YB[:, 0:nk * B].rearrange("p (kf b) -> p kf b", b=B),
            in_=bass.AP(bout_d, off, [[B, 128], [128 * B, nk], [1, B]]))
    P.op("gpsimd", rd, inc=16)
    P.op("vector", lambda E: E.tensor_scalar_mul(
        YB[:, 0:nk * B], YB[:, 0:nk * B], G["bmcol"]))


def build_l1f_ffn(l, final=False):
    def b(nc):
        P = Prog(nc)
        yd = nc.dram_tensor("yin", [B, D, TPC], F32, kind="ExternalInput")
        yo = nc.dram_tensor("yout", [B, D, TPC], F32, kind="ExternalOutput")
        G = _g_weights(nc, ["L1_ffn_ln_g", "L1_ffn_ln_b", "L1_ffn_w1", "L1_ffn_b1",
                            "L1_ffn_w2", "L1_ffn_b2"], W_SHAPES)
        G["ybnd"] = nc.dram_tensor("ybnd", [B, D, 1], F32, kind="ExternalInput")
        x3d = None
        if final:
            x3d = nc.dram_tensor("x3in", [B, D, TPC], F32, kind="ExternalInput")
        C = Consts(P)
        with P.scope(f"C1{l}_"):
            emit_l1f_ffn(P, C, G, l, yd, yo, final=final, x3d=x3d)
        P.finalize()
        return P
    return b


def emit_ln_chunk(P, C, L, HC, gamma_col, beta_col):
    def mm_sum(E):
        last = None
        for kf in range(NKT):
            last = E.matmul(L.pst[0:1, 0:512], C.ones_col[:, :],
                            HC[:, kf * 512:(kf + 1) * 512],
                            start=(kf == 0), stop=(kf == NKT - 1))
        return last
    P.op("tensor", mm_sum)
    for kf in range(NKT):
        P.op("scalar", lambda E, kf=kf: E.activation(
            L.sq[:, :], HC[:, kf * 512:(kf + 1) * 512], AF.Square))
        P.op("tensor", lambda E, kf=kf: E.matmul(
            L.pst[0:1, 512:1024], C.ones_col[:, :], L.sq[:, :],
            start=(kf == 0), stop=(kf == NKT - 1)))
    P.op("vector", lambda E: E.tensor_scalar_mul(L.rowm[:, :], L.pst[0:1, 0:512], 1.0 / D))
    P.op("vector", lambda E: E.tensor_scalar_mul(L.rowe[:, :], L.pst[0:1, 512:1024], 1.0 / D))
    P.op("vector", lambda E: E.tensor_mul(L.rowa[:, :], L.rowm[:, :], L.rowm[:, :]))
    P.op("vector", lambda E: E.tensor_sub(L.rowe[:, :], L.rowe[:, :], L.rowa[:, :]))
    P.op("vector", lambda E: E.tensor_scalar_add(L.rowe[:, :], L.rowe[:, :], LN_EPS))
    P.op("scalar", lambda E: E.activation(L.rowe[:, :], L.rowe[:, :], AF.Sqrt))
    P.op("vector", lambda E: E.reciprocal(L.rowa[:, :], L.rowe[:, :]))
    P.op("vector", lambda E: E.tensor_mul(L.rowm[:, :], L.rowm[:, :], L.rowa[:, :]))
    P.op("vector", lambda E: E.tensor_scalar_mul(L.rowm[:, :], L.rowm[:, :], -1.0))
    P.op("vector", lambda E: E.tensor_copy(L.rowbf[0:1, 0:512], L.rowa[:, :]))
    P.op("vector", lambda E: E.tensor_copy(L.rowbf[0:1, 512:1024], L.rowm[:, :]))
    def mm_bc(E):
        E.matmul(L.pbc[:, 0:512], C.ones_row[0:1, 0:128], L.rowbf[0:1, 0:512],
                 start=True, stop=True)
        return E.matmul(L.pbc[:, 512:1024], C.ones_row[0:1, 0:128], L.rowbf[0:1, 512:1024],
                        start=True, stop=True)
    P.op("tensor", mm_bc)
    for kf in range(NKT):
        hv = HC[:, kf * 512:(kf + 1) * 512]
        P.op("vector", lambda E, hv=hv: E.tensor_mul(L.u32[:, :], hv, L.pbc[:, 0:512]))
        P.op("vector", lambda E: E.tensor_add(L.u32[:, :], L.u32[:, :], L.pbc[:, 512:1024]))
        P.op("vector", lambda E, hv=hv, kf=kf: E.tensor_scalar(
            hv, L.u32[:, :], gamma_col[kf], beta_col[kf], ALU.mult, ALU.add))






def emit_l2a_proj(P, C, G, xd, qo, ko, vo, bo):
    nc = P.nc
    wq_d = G["L2_attn_Wq"]
    wk_d = G["L2_attn_Wk"]
    wv_d = G["L2_attn_Wv"]
    wb_d = G["L2_attn_Wb"]

    xres = P.sb("xres", [128, NKT * T], F32)
    xhat = P.sb("xhat", [128, NKT * T], BF16)
    WA = P.sb("WA", [128, NKT * D], BF16)
    WB = P.sb("WB", [128, NKT * 4], BF16)
    OT = P.sb("OT", [128, 512], BF16)
    BR = P.sb("BR", [1, 4 * T], F32)
    pmain = P.ps("pmain", [128, 1024], F32)
    pst = P.ps("pst", [1, 1024], F32)
    pbc = P.ps("pbc", [128, 1024], F32)
    L = LnScratch(P, pst, pbc)

    load_x_fm(P, xd, xres)
    emit_ln(P, C, L, xres, xhat)

    fused = G.get("fused")
    for wd, od in ((wq_d, qo), (wk_d, ko), (wv_d, vo)):
        w = load_w_nat(P, wd.ap(), D, D, WA, 0)
        for m in range(NKT):
            for c in range(NCH):
                def mmp(E, m=m, c=c, w=w):
                    last = None
                    for k in range(NKT):
                        last = E.matmul(pmain[:, 0:512], w[k][:, m * 128:(m + 1) * 128],
                                        xhat[:, k * T + c * 512:k * T + (c + 1) * 512],
                                        start=(k == 0), stop=(k == NKT - 1))
                    return last
                P.op("tensor", mmp)
                P.op("vector", lambda E: E.tensor_copy(OT[:, :], pmain[:, 0:512]))
                if fused:
                    j = 2 * c + m // 4
                    row0 = j * 512 + (m % 4) * 128
                    P.dma(od[row0:row0 + 128, 0:512], OT[:, :])
                else:
                    P.dma(od[m * 128:(m + 1) * 128, c * 512:(c + 1) * 512], OT[:, :])

    wb = load_w_nat(P, wb_d.ap(), D, 4, WB, 0)
    for h in range(4):
        for c in range(NCH):
            def mmb(E, h=h, c=c):
                last = None
                for k in range(NKT):
                    last = E.matmul(pst[0:1, 0:512], wb[k][:, h:h + 1],
                                    xhat[:, k * T + c * 512:k * T + (c + 1) * 512],
                                    start=(k == 0), stop=(k == NKT - 1))
                return last
            P.op("tensor", mmb)
            P.op("scalar", lambda E, h=h, c=c: E.activation(
                BR[0:1, h * T + c * 512:h * T + (c + 1) * 512], pst[0:1, 0:512], AF.Sigmoid))
    if fused:
        for h in range(4):
            for c in range(NCH):
                j = 2 * c + h // 2
                P.dma(bo[j * 2 + h % 2:j * 2 + h % 2 + 1, :],
                      BR[0:1, h * T + c * 512:h * T + (c + 1) * 512])
    else:
        for h in range(4):
            P.dma(bo[h:h + 1, :], BR[0:1, h * T:(h + 1) * T])


L2A_PROJ_NAMES = ["L2_attn_Wq", "L2_attn_Wk", "L2_attn_Wv", "L2_attn_Wb"]


def build_l2a_proj(nc):
    P = Prog(nc)
    xd = nc.dram_tensor("xin", [B, D, TPC], F32, kind="ExternalInput")
    qo = nc.dram_tensor("qpre", [D, T], BF16, kind="ExternalOutput")
    ko = nc.dram_tensor("kpre", [D, T], BF16, kind="ExternalOutput")
    vo = nc.dram_tensor("vpre", [D, T], BF16, kind="ExternalOutput")
    bo = nc.dram_tensor("beta", [4, T], F32, kind="ExternalOutput")
    G = _g_weights(nc, L2A_PROJ_NAMES, W_SHAPES)
    C = Consts(P)
    with P.scope("A2p_"):
        emit_l2a_proj(P, C, G, xd, qo, ko, vo, bo)
    P.finalize()
    return P


def emit_l2a_scan(P, C, G, qp_d, kp_d, vp_d, bt_d, oo):
    nc = P.nc
    DH = 256
    fused = G.get("fused")
    gr_d = G["L2_attn_g_rms"]
    cq_d = G["convq"]
    ck_d = G["convk"]
    cv_d = G["convv"]

    QC = P.sb("QC", [128, 2 * S], BF16)
    KC = P.sb("KC", [128, 2 * S], BF16)
    VC = P.sb("VC", [128, 2 * S], BF16)
    OBF = P.sb("OBF", [128, 2 * S], BF16)
    BT = P.sb("BT", [1, S], F32)
    BTBF = P.sb("BTBF", [1, S], BF16)
    ACC = P.sb("ACC", [128, 512], F32)
    SQB = P.sb("SQB", [128, 512], BF16)
    WCOL = P.sb("WCOL", [128, 24], F32)
    crow = P.sb("crow", [1, 12 * DH], BF16)
    grow = P.sb("grow", [1, DH], BF16)
    gcol = P.sb("gcol", [128, 2], F32)
    STATE = P.sb("STATE", [128, 2 * DH], F32)
    SB = P.sb("SB", [128, 2 * DH], BF16)
    SG = P.sb("SG", [128, 128], F32)
    LT = P.sb("LT", [128, 128], BF16)
    ATU = P.sb("ATU", [128, 128], BF16)
    YY = P.sb("YY", [128, 128], BF16)
    YT = P.sb("YT", [128, 128], BF16)
    ZB = P.sb("ZB", [128, 128], BF16)
    VTc = P.sb("VTc", [128, DH], BF16)
    KTc = P.sb("KTc", [128, DH], BF16)
    RS = P.sb("RS", [128, DH], F32)
    RB = P.sb("RB", [128, DH], BF16)
    UB = P.sb("UB", [128, DH], BF16)
    WTB = P.sb("WTB", [128, 128], BF16)
    OC = P.sb("OC", [128, DH], F32)
    bcol = P.sb("bcol", [128, 1], F32)
    rown = P.sb("rown", [1, 512], F32)
    rnbf = P.sb("rnbf", [1, 512], BF16)
    MSKU = P.sb("MSKU", [128, 128], F32)
    MSKL = P.sb("MSKL", [128, 128], F32)
    MSKUI = P.sb("MSKUI", [128, 128], F32)
    IDF = P.sb("IDF", [128, 128], F32)

    pA = P.ps("pA", [128, 512], F32)
    pB = P.ps("pB", [128, 512], F32)
    pO = P.ps("pO", [128, 128], F32)
    pS = P.ps("pS", [128, 256], F32)
    pT = P.ps("pT", [128, 256], BF16)
    pst = P.ps("pst", [1, 512], F32)

    iu = np.triu(np.ones((128, 128), dtype=np.float32), 1)
    il = np.tril(np.ones((128, 128), dtype=np.float32), -1)
    iui = np.triu(np.ones((128, 128), dtype=np.float32), 0)
    P.dma(MSKU[:, :], P.inline(iu, "h_msku")[:, :])
    P.dma(MSKL[:, :], P.inline(il, "h_mskl")[:, :])
    P.dma(MSKUI[:, :], P.inline(iui, "h_mskui")[:, :])
    P.dma(IDF[:, :], P.inline(np.eye(128, dtype=np.float32), "h_idf")[:, :])

    P.cast_dma(grow[0:1, :], gr_d[:].rearrange("(a n) -> a n", a=1))
    def mmg(E):
        E.matmul(pO[:, 0:1], grow[0:1, 0:128], C.ones_row[0:1, 0:1], start=True, stop=True)
        return E.matmul(pO[:, 1:2], grow[0:1, 128:256], C.ones_row[0:1, 0:1], start=True, stop=True)
    P.op("tensor", mmg)
    P.op("vector", lambda E: E.tensor_copy(gcol[:, 0:2], pO[:, 0:2]))

    for pair in range(2):
        if fused:
            for dt in range(2):
                for r in range(NCORE):
                    row0 = r * 512 + pair * 256 + dt * 128
                    P.dma(QC[:, dt * S + r * 512:dt * S + (r + 1) * 512],
                          qp_d[row0:row0 + 128, :])
                    P.dma(KC[:, dt * S + r * 512:dt * S + (r + 1) * 512],
                          kp_d[row0:row0 + 128, :])
                    P.dma(VC[:, dt * S + r * 512:dt * S + (r + 1) * 512],
                          vp_d[row0:row0 + 128, :])
            for r in range(NCORE):
                P.dma(BT[0:1, r * 512:(r + 1) * 512],
                      bt_d[r * 2 + pair:r * 2 + pair + 1, :])
        else:
            for dt in range(2):
                P.dma(QC[:, dt * S:(dt + 1) * S], qp_d[pair, dt * 128:(dt + 1) * 128, :])
                P.dma(KC[:, dt * S:(dt + 1) * S], kp_d[pair, dt * 128:(dt + 1) * 128, :])
                P.dma(VC[:, dt * S:(dt + 1) * S], vp_d[pair, dt * 128:(dt + 1) * 128, :])
            P.dma(BT[:, :], bt_d[pair, :, :])
        P.op("vector", lambda E: E.tensor_copy(BTBF[:, :], BT[:, :]))
        for i, cd in enumerate((cq_d, ck_d, cv_d)):
            for j in range(4):
                P.cast_dma(crow[0:1, (i * 4 + j) * DH:(i * 4 + j + 1) * DH], cd[pair, j:j + 1, :])
        def mmc(E):
            last = None
            for i in range(3):
                for j in range(4):
                    for dt in range(2):
                        cc = (i * 4 + j) * 2 + dt
                        last = E.matmul(pS[:, cc:cc + 1],
                                        crow[0:1, (i * 4 + j) * DH + dt * 128:(i * 4 + j) * DH + (dt + 1) * 128],
                                        C.ones_row[0:1, 0:1], start=True, stop=True)
            return last
        P.op("tensor", mmc)
        P.op("vector", lambda E: E.tensor_copy(WCOL[:, 0:24], pS[:, 0:24]))

        for i, AR in enumerate((QC, KC, VC)):
            for dt in range(2):
                for ch in reversed(range(8)):
                    c0 = ch * 512
                    base = AR[:, dt * S:(dt + 1) * S]
                    def cv0(E, base=base, c0=c0, i=i, dt=dt):
                        return E.tensor_scalar_mul(ACC[:, :], base[:, c0:c0 + 512],
                                                   WCOL[:, (i * 4 + 3) * 2 + dt:(i * 4 + 3) * 2 + dt + 1])
                    P.op("vector", cv0)
                    for j in range(3):
                        jj = 2 - j
                        sh = 3 - jj
                        def cvj(E, base=base, c0=c0, i=i, dt=dt, jj=jj, sh=sh):
                            lo = max(c0 - sh, 0)
                            w = c0 + 512 - sh - lo
                            return E.scalar_tensor_tensor(
                                ACC[:, 512 - w:512], base[:, lo:lo + w],
                                WCOL[:, (i * 4 + jj) * 2 + dt:(i * 4 + jj) * 2 + dt + 1],
                                ACC[:, 512 - w:512], ALU.mult, ALU.add)
                        P.op("vector", cvj)
                    P.op("scalar", lambda E, base=base, c0=c0: E.activation(
                        base[:, c0:c0 + 512], ACC[:, :], AF.Silu))
        for AR in (QC, KC):
            for ch in range(8):
                c0 = ch * 512
                for dt in range(2):
                    P.op("scalar", lambda E, AR=AR, c0=c0, dt=dt: E.activation(
                        SQB[:, 0:512], AR[:, dt * S + c0:dt * S + c0 + 512], AF.Square))
                    P.op("tensor", lambda E, dt=dt: E.matmul(
                        pst[0:1, 0:512], C.ones_col[:, :], SQB[:, 0:512],
                        start=(dt == 0), stop=(dt == 1)))
                P.op("vector", lambda E: E.tensor_scalar_add(rown[:, :], pst[0:1, 0:512], 1e-6))
                P.op("scalar", lambda E: E.activation(rown[:, :], rown[:, :], AF.Sqrt))
                P.op("vector", lambda E: E.reciprocal(rown[:, :], rown[:, :]))
                P.op("vector", lambda E: E.tensor_copy(rnbf[:, :], rown[:, :]))
                P.op("tensor", lambda E: E.matmul(pB[:, 0:512], C.ones_row[0:1, 0:128],
                                                  rnbf[0:1, :], start=True, stop=True))
                for dt in range(2):
                    P.op("vector", lambda E, AR=AR, c0=c0, dt=dt: E.tensor_mul(
                        AR[:, dt * S + c0:dt * S + c0 + 512],
                        AR[:, dt * S + c0:dt * S + c0 + 512], pB[:, 0:512]))

        P.op("vector", lambda E: E.memset(STATE[:, :], 0.0))
        for ch in range(32):
            t0 = ch * 128
            P.op("vector", lambda E: E.tensor_copy(SB[:, :], STATE[:, :]))
            P.op("tensor", lambda E, t0=t0: E.matmul(
                pO[:, 0:1], BTBF[0:1, t0:t0 + 128], C.ones_row[0:1, 0:1],
                start=True, stop=True))
            P.op("vector", lambda E: E.tensor_copy(bcol[:, :], pO[:, 0:1]))
            def mmG(E, t0=t0):
                last = None
                for dt in range(2):
                    last = E.matmul(pA[:, 0:128], KC[:, dt * S + t0:dt * S + t0 + 128],
                                    KC[:, dt * S + t0:dt * S + t0 + 128],
                                    start=(dt == 0), stop=(dt == 1))
                return last
            P.op("tensor", mmG)
            P.op("tensor", lambda E, t0=t0: E.matmul(
                pB[:, 0:128], C.ones_row[0:1, 0:128], BTBF[0:1, t0:t0 + 128],
                start=True, stop=True))
            P.op("vector", lambda E: E.tensor_copy(SG[:, :], pA[:, 0:128]))
            P.op("vector", lambda E: E.tensor_mul(SG[:, :], SG[:, :], MSKU[:, :]))
            P.op("vector", lambda E: E.tensor_mul(ATU[:, :], SG[:, :], pB[:, 0:128]))
            P.op("vector", lambda E: E.tensor_add(LT[:, :], ATU[:, :], IDF[:, :]))
            P.op("vector", lambda E: E.tensor_copy(SG[:, :], pA[:, 0:128]))
            P.op("vector", lambda E: E.tensor_mul(SG[:, :], SG[:, :], MSKL[:, :]))
            P.op("vector", lambda E: E.tensor_scalar_mul(SG[:, :], SG[:, :], bcol[:, :]))
            P.op("vector", lambda E: E.tensor_sub(YY[:, :], IDF[:, :], SG[:, :]))
            P.op("vector", lambda E: E.tensor_sub(YT[:, :], IDF[:, :], ATU[:, :]))
            for it in range(6):
                P.op("tensor", lambda E: E.matmul(pA[:, 0:128], LT[:, :], YY[:, :],
                                                  start=True, stop=True))
                P.op("vector", lambda E: E.tensor_copy(ZB[:, :], pA[:, 0:128]))
                def mmp2(E):
                    E.matmul(pA[:, 128:256], YT[:, :], ZB[:, :], start=True, stop=True)
                    return E.matmul(pB[:, 0:128], ZB[:, :], YT[:, :], start=True, stop=True)
                P.op("tensor", mmp2)
                P.op("vector", lambda E: E.scalar_tensor_tensor(
                    YY[:, :], YY[:, :], 2.0, pA[:, 128:256], ALU.mult, ALU.subtract))
                P.op("vector", lambda E: E.scalar_tensor_tensor(
                    YT[:, :], YT[:, :], 2.0, pB[:, 0:128], ALU.mult, ALU.subtract))
            def mmvt(E, t0=t0):
                E.transpose(pT[:, 0:128], VC[:, 0 * S + t0:0 * S + t0 + 128], C.ident[:, :])
                return E.transpose(pT[:, 128:256], VC[:, 1 * S + t0:1 * S + t0 + 128], C.ident[:, :])
            P.op("tensor", mmvt)
            P.op("vector", lambda E: E.tensor_copy(VTc[:, :], pT[:, 0:256]))
            def mmks(E, t0=t0):
                last = None
                for dt in range(2):
                    last = E.matmul(pB[:, 0:256], KC[:, dt * S + t0:dt * S + t0 + 128],
                                    SB[:, dt * DH:(dt + 1) * DH],
                                    start=(dt == 0), stop=(dt == 1))
                return last
            P.op("tensor", mmks)
            P.op("vector", lambda E: E.tensor_sub(RS[:, :], VTc[:, :], pB[:, 0:256]))
            P.op("vector", lambda E: E.tensor_scalar_mul(RB[:, :], RS[:, :], bcol[:, :]))
            P.op("tensor", lambda E: E.matmul(pB[:, 0:256], YT[:, :], RB[:, :],
                                              start=True, stop=True))
            P.op("vector", lambda E: E.tensor_copy(UB[:, :], pB[:, 0:256]))
            def mmwt(E, t0=t0):
                last = None
                for dt in range(2):
                    last = E.matmul(pA[:, 0:128], KC[:, dt * S + t0:dt * S + t0 + 128],
                                    QC[:, dt * S + t0:dt * S + t0 + 128],
                                    start=(dt == 0), stop=(dt == 1))
                return last
            P.op("tensor", mmwt)
            P.op("vector", lambda E: E.tensor_mul(WTB[:, :], pA[:, 0:128], MSKUI[:, :]))
            for dm in range(2):
                def mmo(E, dm=dm, t0=t0):
                    E.matmul(pO[:, 0:128], SB[:, 0 * DH + dm * 128:0 * DH + dm * 128 + 128],
                             QC[:, 0 * S + t0:0 * S + t0 + 128], start=True, stop=False)
                    E.matmul(pO[:, 0:128], SB[:, 1 * DH + dm * 128:1 * DH + dm * 128 + 128],
                             QC[:, 1 * S + t0:1 * S + t0 + 128], start=False, stop=False)
                    return E.matmul(pO[:, 0:128], UB[:, dm * 128:(dm + 1) * 128],
                                    WTB[:, :], start=False, stop=True)
                P.op("tensor", mmo)
                P.op("vector", lambda E, dm=dm: E.tensor_copy(OC[:, dm * 128:(dm + 1) * 128],
                                                              pO[:, 0:128]))
            for dm in range(2):
                P.op("scalar", lambda E, dm=dm: E.activation(
                    SQB[:, 0:128], OC[:, dm * 128:(dm + 1) * 128], AF.Square))
                P.op("tensor", lambda E, dm=dm: E.matmul(
                    pst[0:1, 0:128], C.ones_col[:, :], SQB[:, 0:128],
                    start=(dm == 0), stop=(dm == 1)))
            P.op("vector", lambda E: E.tensor_scalar(rown[0:1, 0:128], pst[0:1, 0:128],
                                                     1.0 / DH, LN_EPS, ALU.mult, ALU.add))
            P.op("scalar", lambda E: E.activation(rown[0:1, 0:128], rown[0:1, 0:128], AF.Sqrt))
            P.op("vector", lambda E: E.reciprocal(rown[0:1, 0:128], rown[0:1, 0:128]))
            P.op("vector", lambda E: E.tensor_copy(rnbf[0:1, 0:128], rown[0:1, 0:128]))
            P.op("tensor", lambda E: E.matmul(pB[:, 0:128], C.ones_row[0:1, 0:128],
                                              rnbf[0:1, 0:128], start=True, stop=True))
            for dm in range(2):
                P.op("vector", lambda E, dm=dm: E.tensor_mul(
                    OC[:, dm * 128:(dm + 1) * 128], OC[:, dm * 128:(dm + 1) * 128],
                    pB[:, 0:128]))
                P.op("vector", lambda E, dm=dm, t0=t0: E.tensor_scalar_mul(
                    OBF[:, dm * S + t0:dm * S + t0 + 128],
                    OC[:, dm * 128:(dm + 1) * 128], gcol[:, dm:dm + 1]))
            def mmkt(E, t0=t0):
                E.transpose(pT[:, 0:128], KC[:, 0 * S + t0:0 * S + t0 + 128], C.ident[:, :])
                return E.transpose(pT[:, 128:256], KC[:, 1 * S + t0:1 * S + t0 + 128], C.ident[:, :])
            P.op("tensor", mmkt)
            P.op("vector", lambda E: E.tensor_copy(KTc[:, :], pT[:, 0:256]))
            for dm in range(2):
                P.op("tensor", lambda E, dm=dm: E.matmul(
                    pS[:, 0:256], KTc[:, dm * 128:(dm + 1) * 128], UB[:, :],
                    start=True, stop=True))
                P.op("vector", lambda E, dm=dm: E.tensor_add(
                    STATE[:, dm * DH:(dm + 1) * DH], STATE[:, dm * DH:(dm + 1) * DH],
                    pS[:, 0:256]))
        if fused:
            for dt in range(2):
                for j in range(NCORE):
                    P.dma(oo[j * 512 + pair * 256 + dt * 128:j * 512 + pair * 256 + (dt + 1) * 128, :],
                          OBF[:, dt * S + j * 512:dt * S + (j + 1) * 512])
        else:
            for dt in range(2):
                P.dma(oo[pair, dt * 128:(dt + 1) * 128, :], OBF[:, dt * S:(dt + 1) * S])


def build_l2a_scan(nc):
    P = Prog(nc)
    DH = 256
    qp_d = nc.dram_tensor("qp", [2, DH, S], BF16, kind="ExternalInput")
    kp_d = nc.dram_tensor("kp", [2, DH, S], BF16, kind="ExternalInput")
    vp_d = nc.dram_tensor("vp", [2, DH, S], BF16, kind="ExternalInput")
    bt_d = nc.dram_tensor("bt", [2, 1, S], F32, kind="ExternalInput")
    oo = nc.dram_tensor("oout", [2, DH, S], BF16, kind="ExternalOutput")
    G = _g_weights(nc, ["L2_attn_g_rms"], W_SHAPES)
    G["convq"] = nc.dram_tensor("convq", [2, 4, DH], F32, kind="ExternalInput")
    G["convk"] = nc.dram_tensor("convk", [2, 4, DH], F32, kind="ExternalInput")
    G["convv"] = nc.dram_tensor("convv", [2, 4, DH], F32, kind="ExternalInput")
    C = Consts(P)
    with P.scope("A2s_"):
        emit_l2a_scan(P, C, G, qp_d, kp_d, vp_d, bt_d, oo)
    P.finalize()
    return P


def emit_l2a_out(P, C, G, xd, xo, o_d):
    nc = P.nc
    wo_d = G["L2_attn_Wo"]
    lng_d = G["L2_attn_ln_g"]
    lnb_d = G["L2_attn_ln_b"]

    xres = P.sb("xres", [128, NKT * T], F32)
    xhat = P.sb("xhat", [128, NKT * T], BF16)
    Y = P.sb("Y", [128, NKT * T], F32)
    OFC = P.sb("OFC", [128, NKT * 512], BF16)
    WA = P.sb("WA", [128, NKT * D], BF16)
    grow = P.sb("grow", [1, 2 * D], BF16)
    gcol = P.sb("gcol", [128, 16], F32)
    pmain = P.ps("pmain", [128, 1024], F32)
    pst = P.ps("pst", [1, 1024], F32)
    pbc = P.ps("pbc", [128, 1024], F32)
    L = LnScratch(P, pst, pbc)

    load_x_fm(P, xd, xres)
    emit_ln(P, C, L, xres, xhat)
    w = load_w_nat(P, wo_d.ap(), D, D, WA, 0)
    P.cast_dma(grow[0:1, 0:D], lng_d[:].rearrange("(a n) -> a n", a=1))
    P.cast_dma(grow[0:1, D:2 * D], lnb_d[:].rearrange("(a n) -> a n", a=1))
    def mm_gc(E):
        last = None
        for kf in range(NKT):
            E.matmul(pbc[:, kf:kf + 1], grow[0:1, kf * 128:(kf + 1) * 128],
                     C.ones_row[0:1, 0:1], start=True, stop=True)
            last = E.matmul(pbc[:, 8 + kf:9 + kf], grow[0:1, D + kf * 128:D + (kf + 1) * 128],
                            C.ones_row[0:1, 0:1], start=True, stop=True)
        return last
    P.op("tensor", mm_gc)
    P.op("vector", lambda E: E.tensor_copy(gcol[:, 0:16], pbc[:, 0:16]))
    gamma_col = [gcol[:, kf:kf + 1] for kf in range(NKT)]
    beta_col = [gcol[:, 8 + kf:9 + kf] for kf in range(NKT)]

    for c in range(NCH):
        c0 = c * 512
        for kf in range(NKT):
            P.dma(OFC[:, kf * 512:(kf + 1) * 512], o_d[kf * 128:(kf + 1) * 128, c0:c0 + 512])
        for m in range(NKT):
            def mmo(E, m=m):
                last = None
                for k in range(NKT):
                    last = E.matmul(pmain[:, 0:512], w[k][:, m * 128:(m + 1) * 128],
                                    OFC[:, k * 512:(k + 1) * 512],
                                    start=(k == 0), stop=(k == NKT - 1))
                return last
            P.op("tensor", mmo)
            P.op("vector", lambda E, m=m, c0=c0: E.tensor_add(
                Y[:, m * T + c0:m * T + c0 + 512], pmain[:, 0:512],
                xhat[:, m * T + c0:m * T + c0 + 512]))
    emit_ln_add(P, C, L, Y, xres, gamma_col, beta_col)
    store_x_fm(P, xres, xo)


def build_l2a_out(nc):
    P = Prog(nc)
    xd = nc.dram_tensor("xin", [B, D, TPC], F32, kind="ExternalInput")
    xo = nc.dram_tensor("xout", [B, D, TPC], F32, kind="ExternalOutput")
    o_d = nc.dram_tensor("ofm", [D, T], BF16, kind="ExternalInput")
    G = _g_weights(nc, ["L2_attn_Wo", "L2_attn_ln_g", "L2_attn_ln_b"], W_SHAPES)
    C = Consts(P)
    with P.scope("A2o_"):
        emit_l2a_out(P, C, G, xd, xo, o_d)
    P.finalize()
    return P


def emit_ln_add(P, C, L, Y, xres, gamma_col, beta_col, t=T, nkt=NKT):
    nch = t // 512
    for c in range(nch):
        c0 = c * 512
        for kf in range(nkt):
            P.op("scalar", lambda E, kf=kf, c0=c0: E.activation(
                L.sq[:, :], Y[:, kf * t + c0:kf * t + c0 + 512], AF.Square))
            P.op("tensor", lambda E, kf=kf: E.matmul(
                L.pst[0:1, 512:1024], C.ones_col[:, :], L.sq[:, :],
                start=(kf == 0), stop=(kf == nkt - 1)))
            P.op("vector", lambda E, kf=kf, c0=c0: E.tensor_copy(
                L.u32[:, :], Y[:, kf * t + c0:kf * t + c0 + 512]))
            P.op("vector", lambda E, kf=kf: E.tensor_copy(L.sq[:, :], L.u32[:, :]))
            P.op("tensor", lambda E, kf=kf: E.matmul(
                L.pst[0:1, 0:512], C.ones_col[:, :], L.sq[:, :],
                start=(kf == 0), stop=(kf == nkt - 1)))
        P.op("vector", lambda E: E.tensor_scalar_mul(L.rowm[:, :], L.pst[0:1, 0:512], 1.0 / D))
        P.op("vector", lambda E: E.tensor_scalar_mul(L.rowe[:, :], L.pst[0:1, 512:1024], 1.0 / D))
        P.op("vector", lambda E: E.tensor_mul(L.rowa[:, :], L.rowm[:, :], L.rowm[:, :]))
        P.op("vector", lambda E: E.tensor_sub(L.rowe[:, :], L.rowe[:, :], L.rowa[:, :]))
        P.op("vector", lambda E: E.tensor_scalar_add(L.rowe[:, :], L.rowe[:, :], LN_EPS))
        P.op("scalar", lambda E: E.activation(L.rowe[:, :], L.rowe[:, :], AF.Sqrt))
        P.op("vector", lambda E: E.reciprocal(L.rowa[:, :], L.rowe[:, :]))
        P.op("vector", lambda E: E.tensor_mul(L.rowm[:, :], L.rowm[:, :], L.rowa[:, :]))
        P.op("vector", lambda E: E.tensor_scalar_mul(L.rowm[:, :], L.rowm[:, :], -1.0))
        P.op("vector", lambda E: E.tensor_copy(L.rowbf[0:1, 0:512], L.rowa[:, :]))
        P.op("vector", lambda E: E.tensor_copy(L.rowbf[0:1, 512:1024], L.rowm[:, :]))
        def mm_bc(E):
            E.matmul(L.pbc[:, 0:512], C.ones_row[0:1, 0:128], L.rowbf[0:1, 0:512],
                     start=True, stop=True)
            return E.matmul(L.pbc[:, 512:1024], C.ones_row[0:1, 0:128],
                            L.rowbf[0:1, 512:1024], start=True, stop=True)
        P.op("tensor", mm_bc)
        for kf in range(nkt):
            yv = Y[:, kf * t + c0:kf * t + c0 + 512]
            xv = xres[:, kf * t + c0:kf * t + c0 + 512]
            P.op("vector", lambda E, yv=yv: E.tensor_mul(L.u32[:, :], yv, L.pbc[:, 0:512]))
            P.op("vector", lambda E: E.tensor_add(L.u32[:, :], L.u32[:, :], L.pbc[:, 512:1024]))
            P.op("vector", lambda E, kf=kf: E.tensor_scalar(
                L.u32[:, :], L.u32[:, :], gamma_col[kf], beta_col[kf], ALU.mult, ALU.add))
            P.op("vector", lambda E, xv=xv: E.tensor_add(xv, xv, L.u32[:, :]))


def stage_l2_attn(xs, w):
    names = ["L2_attn_Wq", "L2_attn_Wk", "L2_attn_Wv", "L2_attn_Wb"]
    res = run_launch(build_l2a_proj, [
        dict(xin=xs[c], **{n: w[n] for n in names}) for c in range(NCORE)])
    qp = [res[c]["qpre"] for c in range(NCORE)]
    kp = [res[c]["kpre"] for c in range(NCORE)]
    vp = [res[c]["vpre"] for c in range(NCORE)]
    bt = [res[c]["beta"] for c in range(NCORE)]

    def pair_list(c):
        b = c // 2
        return [(b, 2 * (c % 2)), (b, 2 * (c % 2) + 1)]

    def gather_fm(arrs, h, b):
        return np.concatenate(
            [arrs[cc][h * 256:(h + 1) * 256, b * 512:(b + 1) * 512] for cc in range(NCORE)],
            axis=1)

    im = []
    for c in range(NCORE):
        prs = pair_list(c)
        im.append(dict(
            qp=np.stack([gather_fm(qp, h, b) for (b, h) in prs]),
            kp=np.stack([gather_fm(kp, h, b) for (b, h) in prs]),
            vp=np.stack([gather_fm(vp, h, b) for (b, h) in prs]),
            bt=np.stack([np.concatenate(
                [bt[cc][h:h + 1, b * 512:(b + 1) * 512] for cc in range(NCORE)], axis=1)
                for (b, h) in prs]),
            convq=np.stack([np.ascontiguousarray(w["L2_attn_conv_q"][:, h * 256:(h + 1) * 256]) for (b, h) in prs]),
            convk=np.stack([np.ascontiguousarray(w["L2_attn_conv_k"][:, h * 256:(h + 1) * 256]) for (b, h) in prs]),
            convv=np.stack([np.ascontiguousarray(w["L2_attn_conv_v"][:, h * 256:(h + 1) * 256]) for (b, h) in prs]),
            L2_attn_g_rms=w["L2_attn_g_rms"],
        ))
    res = run_launch(build_l2a_scan, im)
    oo = [res[c]["oout"] for c in range(NCORE)]

    full = {}
    for c in range(NCORE):
        for j, (b, h) in enumerate(pair_list(c)):
            full[(b, h)] = oo[c][j]
    im = []
    for c in range(NCORE):
        cols = []
        for b in range(B):
            rows = [full[(b, h)][:, c * TPC:(c + 1) * TPC] for h in range(4)]
            cols.append(np.concatenate(rows, axis=0))
        ofm = np.concatenate(cols, axis=1)
        im.append(dict(xin=xs[c], ofm=np.ascontiguousarray(ofm),
                       L2_attn_Wo=w["L2_attn_Wo"], L2_attn_ln_g=w["L2_attn_ln_g"],
                       L2_attn_ln_b=w["L2_attn_ln_b"]))
    res = run_launch(build_l2a_out, im)
    return [res[c]["xout"] for c in range(NCORE)]



DEBUG_L2FFN = False
L2FFN_NAMES = ["L2_ffn_router_W", "L2_ffn_router_b", "L2_ffn_sW1", "L2_ffn_sb1",
               "L2_ffn_sW2", "L2_ffn_sb2", "L2_ffn_eW1", "L2_ffn_eb1",
               "L2_ffn_eW2", "L2_ffn_eb2"]


def emit_l2_ffn(P, C, G, xd, xo):
    nc = P.nc
    E15, HH = 15, 1024
    rw_d = G["L2_ffn_router_W"]
    rb_d = G["L2_ffn_router_b"]
    sw1_d = G["L2_ffn_sW1"]
    sb1_d = G["L2_ffn_sb1"]
    sw2_d = G["L2_ffn_sW2"]
    sb2_d = G["L2_ffn_sb2"]
    ew1_d = G["L2_ffn_eW1"]
    eb1_d = G["L2_ffn_eb1"]
    ew2_d = G["L2_ffn_eW2"]
    eb2_d = G["L2_ffn_eb2"]

    xres = P.sb("xres", [128, NKT * T], F32)
    xhat = P.sb("xhat", [128, NKT * T], BF16)
    W1A = P.sb("W1A", [128, NKT * HH], BF16)
    W2A = P.sb("W2A", [128, 8 * D], BF16)
    WRA = P.sb("WRA", [128, NKT * E15], BF16)
    EB2 = P.sb("EB2", [15, D], BF16)
    HE = P.sb("HE", [128, 8 * 512], BF16)
    b1r = P.sb("b1r", [1, HH], BF16)
    b2r = P.sb("b2r", [1, D], BF16)
    rbr = P.sb("rbr", [1, E15], BF16)
    CWT = P.sb("CWT", [128, NTT * E15], F32)
    CWF = P.sb("CWF", [15, T], BF16)
    CWR1 = P.sb("CWR1", [1, T], BF16)
    LGT = P.sb("LGT", [128, 15], F32)
    ecol = P.sb("ecol", [128, 15], F32)
    colm = P.sb("colm", [128, 1], F32)
    cols = P.sb("cols", [128, 1], F32)
    colr = P.sb("colr", [128, 1], F32)
    MSK = P.sb("MSK", [128, 15], F32)
    PRB = P.sb("PRB", [128, 15], F32)
    RLG = P.sb("RLG", [15, 128], F32)

    pmain = P.ps("pmain", [128, 1024], F32)
    pst = P.ps("pst", [1, 1024], F32)
    pbc = P.ps("pbc", [128, 1024], F32)
    psT = P.ps("psT", [128, 128], BF16)
    L = LnScratch(P, pst, pbc)

    load_x_fm(P, xd, xres)
    emit_ln(P, C, L, xres, xhat)

    wr = load_w_nat(P, rw_d.ap(), D, E15, WRA, 0)
    P.cast_dma(rbr[:, :], rb_d[:].rearrange("(a n) -> a n", a=1))
    for tt in range(NTT):
        t0 = tt * 128
        def mmr(E, t0=t0):
            last = None
            for k in range(NKT):
                last = E.matmul(pst[0:15, 0:128] if False else pmain[0:15, 0:128],
                                wr[k][:, :], xhat[:, k * T + t0:k * T + t0 + 128],
                                start=(k == 0), stop=False)
            return E.matmul(pmain[0:15, 0:128], rbr[0:1, :], C.ones_row[0:1, 0:128],
                            start=False, stop=True)
        P.op("tensor", mmr)
        P.op("vector", lambda E: E.tensor_copy(RLG[:, :], pmain[0:15, 0:128]))
        P.op("tensor", lambda E: E.transpose(pbc[:, 0:15], RLG[:, :],
                                             C.ident32[0:15, 0:15]))
        P.op("vector", lambda E: E.tensor_copy(LGT[:, :], pbc[:, 0:15]))
        P.op("vector", lambda E: E.reduce_max(colm[:, :], LGT[:, :], axis=AX.X, negate=True))
        P.op("scalar", lambda E: E.activation(ecol[:, :], LGT[:, :], AF.Exp,
                                              bias=colm[:, :], accum_out=cols[:, :]))
        P.op("vector", lambda E: E.reciprocal(colr[:, :], cols[:, :]))
        P.op("vector", lambda E: E.tensor_scalar_mul(PRB[:, :], ecol[:, :], colr[:, :]))
        P.op("vector", lambda E: E.reduce_max(colm[:, :], PRB[:, :], axis=AX.X))
        P.op("vector", lambda E: E.tensor_scalar(MSK[:, :], PRB[:, :], colm[:, :], None, ALU.is_ge))
        P.op("vector", lambda E: E.scalar_tensor_tensor(ecol[:, :], MSK[:, :], -1e30,
                                                        PRB[:, :], ALU.mult, ALU.add))
        P.op("vector", lambda E: E.reduce_max(colm[:, :], ecol[:, :], axis=AX.X))
        P.op("vector", lambda E: E.tensor_scalar(MSK[:, :], ecol[:, :], colm[:, :], None, ALU.is_ge))
        P.op("vector", lambda E: E.scalar_tensor_tensor(ecol[:, :], MSK[:, :], -1e30,
                                                        ecol[:, :], ALU.mult, ALU.add))
        P.op("vector", lambda E: E.reduce_max(colm[:, :], ecol[:, :], axis=AX.X))
        P.op("vector", lambda E: E.tensor_scalar(MSK[:, :], PRB[:, :], colm[:, :], None, ALU.is_ge))
        P.op("vector", lambda E, tt=tt: E.tensor_mul(CWT[:, tt * E15:(tt + 1) * E15],
                                                     PRB[:, :], MSK[:, :]))

    CWB = HE
    for tt in range(NTT):
        P.op("vector", lambda E, tt=tt: E.tensor_copy(CWB[:, 0:E15], CWT[:, tt * E15:(tt + 1) * E15]))
        P.op("tensor", lambda E: E.transpose(psT[0:15, 0:128], CWB[:, 0:E15], C.ident[:, :]))
        P.op("vector", lambda E, tt=tt: E.tensor_copy(CWF[:, tt * 128:(tt + 1) * 128],
                                                      psT[0:15, 0:128]))

    w1 = load_w_nat(P, sw1_d.ap(), D, HH, W1A, 0)
    w2 = load_w_nat(P, sw2_d.ap(), HH, D, W2A, 0)
    P.cast_dma(b1r[:, :], sb1_d[:].rearrange("(a n) -> a n", a=1))
    P.cast_dma(b2r[:, :], sb2_d[:].rearrange("(a n) -> a n", a=1))
    eb2v = load_w_nat(P, eb2_d.ap(), E15, D, EB2, 0, kgran=15)
    for c in range(NCH):
        c0 = c * 512
        pp = PP(P, [pmain[:, 0:512], pmain[:, 512:1024]])
        for m in range(NKT):
            def mmh(E, pv, m=m, c0=c0):
                last = None
                for k in range(NKT):
                    last = E.matmul(pv, w1[k][:, m * 128:(m + 1) * 128],
                                    xhat[:, k * T + c0:k * T + c0 + 512],
                                    start=(k == 0), stop=False)
                return E.matmul(pv, b1r[0:1, m * 128:(m + 1) * 128],
                                C.ones_row[0:1, 0:512], start=False, stop=True)
            pp.step(mmh, lambda pv, m=m: P.op(
                "scalar", lambda E: E.activation(HE[:, m * 512:(m + 1) * 512],
                                                 pv, AF.Relu)))
        pp = PP(P, [pmain[:, 0:512], pmain[:, 512:1024]])
        for m in range(NKT):
            def mmo(E, pv, m=m, c0=c0):
                last = None
                for k in range(NKT):
                    last = E.matmul(pv, w2[k][:, m * 128:(m + 1) * 128],
                                    HE[:, k * 512:(k + 1) * 512],
                                    start=(k == 0), stop=False)
                E.matmul(pv, b2r[0:1, m * 128:(m + 1) * 128],
                         C.ones_row[0:1, 0:512], start=False, stop=False)
                return E.matmul(pv, eb2v[0][0:15, m * 128:(m + 1) * 128],
                                CWF[:, c0:c0 + 512], start=False, stop=True)
            def evo(pv, m=m, c0=c0):
                P.op("vector", lambda E: E.tensor_add(
                    xres[:, m * T + c0:m * T + c0 + 512],
                    xres[:, m * T + c0:m * T + c0 + 512], pv))
            pp.step(mmo, evo)

    w1_tok = 0
    for e in range(E15):
        w1e = load_w_nat(P, layer2d(ew1_d, e), D, HH, W1A, 0, wait=w1_tok if e else None)
        w2e = load_w_nat(P, layer2d(ew2_d, e), HH, D, W2A, 0)
        P.cast_dma(b1r[:, :], eb1_d[e:e + 1, :])
        for tt in range(NTT):
            P.op("vector", lambda E, tt=tt, e=e: E.tensor_copy(
                ecol[:, 0:1], CWT[:, tt * E15 + e:tt * E15 + e + 1]))
            P.op("tensor", lambda E: E.transpose(pst[0:1, 0:128], ecol[:, 0:1], C.ident32[:, :]))
            P.op("vector", lambda E, tt=tt: E.tensor_copy(CWR1[0:1, tt * 128:(tt + 1) * 128],
                                                          pst[0:1, 0:128]))
        P.bridge("vector", P.wsem, P.wcnt)
        for c in range(NCH):
            c0 = c * 512
            P.op("tensor", lambda E, c0=c0: E.matmul(pbc[:, 0:512], C.ones_row[0:1, 0:128],
                                                     CWR1[0:1, c0:c0 + 512], start=True, stop=True))
            pp = PP(P, [pmain[:, 0:512], pmain[:, 512:1024]])
            for m in range(NKT):
                def mmh(E, pv, m=m, c0=c0, w1e=w1e):
                    last = None
                    for k in range(NKT):
                        last = E.matmul(pv, w1e[k][:, m * 128:(m + 1) * 128],
                                        xhat[:, k * T + c0:k * T + c0 + 512],
                                        start=(k == 0), stop=False)
                    return E.matmul(pv, b1r[0:1, m * 128:(m + 1) * 128],
                                    C.ones_row[0:1, 0:512], start=False, stop=True)
                def evh(pv, m=m):
                    P.op("scalar", lambda E: E.activation(
                        HE[:, m * 512:(m + 1) * 512], pv, AF.Relu))
                    P.op("vector", lambda E: E.tensor_mul(
                        HE[:, m * 512:(m + 1) * 512], HE[:, m * 512:(m + 1) * 512],
                        pbc[:, 0:512]))
                pp.step(mmh, evh)
            if c == NCH - 1:
                w1_tok = P.tok
            pp = PP(P, [pmain[:, 0:512], pmain[:, 512:1024]])
            for m in range(NKT):
                def mmo(E, pv, m=m, w2e=w2e):
                    last = None
                    for k in range(NKT):
                        last = E.matmul(pv, w2e[k][:, m * 128:(m + 1) * 128],
                                        HE[:, k * 512:(k + 1) * 512],
                                        start=(k == 0), stop=(k == NKT - 1))
                    return last
                def evo(pv, m=m, c0=c0):
                    P.op("vector", lambda E: E.tensor_add(
                        xres[:, m * T + c0:m * T + c0 + 512],
                        xres[:, m * T + c0:m * T + c0 + 512], pv))
                pp.step(mmo, evo)

    store_x_fm(P, xres, xo)


def build_l2_ffn(nc):
    P = Prog(nc)
    xd = nc.dram_tensor("xin", [B, D, TPC], F32, kind="ExternalInput")
    xo = nc.dram_tensor("xout", [B, D, TPC], F32, kind="ExternalOutput")
    G = _g_weights(nc, L2FFN_NAMES, W_SHAPES)
    C = Consts(P)
    with P.scope("F2_"):
        emit_l2_ffn(P, C, G, xd, xo)
    P.finalize()
    return P


def stage_l2_ffn(xs, w):
    names = ["L2_ffn_router_W", "L2_ffn_router_b", "L2_ffn_sW1", "L2_ffn_sb1",
             "L2_ffn_sW2", "L2_ffn_sb2", "L2_ffn_eW1", "L2_ffn_eb1",
             "L2_ffn_eW2", "L2_ffn_eb2"]
    res = run_launch(build_l2_ffn, [
        dict(xin=xs[c], **{n: w[n] for n in names}) for c in range(NCORE)])
    return [res[c]["xout"] for c in range(NCORE)]



def emit_ln_chunkio(P, C, L, XCH, xhat, c0, t=T, nkt=NKT):
    for kf in range(nkt):
        P.op("vector", lambda E, kf=kf: E.tensor_copy(
            L.sq[:, :], XCH[:, kf * 512:(kf + 1) * 512]))
        P.op("tensor", lambda E, kf=kf: E.matmul(
            L.pst[0:1, 0:512], C.ones_col[:, :], L.sq[:, :],
            start=(kf == 0), stop=(kf == nkt - 1)))
    for kf in range(nkt):
        P.op("scalar", lambda E, kf=kf: E.activation(
            L.sq[:, :], XCH[:, kf * 512:(kf + 1) * 512], AF.Square))
        P.op("tensor", lambda E, kf=kf: E.matmul(
            L.pst[0:1, 512:1024], C.ones_col[:, :], L.sq[:, :],
            start=(kf == 0), stop=(kf == nkt - 1)))
    P.op("vector", lambda E: E.tensor_scalar_mul(L.rowm[:, :], L.pst[0:1, 0:512], 1.0 / D))
    P.op("vector", lambda E: E.tensor_scalar_mul(L.rowe[:, :], L.pst[0:1, 512:1024], 1.0 / D))
    P.op("vector", lambda E: E.tensor_mul(L.rowa[:, :], L.rowm[:, :], L.rowm[:, :]))
    P.op("vector", lambda E: E.tensor_sub(L.rowe[:, :], L.rowe[:, :], L.rowa[:, :]))
    P.op("vector", lambda E: E.tensor_scalar_add(L.rowe[:, :], L.rowe[:, :], LN_EPS))
    P.op("scalar", lambda E: E.activation(L.rowe[:, :], L.rowe[:, :], AF.Sqrt))
    P.op("vector", lambda E: E.reciprocal(L.rowa[:, :], L.rowe[:, :]))
    P.op("vector", lambda E: E.tensor_mul(L.rowm[:, :], L.rowm[:, :], L.rowa[:, :]))
    P.op("vector", lambda E: E.tensor_scalar_mul(L.rowm[:, :], L.rowm[:, :], -1.0))
    P.op("vector", lambda E: E.tensor_copy(L.rowbf[0:1, 0:512], L.rowa[:, :]))
    P.op("vector", lambda E: E.tensor_copy(L.rowbf[0:1, 512:1024], L.rowm[:, :]))
    def mm_bc(E):
        E.matmul(L.pbc[:, 0:512], C.ones_row[0:1, 0:128], L.rowbf[0:1, 0:512],
                 start=True, stop=True)
        return E.matmul(L.pbc[:, 512:1024], C.ones_row[0:1, 0:128],
                        L.rowbf[0:1, 512:1024], start=True, stop=True)
    P.op("tensor", mm_bc)
    for kf in range(nkt):
        hv = xhat[:, kf * t + c0:kf * t + c0 + 512]
        P.op("vector", lambda E, kf=kf: E.tensor_mul(
            L.u32[:, :], XCH[:, kf * 512:(kf + 1) * 512], L.pbc[:, 0:512]))
        P.op("vector", lambda E, hv=hv: E.tensor_add(hv, L.u32[:, :], L.pbc[:, 512:1024]))


def emit_l3a_proj(P, C, G, xd, so, go):
    nc = P.nc
    DI = 2048
    ip_d = G["L3_attn_in_proj_w"]

    XCH = P.sb("XCH", [128, NKT * 512], F32)
    xhat = P.sb("xhat", [128, NKT * T], BF16)
    WT = P.sb("WT", [128, NKT * DI], BF16)
    NAT = P.sb("NAT", [128, D], BF16)
    OT = P.sb("OT", [128, 512], BF16)
    pmain = P.ps("pmain", [128, 1024], F32)
    pst = P.ps("pst", [1, 1024], F32)
    pbc = P.ps("pbc", [128, 1024], F32)
    psT = P.ps("psT", [128, 128], BF16)
    L = LnScratch(P, pst, pbc)

    for c in range(NCH):
        for kf in range(NKT):
            srcv = xd[c:c + 1, kf * 128:(kf + 1) * 128, :].rearrange("b p t -> p (b t)")
            P.dma(XCH[:, kf * 512:(kf + 1) * 512], srcv)
        emit_ln_chunkio(P, C, L, XCH, xhat, c * 512)

    for half, od in ((0, so), (1, go)):
        wt = load_w_tr(P, C, ip_d[half * DI:(half + 1) * DI, :], DI, D, WT, 0, NAT, psT)
        for m in range(DI // 128):
            for c in range(NCH):
                def mmp(E, m=m, c=c, wt=wt):
                    last = None
                    for k in range(NKT):
                        last = E.matmul(pmain[:, 0:512], wt[k][:, m * 128:(m + 1) * 128],
                                        xhat[:, k * T + c * 512:k * T + (c + 1) * 512],
                                        start=(k == 0), stop=(k == NKT - 1))
                    return last
                P.op("tensor", mmp)
                P.op("vector", lambda E: E.tensor_copy(OT[:, :], pmain[:, 0:512]))
                P.dma(od[m * 128:(m + 1) * 128, c * 512:(c + 1) * 512], OT[:, :])


def build_l3a_proj(nc):
    P = Prog(nc)
    DI = 2048
    xd = nc.dram_tensor("xin", [B, D, TPC], F32, kind="ExternalInput")
    so = nc.dram_tensor("xssm", [DI, T], BF16, kind="ExternalOutput")
    go = nc.dram_tensor("xgate", [DI, T], BF16, kind="ExternalOutput")
    G = _g_weights(nc, ["L3_attn_in_proj_w"], W_SHAPES)
    C = Consts(P)
    with P.scope("A3p_"):
        emit_l3a_proj(P, C, G, xd, so, go)
    P.finalize()
    return P


L3A_MID_NAMES = ["L3_attn_conv_weight", "L3_attn_conv_bias", "L3_attn_delta_w",
                 "L3_attn_delta_b", "L3_attn_B_w", "L3_attn_C_w", "L3_attn_D",
                 "L3_attn_out_proj_w"]


def emit_l3a_mid(P, C, G, s_d, g_d, dt_o, bx_o, ct_o, g2_o, x2_o, shalo_d=None):
    nc = P.nc
    DI = 2048
    NDT = DI // 128
    fused = G.get("fused")
    cw_d = G["L3_attn_conv_weight"]
    cb_d = G["L3_attn_conv_bias"]
    dw_d = G["L3_attn_delta_w"]
    db_d = G["L3_attn_delta_b"]
    bw_d = G["L3_attn_B_w"]
    cwt_d = G["L3_attn_C_w"]
    dd_d = G["L3_attn_D"]
    op_d = G["L3_attn_out_proj_w"]

    XC = P.sb("XC", [128, NDT * T], BF16)
    XG = P.sb("XG", [128, NDT * 512], BF16)
    XCD = P.sb("XCD", [128, NDT * 512], BF16)
    HALO = P.sb("HALO", [128, NDT * B * 3], F32)
    ACC = P.sb("ACC", [128, 512], F32)
    WCV = P.sb("WCV", [128, NDT * 4], F32)
    CBC = P.sb("CBC", [128, NDT], F32)
    DCOL = P.sb("DCOL", [128, NDT], F32)
    cbrow = P.sb("cbrow", [1, DI], BF16)
    ddrow = P.sb("ddrow", [1, DI], BF16)
    WDT = P.sb("WDT", [128, NDT * 16], BF16)
    WBX = P.sb("WBX", [128, NDT * 16], BF16)
    WCT = P.sb("WCT", [128, NDT * 16], BF16)
    NAT = P.sb("NAT", [128, DI], BF16)
    OPT = P.sb("OPT", [128, NDT * D], BF16)
    OT = P.sb("OT", [128, 512], BF16)
    DBT = P.sb("DBT", [128, 48], F32)
    TMP = P.sb("TMP", [128, 16], F32)
    dbrow = P.sb("dbrow", [1, 16], BF16)
    xab = P.sb("xab", [1, T], BF16)
    rown = P.sb("rown", [1, 512], F32)

    pmain = P.ps("pmain", [128, 1024], F32)
    pst = P.ps("pst", [1, 1024], F32)
    pbc = P.ps("pbc", [128, 1024], F32)
    psT = P.ps("psT", [128, 128], BF16)

    P.dma(WCV[:, :].rearrange("p (t f) -> p t f", f=4),
          cw_d[:, :].rearrange("(t p) f -> p t f", p=128))
    P.cast_dma(cbrow[0:1, :], cb_d[:].rearrange("(a n) -> a n", a=1))
    P.cast_dma(ddrow[0:1, :], dd_d[:].rearrange("(a n) -> a n", a=1))
    P.cast_dma(dbrow[0:1, :], db_d[:].rearrange("(a n) -> a n", a=1))
    def mmwc(E):
        last = None
        for dt in range(NDT):
            last = E.matmul(pbc[:, dt:dt + 1],
                            cbrow[0:1, dt * 128:(dt + 1) * 128],
                            C.ones_row[0:1, 0:1], start=True, stop=True)
        return last
    P.op("tensor", mmwc)
    P.op("vector", lambda E: E.tensor_copy(CBC[:, 0:NDT], pbc[:, 0:NDT]))
    def mmdc(E):
        last = None
        for dt in range(NDT):
            last = E.matmul(pbc[:, dt:dt + 1], ddrow[0:1, dt * 128:(dt + 1) * 128],
                            C.ones_row[0:1, 0:1], start=True, stop=True)
        return last
    P.op("tensor", mmdc)
    P.op("vector", lambda E: E.tensor_copy(DCOL[:, 0:NDT], pbc[:, 0:NDT]))

    if fused:
        hin_d = P.dram("halo_in", [DI, B * 3], BF16)
        hout_d = P.dram("halo_out", [NCORE * DI, B * 3], BF16)
        for bb in range(B):
            P.dma(hin_d[:, bb * 3:(bb + 1) * 3],
                  s_d[:, bb * TPC + TPC - 3:bb * TPC + TPC])
        P.cc("AllGather", ALU.bypass, [hin_d.ap().opt()], [hout_d.ap().opt()])
        prv = G["prv"]

        def rd_halo(E):
            sv = P.dyn(E, "prv") * (DI * B * 3)
            return E.dma_start(
                out=HALO[:, :].rearrange("p (t g) -> p t g", g=B * 3),
                in_=bass.AP(hout_d, sv, [[B * 3, 128], [128 * B * 3, NDT], [1, B * 3]]))
        P.op("gpsimd", rd_halo, inc=16)
        P.op("vector", lambda E: E.tensor_scalar_mul(HALO[:, :], HALO[:, :], G["bmcol"]))
    else:
        P.cast_dma(HALO[:, :].rearrange("p (t g) -> p t g", g=B * 3),
                   shalo_d[:, :].rearrange("(t p) g -> p t g", p=128))

    for dt in range(NDT):
        for ch in range(NCH):
            c0 = ch * 512
            P.dma(XG[:, 0:512], s_d[dt * 128:(dt + 1) * 128, c0:c0 + 512])
            P.op("vector", lambda E, dt=dt: E.tensor_scalar_mul(
                ACC[:, :], XG[:, 0:512], WCV[:, dt * 4 + 3:dt * 4 + 4]))
            for sh in (1, 2, 3):
                jj = 3 - sh
                P.op("vector", lambda E, dt=dt, sh=sh, jj=jj: E.scalar_tensor_tensor(
                    ACC[:, sh:512], XG[:, 0:512 - sh],
                    WCV[:, dt * 4 + jj:dt * 4 + jj + 1],
                    ACC[:, sh:512], ALU.mult, ALU.add))
                P.op("vector", lambda E, dt=dt, sh=sh, jj=jj, ch=ch: E.scalar_tensor_tensor(
                    ACC[:, 0:sh], HALO[:, dt * 12 + ch * 3 + (3 - sh):dt * 12 + ch * 3 + 3 - sh + sh],
                    WCV[:, dt * 4 + jj:dt * 4 + jj + 1],
                    ACC[:, 0:sh], ALU.mult, ALU.add))
            P.op("vector", lambda E, dt=dt: E.tensor_scalar_add(
                ACC[:, :], ACC[:, :], CBC[:, dt:dt + 1]))
            P.op("scalar", lambda E, dt=dt, c0=c0: E.activation(
                XC[:, dt * T + c0:dt * T + c0 + 512], ACC[:, :], AF.Silu))

    for ch in range(NCH):
        c0 = ch * 512
        def mmxa(E, c0=c0):
            last = None
            for dt in range(NDT):
                last = E.matmul(pst[0:1, 0:512], C.ones_col[:, :],
                                XC[:, dt * T + c0:dt * T + c0 + 512],
                                start=(dt == 0), stop=(dt == NDT - 1))
            return last
        P.op("tensor", mmxa)
        P.op("vector", lambda E: E.tensor_scalar_mul(rown[:, :], pst[0:1, 0:512], 1.0 / DI))
        P.op("vector", lambda E, c0=c0: E.tensor_copy(xab[0:1, c0:c0 + 512], rown[:, :]))

    wdt = load_w_tr(P, C, dw_d.ap(), 16, DI, WDT, 0, NAT, psT)
    wbx = load_w_tr(P, C, bw_d.ap(), 16, DI, WBX, 0, NAT, psT)
    wct = load_w_tr(P, C, cwt_d.ap(), 16, DI, WCT, 0, NAT, psT)

    for tt in range(NTT):
        t0 = tt * 128
        def mmdbc2(E, t0=t0):
            last = None
            for k in range(NDT):
                E.matmul(pmain[:, 0:16], XC[:, k * T + t0:k * T + t0 + 128],
                         wdt[k], start=(k == 0), stop=False)
            E.matmul(pmain[:, 0:16], C.ones_row[0:1, 0:128], dbrow[0:1, :],
                     start=False, stop=True)
            for k in range(NDT):
                E.matmul(pmain[:, 16:32], XC[:, k * T + t0:k * T + t0 + 128],
                         wbx[k], start=(k == 0), stop=(k == NDT - 1))
            for k in range(NDT):
                last = E.matmul(pmain[:, 32:48], XC[:, k * T + t0:k * T + t0 + 128],
                                wct[k], start=(k == 0), stop=(k == NDT - 1))
            return last
        P.op("tensor", mmdbc2)
        P.op("scalar", lambda E: E.activation(DBT[:, 0:16], pmain[:, 0:16], AF.Relu))
        P.op("scalar", lambda E: E.activation(TMP[:, :], pmain[:, 0:16], AF.Abs))
        P.op("scalar", lambda E: E.activation(TMP[:, :], TMP[:, :], AF.Exp, scale=-1.0))
        P.op("vector", lambda E: E.tensor_scalar_add(TMP[:, :], TMP[:, :], 1.0))
        P.op("scalar", lambda E: E.activation(TMP[:, :], TMP[:, :], AF.Ln))
        P.op("vector", lambda E: E.tensor_add(DBT[:, 0:16], DBT[:, 0:16], TMP[:, :]))
        def mkxacol(E, tt=tt):
            return E.matmul(pbc[:, 16:17], xab[0:1, tt * 128:(tt + 1) * 128],
                            C.ones_row[0:1, 0:1], start=True, stop=True)
        P.op("tensor", mkxacol)
        P.op("vector", lambda E: E.tensor_copy(TMP[:, 0:1], pbc[:, 16:17]))
        P.op("vector", lambda E: E.tensor_scalar_mul(DBT[:, 16:32], pmain[:, 16:32], TMP[:, 0:1]))
        P.op("vector", lambda E: E.tensor_copy(DBT[:, 32:48], pmain[:, 32:48]))
        if fused:
            bb = tt // 4
            l0c = (tt % 4) * 2
            for half in range(2):
                col = (l0c + half) * 64 + bb * 16
                P.dma(dt_o[0:64, col:col + 16], DBT[half * 64:(half + 1) * 64, 0:16])
                P.dma(bx_o[0:64, col:col + 16], DBT[half * 64:(half + 1) * 64, 16:32])
                P.dma(ct_o[0:64, col:col + 16], DBT[half * 64:(half + 1) * 64, 32:48])
        else:
            P.dma(dt_o[t0:t0 + 128, :], DBT[:, 0:16])
            P.dma(bx_o[t0:t0 + 128, :], DBT[:, 16:32])
            P.dma(ct_o[t0:t0 + 128, :], DBT[:, 32:48])

    opt = load_w_tr(P, C, op_d.ap(), D, DI, OPT, 0, NAT, psT)
    for c in range(NCH):
        c0 = c * 512
        for dt in range(NDT):
            P.dma(XG[:, dt * 512:(dt + 1) * 512], g_d[dt * 128:(dt + 1) * 128, c0:c0 + 512])
            P.op("scalar", lambda E, dt=dt: E.activation(
                XG[:, dt * 512:(dt + 1) * 512], XG[:, dt * 512:(dt + 1) * 512], AF.Silu))
            P.op("vector", lambda E, dt=dt, c0=c0: E.tensor_scalar_mul(
                XCD[:, dt * 512:(dt + 1) * 512], XC[:, dt * T + c0:dt * T + c0 + 512],
                DCOL[:, dt:dt + 1]))
        for m in range(NKT):
            for which, AR, od in ((0, XG, g2_o), (1, XCD, x2_o)):
                def mmg2(E, m=m, AR=AR):
                    last = None
                    for k in range(NDT):
                        last = E.matmul(pmain[:, 0:512], opt[k][:, m * 128:(m + 1) * 128],
                                        AR[:, k * 512:(k + 1) * 512],
                                        start=(k == 0), stop=(k == NDT - 1))
                    return last
                P.op("tensor", mmg2)
                P.op("vector", lambda E: E.tensor_copy(OT[:, :], pmain[:, 0:512]))
                P.dma(od[m * 128:(m + 1) * 128, c0:c0 + 512], OT[:, :])


def build_l3a_mid(nc):
    P = Prog(nc)
    DI = 2048
    s_d = nc.dram_tensor("xssm", [DI, T], BF16, kind="ExternalInput")
    g_d = nc.dram_tensor("xgate", [DI, T], BF16, kind="ExternalInput")
    shalo_d = nc.dram_tensor("shalo", [DI, B * 3], BF16, kind="ExternalInput")
    dt_o = nc.dram_tensor("dtq", [T, 16], F32, kind="ExternalOutput")
    bx_o = nc.dram_tensor("bxq", [T, 16], F32, kind="ExternalOutput")
    ct_o = nc.dram_tensor("ctq", [T, 16], F32, kind="ExternalOutput")
    g2_o = nc.dram_tensor("g2", [D, T], BF16, kind="ExternalOutput")
    x2_o = nc.dram_tensor("x2", [D, T], BF16, kind="ExternalOutput")
    G = _g_weights(nc, L3A_MID_NAMES, W_SHAPES)
    C = Consts(P)
    with P.scope("A3m_"):
        emit_l3a_mid(P, C, G, s_d, g_d, dt_o, bx_o, ct_o, g2_o, x2_o, shalo_d=shalo_d)
    P.finalize()
    return P


def emit_l3a_scan(P, C, G, sdt_d, sbx_d, sct_d, ys_o):
    nc = P.nc
    CK = 64
    NCK = S // CK
    al_d = G["L3_attn_A_log"]

    SDT = P.sb("SDT", [64, NCK * 64], F32)
    SBX = P.sb("SBX", [64, NCK * 64], F32)
    SCT = P.sb("SCT", [64, NCK * 64], F32)
    LT1 = P.sb("LT1", [64, 64], BF16)
    ABC = P.sb("ABC", [64, 64], F32)
    DTB = P.sb("DTB", [64, 64], BF16)
    UB16 = P.sb("UB16", [64, 64], BF16)
    LL = P.sb("LL", [64, 64], F32)
    DF = P.sb("DF", [64, 64], F32)
    EL = P.sb("EL", [64, 64], F32)
    EMU = P.sb("EMU", [64, 64], F32)
    UU = P.sb("UU", [64, 64], F32)
    HH = P.sb("HH", [64, 64], F32)
    YSALL = P.sb("YSALL", [64, NCK * B], F32)
    arow = P.sb("arow", [1, 64], F32)
    crw = P.sb("crw", [1, 64], F32)
    crbf = P.sb("crbf", [1, 64], BF16)
    h0 = P.sb("h0", [1, 64], F32)
    h0c = P.sb("h0c", [1, 64], F32)
    h0cb = P.sb("h0cb", [1, 64], BF16)
    uend = P.sb("uend", [1, 64], F32)
    ecr = P.sb("ecr", [1, 64], F32)

    pA = P.ps("pA", [64, 64], F32)
    pB = P.ps("pB2", [64, 64], F32)
    pst = P.ps("pst", [1, 512], F32)

    if G.get("fused"):
        for r in range(NCORE):
            P.dma(SDT[:, r * 512:(r + 1) * 512], sdt_d[r * 64:(r + 1) * 64, :])
            P.dma(SBX[:, r * 512:(r + 1) * 512], sbx_d[r * 64:(r + 1) * 64, :])
            P.dma(SCT[:, r * 512:(r + 1) * 512], sct_d[r * 64:(r + 1) * 64, :])
    else:
        P.dma(SDT[:, :], sdt_d[:, :])
        P.dma(SBX[:, :], sbx_d[:, :])
        P.dma(SCT[:, :], sct_d[:, :])
    lt_np = np.tril(np.ones((64, 64), dtype=NBF))
    P.dma(LT1[:, :], P.inline(lt_np, "h_lt1")[:, :])

    P.cast_dma(crbf[0:1, 0:16], al_d[:].rearrange("(a n) -> a n", a=1))
    P.op("vector", lambda E: E.tensor_copy(arow[0:1, 0:16], crbf[0:1, 0:16]))
    P.op("scalar", lambda E: E.activation(arow[0:1, 0:16], arow[0:1, 0:16], AF.Exp))
    P.op("vector", lambda E: E.tensor_scalar_mul(arow[0:1, 0:16], arow[0:1, 0:16], -1.0))
    for b in range(1, B):
        P.op("vector", lambda E, b=b: E.tensor_copy(arow[0:1, b * 16:(b + 1) * 16],
                                                    arow[0:1, 0:16]))
    P.op("vector", lambda E: E.tensor_copy(crbf[0:1, 0:64], arow[0:1, 0:64]))
    P.op("tensor", lambda E: E.matmul(pA[:, :], C.ones_row[0:1, 0:64], crbf[0:1, 0:64],
                                      start=True, stop=True))
    P.op("vector", lambda E: E.tensor_copy(ABC[:, :], pA[:, :]))
    P.op("vector", lambda E: E.tensor_mul(SBX[:, :], SBX[:, :], SDT[:, :]))
    P.op("vector", lambda E: E.memset(h0[:, :], 0.0))

    for ch in range(NCK):
        o0 = ch * 64
        P.op("vector", lambda E, o0=o0: E.tensor_copy(DTB[:, :], SDT[:, o0:o0 + 64]))
        P.op("tensor", lambda E: E.matmul(pA[:, :], LT1[:, :], DTB[:, :], start=True, stop=True))
        P.op("vector", lambda E: E.tensor_mul(LL[:, :], pA[:, :], ABC[:, :]))
        P.op("tensor", lambda E: E.matmul(pst[0:1, 0:64], C.ones_col[0:64, :], DTB[:, :],
                                          start=True, stop=True))
        P.op("vector", lambda E: E.tensor_mul(crw[0:1, :], pst[0:1, 0:64], arow[0:1, :]))
        P.op("vector", lambda E: E.tensor_scalar_mul(crw[0:1, :], crw[0:1, :], 0.5))
        P.op("scalar", lambda E: E.activation(ecr[0:1, :], crw[0:1, :], AF.Exp))
        P.op("vector", lambda E: E.tensor_mul(h0c[0:1, :], h0[0:1, :], ecr[0:1, :]))
        P.op("vector", lambda E: E.tensor_copy(crbf[0:1, :], crw[0:1, :]))
        P.op("vector", lambda E: E.tensor_copy(h0cb[0:1, :], h0c[0:1, :]))
        P.op("tensor", lambda E: E.matmul(pB[:, :], C.ones_row[0:1, 0:64], crbf[0:1, :],
                                          start=True, stop=True))
        P.op("vector", lambda E: E.tensor_sub(DF[:, :], LL[:, :], pB[:, :]))
        P.op("scalar", lambda E: E.activation(EL[:, :], DF[:, :], AF.Exp))
        P.op("scalar", lambda E: E.activation(EMU[:, :], DF[:, :], AF.Exp, scale=-1.0))
        P.op("vector", lambda E, o0=o0: E.tensor_mul(UU[:, :], EMU[:, :], SBX[:, o0:o0 + 64]))
        P.op("vector", lambda E: E.tensor_copy(UB16[:, :], UU[:, :]))
        def mmcu(E):
            E.matmul(pA[:, :], LT1[:, :], UB16[:, :], start=True, stop=False)
            return E.matmul(pA[:, :], C.ones_row[0:1, 0:64], h0cb[0:1, :],
                            start=False, stop=True)
        P.op("tensor", mmcu)
        P.op("vector", lambda E: E.tensor_mul(HH[:, :], EL[:, :], pA[:, :]))
        P.op("vector", lambda E, o0=o0: E.tensor_mul(HH[:, :], HH[:, :], SCT[:, o0:o0 + 64]))
        for b in range(B):
            P.op("vector", lambda E, b=b, ch=ch: E.reduce_sum(
                YSALL[:, ch * B + b:ch * B + b + 1], HH[:, b * 16:(b + 1) * 16], axis=AX.X))
        P.op("tensor", lambda E: E.matmul(pst[0:1, 64:128], C.ones_col[0:64, :], UB16[:, :],
                                          start=True, stop=True))
        P.op("vector", lambda E: E.tensor_add(uend[0:1, :], pst[0:1, 64:128], h0c[0:1, :]))
        P.op("vector", lambda E: E.tensor_mul(h0[0:1, :], uend[0:1, :], ecr[0:1, :]))
    P.dma(ys_o[:, :], YSALL[:, :])


def build_l3a_scan(nc):
    P = Prog(nc)
    CK = 64
    NCK = S // CK
    sdt_d = nc.dram_tensor("sdt", [CK, NCK * 64], F32, kind="ExternalInput")
    sbx_d = nc.dram_tensor("sbx", [CK, NCK * 64], F32, kind="ExternalInput")
    sct_d = nc.dram_tensor("sct", [CK, NCK * 64], F32, kind="ExternalInput")
    ys_o = nc.dram_tensor("ysall", [CK, NCK * B], F32, kind="ExternalOutput")
    G = _g_weights(nc, ["L3_attn_A_log"], W_SHAPES)
    C = Consts(P)
    with P.scope("A3s_"):
        emit_l3a_scan(P, C, G, sdt_d, sbx_d, sct_d, ys_o)
    P.finalize()
    return P


def emit_l3a_comb(P, C, G, xd, xo, ys_d, g2_d, x2_d):
    nc = P.nc
    fused = G.get("fused")

    xres = P.sb("xres", [128, NKT * T], F32)
    GG = P.sb("GG", [128, NKT * T], BF16)
    XX = P.sb("XX", [128, NKT * T], BF16)
    ysr = P.sb("ysr_sb", [1, T], F32)
    ysb = P.sb("ysb", [1, T], BF16)
    u32 = P.sb("u32f", [128, 512], F32)
    pbc = P.ps("pbc", [128, 512], F32)

    load_x_fm(P, xd, xres)
    for kf in range(NKT):
        P.dma(GG[:, kf * T:(kf + 1) * T], g2_d[kf * 128:(kf + 1) * 128, :])
        P.dma(XX[:, kf * T:(kf + 1) * T], x2_d[kf * 128:(kf + 1) * 128, :])
    if fused:
        YSO = P.sb("YSO", [64, 32], F32)
        cid = G["cid"]

        def rd_ys(E):
            sv = P.dyn(E, "cid") * 32
            return E.dma_start(out=YSO[:, :],
                               in_=bass.AP(ys_d, sv, [[S // 64 * B, 64], [1, 32]]))
        P.op("gpsimd", rd_ys, inc=16)
        for chl in range(8):
            for bb in range(B):
                P.op("tensor", lambda E, chl=chl, bb=bb: E.transpose(
                    pbc[0:1, 0:64], YSO[:, chl * B + bb:chl * B + bb + 1],
                    C.ident32[0:64, 0:64]))
                P.op("vector", lambda E, chl=chl, bb=bb: E.tensor_copy(
                    ysr[0:1, bb * 512 + chl * 64:bb * 512 + (chl + 1) * 64],
                    pbc[0:1, 0:64]))
    else:
        P.dma(ysr[:, :], ys_d[:, :])
    P.op("vector", lambda E: E.tensor_copy(ysb[:, :], ysr[:, :]))
    for c in range(NCH):
        c0 = c * 512
        P.op("tensor", lambda E, c0=c0: E.matmul(pbc[:, :], C.ones_row[0:1, 0:128],
                                                 ysb[0:1, c0:c0 + 512], start=True, stop=True))
        for m in range(NKT):
            P.op("vector", lambda E, m=m, c0=c0: E.tensor_mul(
                u32[:, :], GG[:, m * T + c0:m * T + c0 + 512], pbc[:, :]))
            P.op("vector", lambda E, m=m, c0=c0: E.tensor_add(
                xres[:, m * T + c0:m * T + c0 + 512],
                xres[:, m * T + c0:m * T + c0 + 512], u32[:, :]))
            P.op("vector", lambda E, m=m, c0=c0: E.tensor_add(
                xres[:, m * T + c0:m * T + c0 + 512],
                xres[:, m * T + c0:m * T + c0 + 512],
                XX[:, m * T + c0:m * T + c0 + 512]))
    store_x_fm(P, xres, xo)


def build_l3a_comb(nc):
    P = Prog(nc)
    xd = nc.dram_tensor("xin", [B, D, TPC], F32, kind="ExternalInput")
    xo = nc.dram_tensor("xout", [B, D, TPC], F32, kind="ExternalOutput")
    ys_d = nc.dram_tensor("ysr", [1, T], F32, kind="ExternalInput")
    g2_d = nc.dram_tensor("g2", [D, T], BF16, kind="ExternalInput")
    x2_d = nc.dram_tensor("x2", [D, T], BF16, kind="ExternalInput")
    G = {}
    C = Consts(P)
    with P.scope("A3c_"):
        emit_l3a_comb(P, C, G, xd, xo, ys_d, g2_d, x2_d)
    P.finalize()
    return P


def stage_l3_attn(xs, w):
    res = run_launch(build_l3a_proj, [
        dict(xin=xs[c], L3_attn_in_proj_w=w["L3_attn_in_proj_w"]) for c in range(NCORE)])
    xssm = [res[c]["xssm"] for c in range(NCORE)]
    xgate = [res[c]["xgate"] for c in range(NCORE)]

    names = ["L3_attn_conv_weight", "L3_attn_conv_bias", "L3_attn_delta_w",
             "L3_attn_delta_b", "L3_attn_B_w", "L3_attn_C_w", "L3_attn_D",
             "L3_attn_out_proj_w"]
    zhalo = np.zeros((2048, B * 3), dtype=NBF)

    def halo(c):
        if c == 0:
            return zhalo
        h = np.zeros((2048, B * 3), dtype=NBF)
        for b in range(B):
            h[:, b * 3:(b + 1) * 3] = xssm[c - 1][:, (b + 1) * TPC - 3:(b + 1) * TPC]
        return h

    res = run_launch(build_l3a_mid, [
        dict(xssm=xssm[c], xgate=xgate[c], shalo=halo(c),
             **{n: w[n] for n in names}) for c in range(NCORE)])
    dtq = [res[c]["dtq"] for c in range(NCORE)]
    bxq = [res[c]["bxq"] for c in range(NCORE)]
    ctq = [res[c]["ctq"] for c in range(NCORE)]
    g2 = [res[c]["g2"] for c in range(NCORE)]
    x2 = [res[c]["x2"] for c in range(NCORE)]

    def scanin(arrs):
        full = np.zeros((B, S, 16), dtype=np.float32)
        for c in range(NCORE):
            a = arrs[c].reshape(B, TPC, 16)
            full[:, c * TPC:(c + 1) * TPC, :] = a
        out = np.zeros((64, (S // 64) * 64), dtype=np.float32)
        for ch in range(S // 64):
            blk = full[:, ch * 64:(ch + 1) * 64, :]
            out[:, ch * 64:(ch + 1) * 64] = np.transpose(blk, (1, 0, 2)).reshape(64, 64)
        return np.ascontiguousarray(out)

    sdt, sbx, sct = scanin(dtq), scanin(bxq), scanin(ctq)
    res = run_launch(build_l3a_scan, [
        dict(sdt=sdt, sbx=sbx, sct=sct, L3_attn_A_log=w["L3_attn_A_log"])
        for c in range(NCORE)])
    ysall = res[0]["ysall"]
    ysf = np.zeros((B, S), dtype=np.float32)
    for ch in range(S // 64):
        ysf[:, ch * 64:(ch + 1) * 64] = ysall[:, ch * B:(ch + 1) * B].T

    res = run_launch(build_l3a_comb, [
        dict(xin=xs[c],
             ysr=np.ascontiguousarray(ysf[:, c * TPC:(c + 1) * TPC].reshape(1, T)),
             g2=g2[c], x2=x2[c]) for c in range(NCORE)])
    return [res[c]["xout"] for c in range(NCORE)]



L3FA_NAMES = ["L3_ffn_lin0_w", "L3_ffn_lin0_b", "L3_ffn_swi1_w1", "L3_ffn_swi1_b1",
              "L3_ffn_swi1_w2", "L3_ffn_swi1_b2", "L3_ffn_gate_w",
              "L3_ffn_sh_w1", "L3_ffn_sh_w3"]
L3FB_NAMES = ["L3_ffn_exp_w1", "L3_ffn_exp_w3", "L3_ffn_exp_w2", "L3_ffn_sh_w2",
              "L3_ffn_lin1_w", "L3_ffn_lin1_b", "L3_ffn_swi2_w1", "L3_ffn_swi2_b1",
              "L3_ffn_swi2_w2", "L3_ffn_swi2_b2"]


def emit_l3f_a(P, C, G, xd, h2_o, shh_o, cwr_o):
    nc = P.nc
    E8, SH = 8, 2048
    l0w_d = G["L3_ffn_lin0_w"]
    l0b_d = G["L3_ffn_lin0_b"]
    s1w1_d = G["L3_ffn_swi1_w1"]
    s1b1_d = G["L3_ffn_swi1_b1"]
    s1w2_d = G["L3_ffn_swi1_w2"]
    s1b2_d = G["L3_ffn_swi1_b2"]
    gw_d = G["L3_ffn_gate_w"]
    shw1_d = G["L3_ffn_sh_w1"]
    shw3_d = G["L3_ffn_sh_w3"]

    XCH = P.sb("XCH", [128, NKT * 512], F32)
    xhat = P.sb("xhat", [128, NKT * T], BF16)
    T1 = P.sb("T1", [128, NKT * T], BF16)
    H2 = xhat
    WA = P.sb("WA", [128, NKT * SH], BF16)
    WB = P.sb("WB", [128, NKT * SH], BF16)
    WG = P.sb("WG", [128, NKT * E8], BF16)
    SHH = P.sb("SHH", [128, 16 * 512], BF16)
    GT = P.sb("GT", [128, 512], BF16)
    brow = P.sb("brow", [1, 3 * D], BF16)
    LGT = P.sb("LGT", [128, E8], F32)
    ecol = P.sb("ecol", [128, E8], F32)
    PRB = P.sb("PRB", [128, E8], F32)
    MSK = P.sb("MSK", [128, E8], F32)
    CWc = P.sb("CWc", [128, E8], F32)
    CWb = P.sb("CWb", [128, E8], BF16)
    RLG = P.sb("RLG", [E8, 128], F32)
    colm = P.sb("colm", [128, 1], F32)
    cols = P.sb("cols", [128, 1], F32)
    colr = P.sb("colr", [128, 1], F32)
    CWR1 = P.sb("CWR1", [1, 128], BF16)
    NAT = P.sb("NAT", [128, D], BF16)

    pmain = P.ps("pmain", [128, 1024], F32)
    pst = P.ps("pst", [1, 1024], F32)
    pbc = P.ps("pbc", [128, 1024], F32)
    psT = P.ps("psT", [128, 128], BF16)
    L = LnScratch(P, pst, pbc)

    for c in range(NCH):
        for kf in range(NKT):
            srcv = xd[c:c + 1, kf * 128:(kf + 1) * 128, :].rearrange("b p t -> p (b t)")
            P.dma(XCH[:, kf * 512:(kf + 1) * 512], srcv)
        emit_ln_chunkio(P, C, L, XCH, xhat, c * 512)

    w0 = load_w_nat(P, l0w_d.ap(), D, D, WA, 0)
    P.cast_dma(brow[0:1, 0:D], l0b_d[:].rearrange("(a n) -> a n", a=1))
    for m in range(NKT):
        for c in range(NCH):
            def mm0(E, m=m, c=c):
                last = None
                for k in range(NKT):
                    last = E.matmul(pmain[:, 0:512], w0[k][:, m * 128:(m + 1) * 128],
                                    xhat[:, k * T + c * 512:k * T + (c + 1) * 512],
                                    start=(k == 0), stop=False)
                return E.matmul(pmain[:, 0:512], brow[0:1, m * 128:(m + 1) * 128],
                                C.ones_row[0:1, 0:512], start=False, stop=True)
            P.op("tensor", mm0)
            P.op("vector", lambda E, m=m, c=c: E.tensor_copy(
                T1[:, m * T + c * 512:(m * T + (c + 1) * 512)], pmain[:, 0:512]))
    w1 = load_w_nat(P, s1w1_d.ap(), D, D, WA, 0)
    w2 = load_w_nat(P, s1w2_d.ap(), D, D, WB, 0)
    P.cast_dma(brow[0:1, D:2 * D], s1b1_d[:].rearrange("(a n) -> a n", a=1))
    P.cast_dma(brow[0:1, 2 * D:3 * D], s1b2_d[:].rearrange("(a n) -> a n", a=1))
    for m in range(NKT):
        for c in range(NCH):
            def mm12(E, m=m, c=c):
                last = None
                for k in range(NKT):
                    E.matmul(pmain[:, 0:512], w1[k][:, m * 128:(m + 1) * 128],
                             T1[:, k * T + c * 512:k * T + (c + 1) * 512],
                             start=(k == 0), stop=False)
                E.matmul(pmain[:, 0:512], brow[0:1, D + m * 128:D + (m + 1) * 128],
                         C.ones_row[0:1, 0:512], start=False, stop=True)
                for k in range(NKT):
                    E.matmul(pmain[:, 512:1024], w2[k][:, m * 128:(m + 1) * 128],
                             T1[:, k * T + c * 512:k * T + (c + 1) * 512],
                             start=(k == 0), stop=False)
                return E.matmul(pmain[:, 512:1024], brow[0:1, 2 * D + m * 128:2 * D + (m + 1) * 128],
                                C.ones_row[0:1, 0:512], start=False, stop=True)
            P.op("tensor", mm12)
            P.op("scalar", lambda E: E.activation(GT[:, :], pmain[:, 0:512], AF.Silu))
            P.op("vector", lambda E, m=m, c=c: E.tensor_mul(
                H2[:, m * T + c * 512:m * T + (c + 1) * 512], GT[:, :], pmain[:, 512:1024]))
            P.op("vector", lambda E, m=m, c=c: E.tensor_copy(GT[:, :],
                H2[:, m * T + c * 512:m * T + (c + 1) * 512]))
            P.dma(h2_o[m * 128:(m + 1) * 128, c * 512:(c + 1) * 512], GT[:, :])

    wg = load_w_tr(P, C, gw_d.ap(), E8, D, WG, 0, NAT, psT)
    for tt in range(NTT):
        t0 = tt * 128
        def mmg(E, t0=t0):
            last = None
            for k in range(NKT):
                last = E.matmul(pmain[0:E8, 0:128], wg[k],
                                H2[:, k * T + t0:k * T + t0 + 128],
                                start=(k == 0), stop=(k == NKT - 1))
            return last
        P.op("tensor", mmg)
        P.op("vector", lambda E: E.tensor_copy(RLG[:, :], pmain[0:E8, 0:128]))
        P.op("tensor", lambda E: E.transpose(pbc[:, 0:E8], RLG[:, :], C.ident32[0:E8, 0:E8]))
        P.op("vector", lambda E: E.tensor_copy(LGT[:, :], pbc[:, 0:E8]))
        P.op("vector", lambda E: E.reduce_max(colm[:, :], LGT[:, :], axis=AX.X, negate=True))
        P.op("scalar", lambda E: E.activation(ecol[:, :], LGT[:, :], AF.Exp,
                                              bias=colm[:, :], accum_out=cols[:, :]))
        P.op("vector", lambda E: E.reciprocal(colr[:, :], cols[:, :]))
        P.op("vector", lambda E: E.tensor_scalar_mul(PRB[:, :], ecol[:, :], colr[:, :]))
        P.op("vector", lambda E: E.tensor_copy(ecol[:, :], PRB[:, :]))
        for r in range(3):
            P.op("vector", lambda E: E.reduce_max(colm[:, :], ecol[:, :], axis=AX.X))
            P.op("vector", lambda E: E.tensor_scalar(MSK[:, :], ecol[:, :], colm[:, :], None, ALU.is_ge))
            P.op("vector", lambda E: E.scalar_tensor_tensor(ecol[:, :], MSK[:, :], -1e30,
                                                            ecol[:, :], ALU.mult, ALU.add))
        P.op("vector", lambda E: E.reduce_max(colm[:, :], ecol[:, :], axis=AX.X))
        P.op("vector", lambda E: E.tensor_scalar(MSK[:, :], PRB[:, :], colm[:, :], None, ALU.is_ge))
        P.op("vector", lambda E: E.tensor_mul(CWc[:, :], PRB[:, :], MSK[:, :]))
        P.op("vector", lambda E: E.reduce_sum(cols[:, :], CWc[:, :], axis=AX.X))
        P.op("vector", lambda E: E.reciprocal(colr[:, :], cols[:, :]))
        P.op("vector", lambda E: E.tensor_scalar_mul(CWc[:, :], CWc[:, :], colr[:, :]))
        P.op("vector", lambda E: E.tensor_copy(CWb[:, :], CWc[:, :]))
        for e in range(E8):
            P.op("tensor", lambda E, e=e: E.transpose(psT[0:1, 0:128], CWb[:, e:e + 1],
                                                      C.ident[:, :]))
            P.op("vector", lambda E: E.tensor_copy(CWR1[0:1, :], psT[0:1, 0:128]))
            P.dma(cwr_o[e:e + 1, t0:t0 + 128], CWR1[0:1, :])

    ws1 = load_w_nat(P, shw1_d.ap(), D, SH, WA, 0)
    ws3 = load_w_nat(P, shw3_d.ap(), D, SH, WB, 0)
    for c in range(NCH):
        c0 = c * 512
        for m in range(16):
            def mmsh(E, m=m, c0=c0):
                last = None
                for k in range(NKT):
                    E.matmul(pmain[:, 0:512], ws1[k][:, m * 128:(m + 1) * 128],
                             H2[:, k * T + c0:k * T + c0 + 512],
                             start=(k == 0), stop=(k == NKT - 1))
                for k in range(NKT):
                    last = E.matmul(pmain[:, 512:1024], ws3[k][:, m * 128:(m + 1) * 128],
                                    H2[:, k * T + c0:k * T + c0 + 512],
                                    start=(k == 0), stop=(k == NKT - 1))
                return last
            P.op("tensor", mmsh)
            P.op("scalar", lambda E: E.activation(GT[:, :], pmain[:, 0:512], AF.Silu))
            P.op("vector", lambda E, m=m: E.tensor_mul(SHH[:, m * 512:(m + 1) * 512],
                                                       GT[:, :], pmain[:, 512:1024]))
        for m in range(16):
            P.dma(shh_o[m * 128:(m + 1) * 128, c0:c0 + 512], SHH[:, m * 512:(m + 1) * 512])


def build_l3f_a(nc):
    P = Prog(nc)
    E8, SH = 8, 2048
    xd = nc.dram_tensor("xin", [B, D, TPC], F32, kind="ExternalInput")
    h2_o = nc.dram_tensor("h2", [D, T], BF16, kind="ExternalOutput")
    shh_o = nc.dram_tensor("shh", [SH, T], BF16, kind="ExternalOutput")
    cwr_o = nc.dram_tensor("cwr", [E8, T], BF16, kind="ExternalOutput")
    G = _g_weights(nc, L3FA_NAMES, W_SHAPES)
    C = Consts(P)
    with P.scope("F3a_"):
        emit_l3f_a(P, C, G, xd, h2_o, shh_o, cwr_o)
    P.finalize()
    return P


def emit_l3f_b(P, C, G, xd, xo, h2_d, shh_d, cwr_d):
    nc = P.nc
    E8, SH = 8, 2048
    ew1_d = G["L3_ffn_exp_w1"]
    ew3_d = G["L3_ffn_exp_w3"]
    ew2_d = G["L3_ffn_exp_w2"]
    shw2_d = G["L3_ffn_sh_w2"]
    l1w_d = G["L3_ffn_lin1_w"]
    l1b_d = G["L3_ffn_lin1_b"]
    s2w1_d = G["L3_ffn_swi2_w1"]
    s2b1_d = G["L3_ffn_swi2_b1"]
    s2w2_d = G["L3_ffn_swi2_w2"]
    s2b2_d = G["L3_ffn_swi2_b2"]

    xres = P.sb("xres", [128, NKT * T], F32)
    H2 = P.sb("H2", [128, NKT * T], BF16)
    R3 = P.sb("R3", [128, NKT * T], BF16)
    WA = P.sb("WA", [128, 16 * D], BF16)
    WB = P.sb("WB", [128, NKT * D], BF16)
    SHC = P.sb("SHC", [128, 16 * 512], BF16)
    GU = SHC
    GT = P.sb("GT", [128, 512], BF16)
    cwrow = P.sb("cwrow", [1, T], BF16)
    brow = P.sb("brow", [1, 3 * D], BF16)
    u32 = P.sb("u32f", [128, 512], F32)

    pmain = P.ps("pmain", [128, 1024], F32)
    pmain2 = P.ps("pmain2", [128, 1024], F32)
    pbc = P.ps("pbc", [128, 512], F32)

    load_x_fm(P, xd, xres)
    for kf in range(NKT):
        P.dma(H2[:, kf * T:(kf + 1) * T], h2_d[kf * 128:(kf + 1) * 128, :])

    ws2 = load_w_nat(P, shw2_d.ap(), SH, D, WA, 0)
    for c in range(NCH):
        c0 = c * 512
        for m in range(16):
            P.dma(SHC[:, m * 512:(m + 1) * 512], shh_d[m * 128:(m + 1) * 128, c0:c0 + 512])
        pp = PP(P, [pmain[:, 0:512], pmain[:, 512:1024]])
        for m in range(NKT):
            def mmsh2(E, pv, m=m):
                last = None
                for k in range(16):
                    last = E.matmul(pv, ws2[k][:, m * 128:(m + 1) * 128],
                                    SHC[:, k * 512:(k + 1) * 512],
                                    start=(k == 0), stop=(k == 15))
                return last
            pp.step(mmsh2, lambda pv, m=m, c0=c0: P.op(
                "vector", lambda E: E.tensor_copy(
                    R3[:, m * T + c0:m * T + c0 + 512], pv)))

    w1_tok = 0
    for e in range(E8):
        w1e = load_w_nat(P, layer2d(ew1_d, e), D, D, WA, 0, wait=w1_tok if e else None)
        w3e = load_w_nat(P, layer2d(ew3_d, e), D, D, WA, NKT * D, wait=w1_tok if e else None)
        w2e = load_w_nat(P, layer2d(ew2_d, e), D, D, WB, 0)
        P.dma(cwrow[0:1, :], cwr_d[e:e + 1, :])
        P.bridge("vector", P.wsem, P.wcnt)
        for c in range(NCH):
            c0 = c * 512
            P.op("tensor", lambda E, c0=c0: E.matmul(pbc[:, 0:512], C.ones_row[0:1, 0:128],
                                                     cwrow[0:1, c0:c0 + 512], start=True, stop=True))
            pp = PP(P, [pmain, pmain2])
            for m in range(NKT):
                def mmgu(E, pv, m=m, c0=c0, w1e=w1e, w3e=w3e):
                    last = None
                    for k in range(NKT):
                        E.matmul(pv[:, 0:512], w1e[k][:, m * 128:(m + 1) * 128],
                                 H2[:, k * T + c0:k * T + c0 + 512],
                                 start=(k == 0), stop=(k == NKT - 1))
                    for k in range(NKT):
                        last = E.matmul(pv[:, 512:1024], w3e[k][:, m * 128:(m + 1) * 128],
                                        H2[:, k * T + c0:k * T + c0 + 512],
                                        start=(k == 0), stop=(k == NKT - 1))
                    return last
                def evgu(pv, m=m):
                    P.op("scalar", lambda E: E.activation(GT[:, :], pv[:, 0:512], AF.Silu))
                    P.op("vector", lambda E: E.tensor_mul(GU[:, m * 512:(m + 1) * 512],
                                                          GT[:, :], pv[:, 512:1024]))
                    P.op("vector", lambda E: E.tensor_mul(GU[:, m * 512:(m + 1) * 512],
                                                          GU[:, m * 512:(m + 1) * 512],
                                                          pbc[:, 0:512]))
                pp.step(mmgu, evgu)
            if c == NCH - 1:
                w1_tok = P.tok
            pp = PP(P, [pmain[:, 0:512], pmain[:, 512:1024]])
            for m in range(NKT):
                def mmo(E, pv, m=m, w2e=w2e):
                    last = None
                    for k in range(NKT):
                        last = E.matmul(pv, w2e[k][:, m * 128:(m + 1) * 128],
                                        GU[:, k * 512:(k + 1) * 512],
                                        start=(k == 0), stop=(k == NKT - 1))
                    return last
                def evo(pv, m=m, c0=c0):
                    P.op("vector", lambda E: E.tensor_add(
                        R3[:, m * T + c0:m * T + c0 + 512],
                        R3[:, m * T + c0:m * T + c0 + 512], pv))
                pp.step(mmo, evo)

    wl1 = load_w_nat(P, l1w_d.ap(), D, D, WA, 0)
    P.cast_dma(brow[0:1, 0:D], l1b_d[:].rearrange("(a n) -> a n", a=1))
    T2 = H2
    pp = PP(P, [pmain[:, 0:512], pmain[:, 512:1024]])
    for m in range(NKT):
        for c in range(NCH):
            def mml1(E, pv, m=m, c=c):
                last = None
                for k in range(NKT):
                    last = E.matmul(pv, wl1[k][:, m * 128:(m + 1) * 128],
                                    R3[:, k * T + c * 512:k * T + (c + 1) * 512],
                                    start=(k == 0), stop=False)
                return E.matmul(pv, brow[0:1, m * 128:(m + 1) * 128],
                                C.ones_row[0:1, 0:512], start=False, stop=True)
            pp.step(mml1, lambda pv, m=m, c=c: P.op(
                "vector", lambda E: E.tensor_copy(
                    T2[:, m * T + c * 512:m * T + (c + 1) * 512], pv)))
    wsa = load_w_nat(P, s2w1_d.ap(), D, D, WA, 0)
    wsb = load_w_nat(P, s2w2_d.ap(), D, D, WB, 0)
    P.cast_dma(brow[0:1, D:2 * D], s2b1_d[:].rearrange("(a n) -> a n", a=1))
    P.cast_dma(brow[0:1, 2 * D:3 * D], s2b2_d[:].rearrange("(a n) -> a n", a=1))
    pp = PP(P, [pmain, pmain2])
    for m in range(NKT):
        for c in range(NCH):
            def mm2(E, pv, m=m, c=c):
                last = None
                for k in range(NKT):
                    E.matmul(pv[:, 0:512], wsa[k][:, m * 128:(m + 1) * 128],
                             T2[:, k * T + c * 512:k * T + (c + 1) * 512],
                             start=(k == 0), stop=False)
                E.matmul(pv[:, 0:512], brow[0:1, D + m * 128:D + (m + 1) * 128],
                         C.ones_row[0:1, 0:512], start=False, stop=True)
                for k in range(NKT):
                    E.matmul(pv[:, 512:1024], wsb[k][:, m * 128:(m + 1) * 128],
                             T2[:, k * T + c * 512:k * T + (c + 1) * 512],
                             start=(k == 0), stop=False)
                return E.matmul(pv[:, 512:1024], brow[0:1, 2 * D + m * 128:2 * D + (m + 1) * 128],
                                C.ones_row[0:1, 0:512], start=False, stop=True)
            def ev2(pv, m=m, c=c):
                P.op("scalar", lambda E: E.activation(GT[:, :], pv[:, 0:512], AF.Silu))
                P.op("vector", lambda E: E.tensor_mul(u32[:, :], GT[:, :], pv[:, 512:1024]))
                xv = xres[:, m * T + c * 512:m * T + (c + 1) * 512]
                P.op("vector", lambda E, xv=xv: E.tensor_add(xv, xv, u32[:, :]))
            pp.step(mm2, ev2)
    store_x_fm(P, xres, xo)


def build_l3f_b(nc):
    P = Prog(nc)
    E8, SH = 8, 2048
    xd = nc.dram_tensor("xin", [B, D, TPC], F32, kind="ExternalInput")
    xo = nc.dram_tensor("xout", [B, D, TPC], F32, kind="ExternalOutput")
    h2_d = nc.dram_tensor("h2", [D, T], BF16, kind="ExternalInput")
    shh_d = nc.dram_tensor("shh", [SH, T], BF16, kind="ExternalInput")
    cwr_d = nc.dram_tensor("cwr", [E8, T], BF16, kind="ExternalInput")
    G = _g_weights(nc, L3FB_NAMES, W_SHAPES)
    C = Consts(P)
    with P.scope("F3b_"):
        emit_l3f_b(P, C, G, xd, xo, h2_d, shh_d, cwr_d)
    P.finalize()
    return P


def stage_l3_ffn(xs, w):
    names_a = ["L3_ffn_lin0_w", "L3_ffn_lin0_b", "L3_ffn_swi1_w1", "L3_ffn_swi1_b1",
               "L3_ffn_swi1_w2", "L3_ffn_swi1_b2", "L3_ffn_gate_w",
               "L3_ffn_sh_w1", "L3_ffn_sh_w3"]
    res = run_launch(build_l3f_a, [
        dict(xin=xs[c], **{n: w[n] for n in names_a}) for c in range(NCORE)])
    names_b = ["L3_ffn_exp_w1", "L3_ffn_exp_w3", "L3_ffn_exp_w2", "L3_ffn_sh_w2",
               "L3_ffn_lin1_w", "L3_ffn_lin1_b", "L3_ffn_swi2_w1", "L3_ffn_swi2_b1",
               "L3_ffn_swi2_w2", "L3_ffn_swi2_b2"]
    res = run_launch(build_l3f_b, [
        dict(xin=xs[c], h2=res[c]["h2"], shh=res[c]["shh"], cwr=res[c]["cwr"],
             **{n: w[n] for n in names_b}) for c in range(NCORE)])
    return [res[c]["xout"] for c in range(NCORE)]




def build_fused(nc):
    P = Prog(nc)
    U32 = mybir.dt.uint32
    xin = nc.dram_tensor("xin", [B, D, TPC], F32, kind="ExternalInput")
    xout = nc.dram_tensor("xout", [B, D, TPC], F32, kind="ExternalOutput")
    G = _g_weights(nc, list(W_SHAPES), W_SHAPES)
    G["fused"] = True
    G["rope"] = nc.dram_tensor("rope", [4, 64, TPC], F32, kind="ExternalInput")
    G["pmcol"] = nc.dram_tensor("pmcol", [128, 1], F32, kind="ExternalInput")
    bmc_d = nc.dram_tensor("bmc", [128, 1], F32, kind="ExternalInput")
    G["prv"] = nc.dram_tensor("prv", [1, 1], U32, kind="ExternalInput")
    G["cid"] = nc.dram_tensor("cid", [1, 1], U32, kind="ExternalInput")
    G["pbh"] = nc.dram_tensor("pbh", [2, 2], U32, kind="ExternalInput")
    P.dyn_handles = {
        "prv": (G["prv"], 0, 0), "cid": (G["cid"], 0, 0),
    }
    G["convq"] = nc.dram_tensor("convq", [2, 4, 256], F32, kind="ExternalInput")
    G["convk"] = nc.dram_tensor("convk", [2, 4, 256], F32, kind="ExternalInput")
    G["convv"] = nc.dram_tensor("convv", [2, 4, 256], F32, kind="ExternalInput")
    C = Consts(P)
    BMC = P.sb("bmc_sb", [128, 1], F32)
    P.dma(BMC[:, :], bmc_d[:, :])
    G["bmcol"] = BMC[:, :]

    xb1 = nc.dram_tensor("xb1", [B, D, TPC], F32)
    xb2 = nc.dram_tensor("xb2", [B, D, TPC], F32)
    yb1 = nc.dram_tensor("yb1", [B, D, TPC], F32)
    yb2 = nc.dram_tensor("yb2", [B, D, TPC], F32)

    with P.scope("s0_"):
        emit_l0_attn(P, C, G, xin, xb1)
    with P.scope("s1_"):
        emit_l0_ffn(P, C, G, xb1, xb2)
    with P.scope("s2_"):
        emit_l1_attn(P, C, G, xb2, xb1)
    with P.scope("s3p_"):
        emit_l1f_prep(P, C, G, xb1, yb1)
    for l in range(2):
        q_d = nc.dram_tensor(f"l1q{l}", [8 * B * 64, TPC], BF16)
        k_d = nc.dram_tensor(f"l1k{l}", [8 * B * 64, TPC], BF16)
        v_d = nc.dram_tensor(f"l1v{l}", [T, 512], BF16)
        kall = nc.dram_tensor(f"l1kall{l}", [NCORE * 8 * B * 64, TPC], BF16)
        vall = nc.dram_tensor(f"l1vall{l}", [NCORE * T, 512], BF16)
        with P.scope(f"s3q{l}_"):
            emit_l1f_qkv(P, C, G, l, yb1, q_d, k_d, v_d)
        P.cc("AllGather", ALU.bypass, [k_d.ap().opt()], [kall.ap().opt()])
        P.cc("AllGather", ALU.bypass, [v_d.ap().opt()], [vall.ap().opt()])
        with P.scope(f"s3a{l}_"):
            emit_l1f_attn(P, C, G, l, yb1, yb2, q_d, k_d, v_d,
                          kall_d=kall, vall_d=vall)
        with P.scope(f"s3f{l}_"):
            emit_l1f_ffn(P, C, G, l, yb2, yb1, final=(l == 1), x3d=xb1)
    x4 = yb1
    qa_i = nc.dram_tensor("qa_i", [NCORE * 512, 512], BF16)
    ka_i = nc.dram_tensor("ka_i", [NCORE * 512, 512], BF16)
    va_i = nc.dram_tensor("va_i", [NCORE * 512, 512], BF16)
    ba_i = nc.dram_tensor("ba_i", [NCORE * 2, 512], F32)
    qa_o = nc.dram_tensor("qa_o", [NCORE * 512, 512], BF16)
    ka_o = nc.dram_tensor("ka_o", [NCORE * 512, 512], BF16)
    va_o = nc.dram_tensor("va_o", [NCORE * 512, 512], BF16)
    ba_o = nc.dram_tensor("ba_o", [NCORE * 2, 512], F32)
    oa_i = nc.dram_tensor("oa_i", [NCORE * 512, 512], BF16)
    oa_o = nc.dram_tensor("oa_o", [NCORE * 512, 512], BF16)
    ofm = nc.dram_tensor("ofm", [D, T], BF16)
    with P.scope("s4p_"):
        emit_l2a_proj(P, C, G, x4, qa_i, ka_i, va_i, ba_i)
    P.cc("AllToAll", ALU.bypass, [qa_i.ap().opt()], [qa_o.ap().opt()])
    P.cc("AllToAll", ALU.bypass, [ka_i.ap().opt()], [ka_o.ap().opt()])
    P.cc("AllToAll", ALU.bypass, [va_i.ap().opt()], [va_o.ap().opt()])
    P.cc("AllToAll", ALU.bypass, [ba_i.ap().opt()], [ba_o.ap().opt()])
    with P.scope("s4s_"):
        emit_l2a_scan(P, C, G, qa_o, ka_o, va_o, ba_o, oa_i)
    P.cc("AllToAll", ALU.bypass, [oa_i.ap().opt()], [oa_o.ap().opt()])
    with P.scope("s4g_"):
        for r in range(NCORE):
            for p in range(2):
                for dt in range(2):
                    b_r = r // 2
                    h_r = 2 * (r % 2) + p
                    row0 = r * 512 + p * 256 + dt * 128
                    P.dma(ofm[h_r * 256 + dt * 128:h_r * 256 + (dt + 1) * 128,
                              b_r * 512:(b_r + 1) * 512],
                          oa_o[row0:row0 + 128, :])
    with P.scope("s4o_"):
        emit_l2a_out(P, C, G, x4, xb2, ofm)
    with P.scope("s5_"):
        emit_l2_ffn(P, C, G, xb2, xb1)
    xssm = nc.dram_tensor("xssm", [2048, T], BF16)
    xgate = nc.dram_tensor("xgate", [2048, T], BF16)
    dts = nc.dram_tensor("dts", [64, 512], F32)
    bxs = nc.dram_tensor("bxs", [64, 512], F32)
    cts = nc.dram_tensor("cts", [64, 512], F32)
    dta = nc.dram_tensor("dta", [NCORE * 64, 512], F32)
    bxa = nc.dram_tensor("bxa", [NCORE * 64, 512], F32)
    cta = nc.dram_tensor("cta", [NCORE * 64, 512], F32)
    g2 = nc.dram_tensor("g2", [D, T], BF16)
    x2 = nc.dram_tensor("x2", [D, T], BF16)
    ysall = nc.dram_tensor("ysall", [64, (S // 64) * B], F32)
    with P.scope("s6p_"):
        emit_l3a_proj(P, C, G, xb1, xssm, xgate)
    with P.scope("s6m_"):
        emit_l3a_mid(P, C, G, xssm, xgate, dts, bxs, cts, g2, x2)
    P.cc("AllGather", ALU.bypass, [dts.ap().opt()], [dta.ap().opt()])
    P.cc("AllGather", ALU.bypass, [bxs.ap().opt()], [bxa.ap().opt()])
    P.cc("AllGather", ALU.bypass, [cts.ap().opt()], [cta.ap().opt()])
    with P.scope("s6s_"):
        emit_l3a_scan(P, C, G, dta, bxa, cta, ysall)
    with P.scope("s6c_"):
        emit_l3a_comb(P, C, G, xb1, xb2, ysall, g2, x2)
    h2 = nc.dram_tensor("h2", [D, T], BF16)
    shh = nc.dram_tensor("shh", [2048, T], BF16)
    cwr = nc.dram_tensor("cwr", [8, T], BF16)
    with P.scope("s7a_"):
        emit_l3f_a(P, C, G, xb2, h2, shh, cwr)
    with P.scope("s7b_"):
        emit_l3f_b(P, C, G, xb2, xout, h2, shh, cwr)
    P.finalize()
    return P


def kernel_fused(inputs):
    xs = shard_x(inputs["L0_attn_x"])
    w = {k: np.ascontiguousarray(np.asarray(v, dtype=np.float32))
         for k, v in inputs.items() if k != "L0_attn_x"}
    in_maps = []
    for c in range(NCORE):
        b = c // 2
        h0 = 2 * (c % 2)
        im = dict(w)
        im["xin"] = xs[c]
        im["rope"] = _rope_tables(c)
        im["pmcol"] = np.full((128, 1), -1e9 if c == 0 else 0.0, dtype=np.float32)
        im["bmc"] = np.full((128, 1), 0.0 if c == 0 else 1.0, dtype=np.float32)
        im["prv"] = np.array([[max(c - 1, 0)]], dtype=np.uint32)
        im["cid"] = np.array([[c]], dtype=np.uint32)
        im["pbh"] = np.array([[b, h0], [b, h0 + 1]], dtype=np.uint32)
        prs = [(b, h0), (b, h0 + 1)]
        for nm, key in (("convq", "L2_attn_conv_q"), ("convk", "L2_attn_conv_k"),
                        ("convv", "L2_attn_conv_v")):
            im[nm] = np.ascontiguousarray(np.stack(
                [w[key][:, hh * 256:(hh + 1) * 256] for (_, hh) in prs]))
        in_maps.append(im)
    res = run_launch(build_fused, in_maps)
    return unshard_x([res[c]["xout"] for c in range(NCORE)])


FUSED = True


def run_launch(build_fn, in_maps):
    nc = bass.Bass("TRN2", target_bir_lowering=False)
    build_fn(nc)
    res = run_bass_kernel_spmd(nc, in_maps, core_ids=list(range(NCORE)))
    return res.results


def shard_x(x_bsd):
    xt = np.ascontiguousarray(np.transpose(np.asarray(x_bsd, dtype=np.float32), (0, 2, 1)))
    return [np.ascontiguousarray(xt[:, :, c * TPC:(c + 1) * TPC]) for c in range(NCORE)]


def unshard_x(slices):
    xt = np.concatenate([s for s in slices], axis=2)
    return np.ascontiguousarray(np.transpose(xt, (0, 2, 1)))


def _rope_tables(c):
    half = np.arange(0, 64, 2, dtype=np.float64)
    inv_freq = 1.0 / (10000.0 ** (half / 64.0))
    t = np.arange(c * TPC, (c + 1) * TPC, dtype=np.float64)
    freqs = t[:, None] * inv_freq[None, :]
    freqs = np.concatenate([freqs, freqs], -1)
    base = (half + 0.4 * 64) / (1.4 * 64)
    power = (t - S // 2) / (512 // 2)
    scale = base[None, :] ** power[:, None]
    scale = np.concatenate([scale, scale], -1)
    cos = np.cos(freqs)
    sin = np.sin(freqs)
    out = np.stack([(cos * scale * 8.0).T, (sin * scale * 8.0).T,
                    (cos / scale).T, (sin / scale).T])
    return np.ascontiguousarray(out.astype(np.float32))


def _bnd(ys, c):
    if c == 0:
        return np.zeros((B, D, 1), dtype=np.float32)
    return np.ascontiguousarray(ys[c - 1][:, :, TPC - 1:TPC])


def stage_l0_attn(xs, w):
    res = run_launch(build_l0_attn, [
        dict(xin=xs[c], **{n: w[n] for n in
             ["L0_attn_Wq", "L0_attn_Wk", "L0_attn_Wv", "L0_attn_Wr"]})
        for c in range(NCORE)])
    return [res[c]["xout"] for c in range(NCORE)]


def stage_l0_ffn(xs, w):
    names = ["L0_ffn_W", "L0_ffn_Wp", "L0_ffn_bp", "L0_ffn_sim", "L0_ffn_temp",
             "L0_ffn_Wd0", "L0_ffn_Wu0", "L0_ffn_Wd1", "L0_ffn_Wu1",
             "L0_ffn_Wd2", "L0_ffn_Wu2", "L0_ffn_Wd3", "L0_ffn_Wu3"]
    res = run_launch(build_l0_ffn, [
        dict(xin=xs[c], **{n: w[n] for n in names}) for c in range(NCORE)])
    return [res[c]["xout"] for c in range(NCORE)]


def stage_l1_attn(xs, w):
    names = ["L1_attn_Wq", "L1_attn_bq", "L1_attn_Wk", "L1_attn_bk",
             "L1_attn_Wv", "L1_attn_bv", "L1_attn_Wo", "L1_attn_bo"]
    res = run_launch(build_l1_attn, [
        dict(xin=xs[c], **{n: w[n] for n in names}) for c in range(NCORE)])
    return [res[c]["xout"] for c in range(NCORE)]


def stage_l1_ffn(xs, w):
    ropes = [_rope_tables(c) for c in range(NCORE)]
    res = run_launch(build_l1f_prep, [dict(xin=xs[c]) for c in range(NCORE)])
    ys = [res[c]["yout"] for c in range(NCORE)]
    zk = np.zeros((8 * B * 64, TPC), dtype=NBF)
    zv = np.zeros((T, 512), dtype=NBF)
    for l in range(2):
        res = run_launch(build_l1f_qkv(l), [
            dict(yin=ys[c], ybnd=_bnd(ys, c), rope=ropes[c],
                 L1_ffn_wqkv=w["L1_ffn_wqkv"], L1_ffn_q_scale=w["L1_ffn_q_scale"],
                 L1_ffn_k_scale=w["L1_ffn_k_scale"]) for c in range(NCORE)])
        qs = [res[c]["qout"] for c in range(NCORE)]
        ks = [res[c]["kout"] for c in range(NCORE)]
        vs = [res[c]["vout"] for c in range(NCORE)]
        res = run_launch(build_l1f_attn(l), [
            dict(yin=ys[c], qin=qs[c],
                 kprev=(ks[c - 1] if c > 0 else zk), kown=ks[c],
                 vprev=(vs[c - 1] if c > 0 else zv), vown=vs[c],
                 pmcol=np.full((128, 1), -1e9 if c == 0 else 0.0, dtype=np.float32),
                 L1_ffn_wo=w["L1_ffn_wo"]) for c in range(NCORE)])
        ys = [res[c]["yout"] for c in range(NCORE)]
        fnames = ["L1_ffn_ln_g", "L1_ffn_ln_b", "L1_ffn_w1", "L1_ffn_b1",
                  "L1_ffn_w2", "L1_ffn_b2"]
        final = (l == 1)
        im = [dict(yin=ys[c], ybnd=_bnd(ys, c), **{n: w[n] for n in fnames})
              for c in range(NCORE)]
        if final:
            for c in range(NCORE):
                im[c]["x3in"] = xs[c]
        res = run_launch(build_l1f_ffn(l, final=final), im)
        ys = [res[c]["yout"] for c in range(NCORE)]
    return ys


STAGES = [stage_l0_attn, stage_l0_ffn, stage_l1_attn, stage_l1_ffn, stage_l2_attn, stage_l2_ffn, stage_l3_attn, stage_l3_ffn]


def kernel(**inputs):
    if FUSED:
        return kernel_fused(inputs)
    xs = shard_x(inputs["L0_attn_x"])
    w = {k: np.ascontiguousarray(np.asarray(v, dtype=np.float32))
         for k, v in inputs.items() if k != "L0_attn_x"}
    for stage in STAGES:
        xs = stage(xs, w)
    return unshard_x(xs)


W_SHAPES.update({
    "L0_attn_Wq": (1024, 1024),
    "L0_attn_Wk": (1024, 1024),
    "L0_attn_Wv": (1024, 1024),
    "L0_attn_Wr": (1024, 1024),
    "L0_ffn_W": (1024, 1024),
    "L0_ffn_Wp": (256, 1024),
    "L0_ffn_bp": (256,),
    "L0_ffn_sim": (256, 4),
    "L0_ffn_temp": (1,),
    "L0_ffn_Wd0": (1, 1024),
    "L0_ffn_Wu0": (1024, 1),
    "L0_ffn_Wd1": (2, 1024),
    "L0_ffn_Wu1": (1024, 2),
    "L0_ffn_Wd2": (4, 1024),
    "L0_ffn_Wu2": (1024, 4),
    "L0_ffn_Wd3": (8, 1024),
    "L0_ffn_Wu3": (1024, 8),
    "L1_attn_Wq": (1024, 1024),
    "L1_attn_bq": (1024,),
    "L1_attn_Wk": (1024, 1024),
    "L1_attn_bk": (1024,),
    "L1_attn_Wv": (1024, 1024),
    "L1_attn_bv": (1024,),
    "L1_attn_Wo": (1024, 1024),
    "L1_attn_bo": (1024,),
    "L1_ffn_wqkv": (2, 1024, 1536),
    "L1_ffn_q_scale": (2, 64),
    "L1_ffn_k_scale": (2, 64),
    "L1_ffn_wo": (2, 512, 1024),
    "L1_ffn_ln_g": (2, 1024),
    "L1_ffn_ln_b": (2, 1024),
    "L1_ffn_w1": (2, 1024, 5460),
    "L1_ffn_b1": (2, 5460),
    "L1_ffn_w2": (2, 2730, 1024),
    "L1_ffn_b2": (2, 1024),
    "L2_attn_Wq": (1024, 1024),
    "L2_attn_Wk": (1024, 1024),
    "L2_attn_Wv": (1024, 1024),
    "L2_attn_Wb": (1024, 4),
    "L2_attn_conv_q": (4, 1024),
    "L2_attn_conv_k": (4, 1024),
    "L2_attn_conv_v": (4, 1024),
    "L2_attn_g_rms": (256,),
    "L2_attn_Wo": (1024, 1024),
    "L2_attn_ln_g": (1024,),
    "L2_attn_ln_b": (1024,),
    "L2_ffn_router_W": (1024, 15),
    "L2_ffn_router_b": (15,),
    "L2_ffn_sW1": (1024, 1024),
    "L2_ffn_sb1": (1024,),
    "L2_ffn_sW2": (1024, 1024),
    "L2_ffn_sb2": (1024,),
    "L2_ffn_eW1": (15, 1024, 1024),
    "L2_ffn_eb1": (15, 1024),
    "L2_ffn_eW2": (15, 1024, 1024),
    "L2_ffn_eb2": (15, 1024),
    "L3_attn_in_proj_w": (4096, 1024),
    "L3_attn_conv_weight": (2048, 4),
    "L3_attn_conv_bias": (2048,),
    "L3_attn_A_log": (16,),
    "L3_attn_D": (2048,),
    "L3_attn_delta_w": (16, 2048),
    "L3_attn_delta_b": (16,),
    "L3_attn_B_w": (16, 2048),
    "L3_attn_C_w": (16, 2048),
    "L3_attn_out_proj_w": (1024, 2048),
    "L3_ffn_lin0_w": (1024, 1024),
    "L3_ffn_lin0_b": (1024,),
    "L3_ffn_swi1_w1": (1024, 1024),
    "L3_ffn_swi1_b1": (1024,),
    "L3_ffn_swi1_w2": (1024, 1024),
    "L3_ffn_swi1_b2": (1024,),
    "L3_ffn_gate_w": (8, 1024),
    "L3_ffn_exp_w1": (8, 1024, 1024),
    "L3_ffn_exp_w3": (8, 1024, 1024),
    "L3_ffn_exp_w2": (8, 1024, 1024),
    "L3_ffn_sh_w1": (1024, 2048),
    "L3_ffn_sh_w3": (1024, 2048),
    "L3_ffn_sh_w2": (2048, 1024),
    "L3_ffn_lin1_w": (1024, 1024),
    "L3_ffn_lin1_b": (1024,),
    "L3_ffn_swi2_w1": (1024, 1024),
    "L3_ffn_swi2_b1": (1024,),
    "L3_ffn_swi2_w2": (1024, 1024),
    "L3_ffn_swi2_b2": (1024,),
})
